# version 06
# baseline (speedup 1.0000x reference)
import numpy as np
import ml_dtypes

import concourse.bass as bass
import concourse.mybir as mybir
from concourse.bass_utils import run_bass_kernel_spmd

F32 = mybir.dt.float32
BF16 = mybir.dt.bfloat16
I32 = mybir.dt.int32
AF = mybir.ActivationFunctionType
OP = mybir.AluOpType
AXX = mybir.AxisListType.X

NCORES = 8
T = 1024
TH = 512
D = 4096
M = 14336
DA = 512
DFF = 2048
NL = 4
EPS = 1e-5

MC = M // 128
VC = D // 128
DC = DA // 128
FC = DFF // 128
MO = M // NCORES
HO = D // NCORES
MSUB = 3584
NSUB = M // MSUB

NF4 = [-1.0, -0.6961928009986877, -0.5250730514526367, -0.39491748809814453,
       -0.28444138169288635, -0.18477343022823334, -0.09105003625154495, 0.0,
       0.07958029955625534, 0.16093020141124725, 0.24611230194568634,
       0.33791524171829224, 0.44070982933044434, 0.5626170039176941,
       0.7229568362236023, 1.0]
NF_A = NF4[0]
NF_B = NF4[1] - NF4[0]
NF_G = [NF4[t + 1] - 2 * NF4[t] + NF4[t - 1] for t in range(1, 15)]

_LAST_RES = None
DEBUG = False

PQ = ["ln1_w", "ln1_b", "ln2_w", "ln2_b", "sc_lin",
      "phz_re", "phz_im", "lc_re", "lc_im", "b2"]
PCMAP = {}
_col = 0
for _l in range(NL):
    for _nm in PQ:
        PCMAP[(_nm, _l)] = _col
        _col += 4
for _l in range(NL):
    PCMAP[("b1", _l)] = _col
    _col += 16
for _nm in ["bin", "lnl_w", "lnl_b"]:
    PCMAP[(_nm, 0)] = _col
    _col += 4
PCMAP[("cmask", 0)] = _col
_col += 8
PCMAP[("negt", 0)] = _col
_col += 14
PCMAP[("cmask8", 0)] = _col
_col += 64
PC_DATA = _col
PC_SCR = PC_DATA
PCOLS = PC_DATA + 76

AR = 102400
A_XFM = 0
A_GB = 16384
A_UB = 24576
A_HACT = 32768
A_DBUF = 16384
A_HSIL = 98816
F_XS = 0
F_OT = 2048
F_LNX = 16384
F_LNSQ = 20480
A_LNB = 49152
F_DQ_QI = 0
F_DQ_QF = 4096
F_DQ_RL = 8192
F_DQ_AC = 12288
F_DQ_AM = 16384
A_DQ_WB = 40960
A_DQ_ST = 49152
F_H = 0
F_Y = 4096
F_U = 16384
F_V = 20480
F_C = 24576
F_S = 28672
F_P = 32768
F_RHO = 37120
F_TMD = 38144
F_LN2 = 7168
A_UMLP = 78336
A_WO = 78336
A_XFM2 = 78336
A_Y2 = 94720
A_HLN = 94720
A_WSTR = 32768
A_WIN = 65536
F_STAT = 39168
F_MTMP = 14336
F_BOUT = 14336
F_X1S = 14848
F_WTMP = 15360


class KB:
    def __init__(self):
        nc = bass.Bass("TRN2", target_bir_lowering=False)
        self.nc = nc
        self.qs_n = 0
        self.qa_n = 0
        self.qg_n = 0

        dt = nc.dram_tensor
        self.x_in = dt("x_in", [T, D], F32, kind="ExternalInput")
        self.gq = dt("gq", [MO, D], I32, kind="ExternalInput")
        self.uq = dt("uq", [MO, D], I32, kind="ExternalInput")
        self.dq = dt("dq", [HO, M], I32, kind="ExternalInput")
        self.ga = dt("ga", [MO, 64], F32, kind="ExternalInput")
        self.ua = dt("ua", [MO, 64], F32, kind="ExternalInput")
        self.da = dt("da", [HO, M // 64], F32, kind="ExternalInput")
        self.win = dt("win", [DA, D], F32, kind="ExternalInput")
        self.w1 = dt("w1", [NL, DFF, DA], F32, kind="ExternalInput")
        self.w2 = dt("w2", [NL, DA, DFF], F32, kind="ExternalInput")
        self.wout = dt("wout", [D, DA], F32, kind="ExternalInput")
        self.prm_in = dt("prm_in", [128, PC_DATA], F32, kind="ExternalInput")
        self.bout_rep = dt("bout_rep", [128, D], F32, kind="ExternalInput")
        self.out = dt("out", [T, D], F32, kind="ExternalOutput")

        self.wg_own = dt("wg_own", [MO, D], BF16)
        self.wu_own = dt("wu_own", [MO, D], BF16)
        self.wd_own = dt("wd_own", [4 * NSUB * 128, MSUB], BF16)
        self.wg_gath = dt("wg_gath", [M, D], BF16)
        self.wu_gath = dt("wu_gath", [M, D], BF16)
        self.wd_gath = dt("wd_gath", [NCORES * 4 * NSUB * 128, MSUB], BF16)
        self.winT = dt("winT", [DC * 128, D], BF16)
        self.w1T = dt("w1T", [NL * 128, FC * DC * 128], BF16)
        self.w2T = dt("w2T", [NL * 128, FC * DC * 128], BF16)
        self.woutT = dt("woutT", [DC * 128, D], BF16)
        self.x1b = dt("x1b", [T, D], F32)
        self.x2b = dt("x2b", [T, D], F32)
        self.dbg1 = dt("dbg1", [T, D], F32, kind="ExternalOutput") if DEBUG else None
        self.dbg2 = dt("dbg2", [T, D], F32, kind="ExternalOutput") if DEBUG else None
        self.dbgh = dt("dbgh", [8, DA, T], F32, kind="ExternalOutput") if DEBUG else None
        self.dbgp = dt("dbgp", [128, 64], F32, kind="ExternalOutput") if DEBUG else None
        self.cc_in = [dt(f"cc_in{l}", [1, 1024], F32) for l in range(NL)]
        self.cc_out = [dt(f"cc_out{l}", [NCORES, 1024], F32) for l in range(NL)]

        self.ident_b = nc.alloc_sbuf_tensor("ident_b", [128, 128], BF16)
        self.ident_f = nc.alloc_sbuf_tensor("ident_f", [128, 128], F32)
        self.ones_f = nc.alloc_sbuf_tensor("ones_f", [128, 1], F32)
        self.ones_r = nc.alloc_sbuf_tensor("ones_r", [1, 128], F32)
        self.prm = nc.alloc_sbuf_tensor("prm", [128, PCOLS], F32)
        self.arena = nc.alloc_sbuf_tensor("arena", [128, AR], BF16)
        self.ar_b = self.arena.ap()
        self.ar_f = self.arena.ap().bitcast(F32)

        ps = nc.alloc_psum_tensor
        self.ps_a = ps("ps_a", [128, 512], F32)
        self.ps_b = ps("ps_b", [128, 512], F32)
        self.ps_d = [ps(f"ps_d{i}", [128, 512], F32) for i in range(4)]
        self.ps_t = ps("ps_t", [128, 1024], BF16)
        self.ps_m = ps("ps_m", [128, 512], F32)

        self.qs = nc.alloc_semaphore(name="qs")
        self.qa = nc.alloc_semaphore(name="qa")
        self.qg = nc.alloc_semaphore(name="qg")
        self.sc_w = [nc.alloc_semaphore(name=f"scw{i}") for i in range(3)]
        self.sc_l = [nc.alloc_semaphore(name=f"scl{i}") for i in range(NL)]
        self.pe_s = nc.alloc_semaphore(name="pe_s")
        self.ae_s = nc.alloc_semaphore(name="ae_s")
        self.de_s = nc.alloc_semaphore(name="de_s")
        self.pe_n = 0
        self.ae_n = 0
        self.de_n = 0

        ET = mybir.EngineType
        self._bar_engines = [ET.PE, ET.Activation, ET.DVE, ET.SP]
        self.st_s = nc.alloc_semaphore(name="st_s")
        self.st_n = 0
        self.pt_s = nc.alloc_semaphore(name="pt_s")
        self.ev_s = nc.alloc_semaphore(name="ev_s")
        self.pt_n = 0
        self.ev_n = 0
        self.c_ident_b = nc.inline_tensor(
            np.eye(128, dtype=np.float32).astype(ml_dtypes.bfloat16), name="cib")
        self.c_ident_f = nc.inline_tensor(np.eye(128, dtype=np.float32), name="cif")

    def bar(self):
        self.nc.multi_engine_barrier(self._bar_engines)

    def allbar(self):
        self.nc.all_engine_barrier()

    def ld(self, out, in_):
        self.nc.sync.dma_start(out=out, in_=in_).then_inc(self.qs, 16)
        self.qs_n += 16

    def ld_wait(self):
        self.nc.sync.wait_ge(self.qs, self.qs_n)

    def st(self, out, in_):
        self.nc.scalar.dma_start(out=out, in_=in_).then_inc(self.qa, 16)
        self.qa_n += 16

    def st_wait(self):
        self.nc.scalar.wait_ge(self.qa, self.qa_n)

    def fv(self, off, cols, rows=128):
        return self.ar_f[0:rows, off:off + cols]

    def bv(self, off, cols):
        return self.ar_b[:, off:off + cols]

    def pcol(self, nm, l=0, tile=0, n=1):
        c = PCMAP[(nm, l)] + tile
        return self.prm.ap()[:, c:c + n]

    def scol(self, i, n=1):
        c = PC_SCR + i
        return self.prm.ap()[:, c:c + n]

    def prep_consts(self):
        nc = self.nc
        self.ld(self.ident_b.ap(), self.c_ident_b.ap())
        self.ld(self.ident_f.ap(), self.c_ident_f.ap())
        self.ld(self.prm.ap()[:, 0:PC_DATA], self.prm_in.ap())
        nc.vector.memset(self.ones_f.ap(), 1.0)
        nc.vector.memset(self.ones_r.ap(), 1.0)
        self.eps_c = self.scol(70)
        self.mhalf_c = self.scol(71)
        nc.vector.memset(self.eps_c, EPS)
        nc.vector.memset(self.mhalf_c, -0.5)
        self.ld_wait()
        self.bar()

    def dq_ld(self, k, q_src, am_src, fcols):
        b = k % 2
        qi = self.ar_f[:, b * 4096:b * 4096 + fcols].bitcast(I32)
        amax = self.fv(22528 + b * 64, fcols // 64)
        self.ld(qi, q_src)
        self.ld(amax, am_src)
        return self.qs_n

    def dequant_tile(self, k, fcols, qs_need):
        nc = self.nc
        b = k % 2
        qi = self.ar_f[:, b * 4096:b * 4096 + fcols].bitcast(I32)
        qf = self.fv(8192, fcols)
        rl = [self.fv(12288, fcols), self.fv(26624, fcols)]
        acc = self.fv(16384, fcols)
        nb = fcols // 64
        amax = self.fv(22528 + b * 64, nb)
        wb = self.bv(A_DQ_WB, fcols)
        ae0, de0 = self.ae_n, self.de_n
        nc.vector.wait_ge(self.qs, qs_need)
        nc.vector.tensor_copy(out=qf, in_=qi).then_inc(self.de_s, 1)
        self.de_n += 1
        nc.vector.tensor_scalar(out=acc, in0=qf, scalar1=NF_B, scalar2=NF_A,
                                op0=OP.mult, op1=OP.add)
        nc.scalar.wait_ge(self.de_s, de0 + 1)
        for t in range(1, 15):
            rb = t % 2
            if t > 2:
                nc.scalar.wait_ge(self.de_s, de0 + 1 + (t - 2))
            nc.scalar.activation(out=rl[rb], in_=qf, func=AF.Relu,
                                 bias=self.pcol("negt", 0, t - 1),
                                 scale=1.0).then_inc(self.ae_s, 1)
            self.ae_n += 1
            nc.vector.wait_ge(self.ae_s, ae0 + t)
            op_ = nc.vector.scalar_tensor_tensor(
                out=acc, in0=rl[rb], scalar=NF_G[t - 1], in1=acc,
                op0=OP.mult, op1=OP.add)
            op_.then_inc(self.de_s, 1)
            self.de_n += 1
        amv = amax.unsqueeze(2).broadcast_to([128, nb, 64])
        accv = acc.rearrange("p (nb b) -> p nb b", b=64)
        wv = wb.rearrange("p (nb b) -> p nb b", b=64)
        nc.vector.tensor_tensor(out=wv, in0=accv, in1=amv, op=OP.mult)
        self.bar()
        return wb

    def transpose_pipe(self, srcs, writer):
        nc = self.nc
        banks = [self.ps_t.ap(), self.ps_d[2].ap().bitcast(BF16)]
        pt0, ev0 = self.pt_n, self.ev_n
        ngroups = (len(srcs) + 7) // 8
        for g in range(ngroups):
            g0 = g * 8
            kn = min(8, len(srcs) - g0)
            bk = banks[g % 2]
            if g >= 2:
                nc.tensor.wait_ge(self.ev_s, ev0 + g - 1)
            for i in range(kn):
                mm = nc.tensor.transpose(bk[:, i * 128:(i + 1) * 128],
                                         srcs[g0 + i], self.ident_b.ap())
                if i == kn - 1:
                    mm.then_inc(self.pt_s, 1)
            self.pt_n += 1
            nc.vector.wait_ge(self.pt_s, pt0 + g + 1)
            last = writer(g0, kn, bk)
            last.then_inc(self.ev_s, 1)
            self.ev_n += 1

    def transpose_wait(self):
        self.nc.scalar.wait_ge(self.ev_s, self.ev_n)
        self.bar()

    def transpose_to_strip(self, src_b, strip, nk):
        srcs = [src_b[:, k * 128:(k + 1) * 128] for k in range(nk)]

        def wr(g0, kn, bk):
            return self.nc.vector.tensor_copy(
                out=strip[:, g0 * 128:(g0 + kn) * 128], in_=bk[:, 0:kn * 128])
        self.transpose_pipe(srcs, wr)
        self.transpose_wait()

    def prep_ffn_weights(self):
        nc = self.nc
        strip = self.bv(A_DQ_ST, 4096)
        tiles = []
        for (qsrc, asrc, own) in [(self.gq, self.ga, self.wg_own),
                                  (self.uq, self.ua, self.wu_own)]:
            ownv = own.ap().rearrange("(s p) f -> p s f", p=128)
            for mc in range(MO // 128):
                tiles.append((qsrc.ap()[mc * 128:(mc + 1) * 128, :],
                              asrc.ap()[mc * 128:(mc + 1) * 128, :],
                              D, ownv[:, mc, :], 32))
        wdv = self.wd_own.ap().rearrange("(s p) f -> p s f", p=128)
        for hc in range(4):
            for ms in range(NSUB):
                tiles.append((
                    self.dq.ap()[hc * 128:(hc + 1) * 128,
                                 ms * MSUB:(ms + 1) * MSUB],
                    self.da.ap()[hc * 128:(hc + 1) * 128,
                                 ms * (MSUB // 64):(ms + 1) * (MSUB // 64)],
                    MSUB, wdv[:, ms * 4 + hc, :], 28))
        ags = {13: 0, 27: 1, 43: 2}
        ag_args = [(self.wg_own, self.wg_gath, self.sc_w[0]),
                   (self.wu_own, self.wu_gath, self.sc_w[1]),
                   (self.wd_own, self.wd_gath, self.sc_w[2])]
        qs_need = self.dq_ld(0, *tiles[0][:3])
        for k, (qsrc, asrc, fcols, dst, nk) in enumerate(tiles):
            if k + 1 < len(tiles):
                nxt = self.dq_ld(k + 1, *tiles[k + 1][:3])
            wb = self.dequant_tile(k, fcols, qs_need)
            qs_need = nxt if k + 1 < len(tiles) else qs_need
            self.transpose_to_strip(wb, strip, nk)
            self.st(dst, strip[:, 0:fcols])
            self.st_wait()
            self.bar()
            if k in ags:
                own, gath, sem = ag_args[ags[k]]
                nc.gpsimd.wait_ge(self.qa, self.qa_n)
                nc.gpsimd.collective_compute(
                    "AllGather", OP.bypass,
                    replica_groups=[list(range(NCORES))],
                    ins=[own.ap().opt()], outs=[gath.ap().opt()],
                ).then_inc(sem)

    def prep_attn_weights(self):
        nc = self.nc
        src16 = self.fv(0, 16384)
        cast16 = self.bv(32768, 16384)
        strip8 = self.bv(49152, 8192)
        wstr = [self.bv(57344 + i * 4096, 4096) for i in range(4)]

        def group_transpose(cast_v, pairs, dst_writer):
            srcs = [cast_v[:, c:c + 128] for c in pairs]
            self.transpose_pipe(srcs, dst_writer)
            self.transpose_wait()

        self.ld(src16.rearrange("p (dc v) -> p dc v", dc=4),
                self.win.ap().rearrange("(dc p) v -> p dc v", p=128))
        self.ld_wait()
        self.bar()
        nc.vector.tensor_copy(out=cast16, in_=src16)
        self.bar()
        winTv = self.winT.ap().rearrange("(s p) f -> p s f", p=128)
        for dc in range(DC):
            pairs = [dc * 4096 + vc * 128 for vc in range(VC)]

            def wr(g0, kn, bk, dc=dc):
                return nc.vector.tensor_copy(
                    out=wstr[dc][:, g0 * 128:(g0 + kn) * 128],
                    in_=bk[:, 0:kn * 128])
            group_transpose(cast16, pairs, wr)
        for dc in range(DC):
            self.st(winTv[:, dc, :], wstr[dc])
        self.st_wait()
        self.bar()
        w1Tv = self.w1T.ap().rearrange("(s p) f -> p s f", p=128)
        w2Tv = self.w2T.ap().rearrange("(s p) f -> p s f", p=128)
        for l in range(NL):
            self.ld(src16[:, 0:8192].rearrange("p (fc d) -> p fc d", fc=FC),
                    self.w1.ap()[l].rearrange("(fc p) d -> p fc d", p=128))
            self.ld_wait()
            self.bar()
            nc.vector.tensor_copy(out=cast16[:, 0:8192], in_=src16[:, 0:8192])
            self.bar()
            pairs = [fc * 512 + dc * 128 for fc in range(FC) for dc in range(DC)]

            def wr1(g0, kn, bk):
                return nc.vector.tensor_copy(
                    out=strip8[:, g0 * 128:(g0 + kn) * 128],
                    in_=bk[:, 0:kn * 128])
            group_transpose(cast16, pairs, wr1)
            self.st(w1Tv[:, l, :], strip8)
            self.st_wait()
            self.bar()
            self.ld(src16[:, 0:8192].rearrange("p (dc f) -> p dc f", dc=DC),
                    self.w2.ap()[l].rearrange("(dc p) f -> p dc f", p=128))
            self.ld_wait()
            self.bar()
            nc.vector.tensor_copy(out=cast16[:, 0:8192], in_=src16[:, 0:8192])
            self.bar()
            pairs = [dc * 2048 + fc * 128 for dc in range(DC) for fc in range(FC)]
            group_transpose(cast16, pairs, wr1)
            self.st(w2Tv[:, l, :], strip8)
            self.st_wait()
            self.bar()
        self.ld(src16.rearrange("p (vc d) -> p vc d", vc=VC),
                self.wout.ap().rearrange("(vc p) d -> p vc d", p=128))
        self.ld_wait()
        self.bar()
        nc.vector.tensor_copy(out=cast16, in_=src16)
        self.bar()
        woutTv = self.woutT.ap().rearrange("(s p) f -> p s f", p=128)
        pairs = [vc * 512 + dc * 128 for vc in range(VC) for dc in range(DC)]

        def wrout(g0, kn, bk):
            vc0 = g0 // 4
            last = None
            for dc in range(DC):
                sv = bk.rearrange("p (v d f) -> p v d f",
                                  v=2, d=4)[:, 0:kn // 4, dc, :]
                dv = wstr[dc][:, vc0 * 128:(vc0 + kn // 4) * 128]
                last = nc.vector.tensor_copy(out=dv, in_=sv)
            return last
        group_transpose(cast16, pairs, wrout)
        for dc in range(DC):
            self.st(woutTv[:, dc, :], wstr[dc])
        self.st_wait()
        self.bar()

    def ln_big_half(self, src, th, xfm):
        nc = self.nc
        xt = self.fv(F_LNX, 4096)
        sq = self.fv(F_LNSQ, 4096)
        xln = self.bv(A_LNB, 4096)
        ssum = self.scol(0)
        nmu = self.scol(1)
        msq = self.scol(2)
        rstd = self.scol(3)
        for tt in range(4):
            r0 = th * TH + tt * 128
            self.ld(xt, src.ap()[r0:r0 + 128, :])
            self.ld_wait()
            self.bar()
            nc.vector.tensor_reduce(out=ssum, in_=xt, axis=AXX, op=OP.add)
            nc.scalar.activation(out=sq, in_=xt, func=AF.Square)
            self.bar()
            nc.vector.tensor_reduce(out=msq, in_=sq, axis=AXX, op=OP.add)
            nc.vector.tensor_scalar(out=nmu, in0=ssum, scalar1=-1.0 / D,
                                    scalar2=None, op0=OP.mult)
            nc.vector.tensor_scalar(out=msq, in0=msq, scalar1=1.0 / D,
                                    scalar2=None, op0=OP.mult)
            self.bar()
            nc.scalar.activation(out=rstd, in_=nmu, func=AF.Square)
            self.bar()
            nc.vector.tensor_tensor(out=msq, in0=msq, in1=rstd, op=OP.subtract)
            self.bar()
            nc.scalar.activation(out=msq, in_=msq, func=AF.Ln, bias=self.eps_c, scale=1.0)
            self.bar()
            nc.scalar.activation(out=rstd, in_=msq, func=AF.Exp, scale=self.mhalf_c)
            self.bar()
            nc.vector.tensor_scalar(out=xln, in0=xt, scalar1=nmu, scalar2=rstd,
                                    op0=OP.add, op1=OP.mult)
            self.bar()
            srcs = [xln[:, k * 128:(k + 1) * 128] for k in range(32)]

            def wrln(g0, kn, bk, tt=tt):
                dstv = xfm.rearrange("p (v t) -> p v t", t=512)[
                    :, g0:g0 + kn, tt * 128:(tt + 1) * 128]
                srcv = bk.rearrange("p (v t) -> p v t", t=128)[:, 0:kn, :]
                return nc.vector.tensor_copy(out=dstv, in_=srcv)
            self.transpose_pipe(srcs, wrln)
            self.transpose_wait()

    def ffn(self, src, dst):
        nc = self.nc
        nc.sync.wait_ge(self.sc_w[0], 1)
        nc.sync.wait_ge(self.sc_w[1], 1)
        self.bar()
        xfm = self.bv(A_XFM, 16384)
        gbuf = [self.bv(A_GB + i * 4096, 4096) for i in range(2)]
        ubuf = [self.bv(A_UB + i * 4096, 4096) for i in range(2)]
        hact = self.bv(A_HACT, MC * 512)
        dbuf = [self.bv(0, 4 * MSUB), self.bv(16384, 4 * MSUB)]
        hsil = [self.bv(A_HSIL + i * 512, 512) for i in range(2)]
        wgv = self.wg_gath.ap().rearrange("(s p) f -> p s f", p=128)
        wuv = self.wu_gath.ap().rearrange("(s p) f -> p s f", p=128)
        wdv = self.wd_gath.ap().rearrange("(s p) f -> p s f", p=128)
        xs = self.fv(45056, 2048)
        ot = self.fv(47104, 2048)
        ps_g = [self.ps_a, self.ps_d[0]]
        ps_u = [self.ps_b, self.ps_d[1]]
        for th in range(2):
            self.ln_big_half(src, th, xfm)
            pe0, ae0, de0 = self.pe_n, self.ae_n, self.de_n
            qs_after = {}
            for mg in range(MC):
                b = mg % 2
                if mg >= 2:
                    nc.sync.wait_ge(self.pe_s, pe0 + 2 * (mg - 2) + 2)
                self.ld(gbuf[b], wgv[:, mg, :])
                self.ld(ubuf[b], wuv[:, mg, :])
                qs_after[mg] = self.qs_n
                nc.tensor.wait_ge(self.qs, qs_after[mg])
                if mg >= 2:
                    nc.tensor.wait_ge(self.ae_s, ae0 + (mg - 2) + 1)
                    nc.tensor.wait_ge(self.de_s, de0 + (mg - 2) + 1)
                for k in range(VC):
                    mm = nc.tensor.matmul(ps_g[b].ap(),
                                          gbuf[b][:, k * 128:(k + 1) * 128],
                                          xfm[:, k * 512:(k + 1) * 512],
                                          start=(k == 0), stop=(k == VC - 1))
                    if k == VC - 1:
                        mm.then_inc(self.pe_s, 1)
                self.pe_n += 1
                for k in range(VC):
                    mm = nc.tensor.matmul(ps_u[b].ap(),
                                          ubuf[b][:, k * 128:(k + 1) * 128],
                                          xfm[:, k * 512:(k + 1) * 512],
                                          start=(k == 0), stop=(k == VC - 1))
                    if k == VC - 1:
                        mm.then_inc(self.pe_s, 1)
                self.pe_n += 1
                nc.scalar.wait_ge(self.pe_s, pe0 + 2 * mg + 1)
                if mg >= 2:
                    nc.scalar.wait_ge(self.de_s, de0 + (mg - 2) + 1)
                nc.scalar.activation(out=hsil[b], in_=ps_g[b].ap(),
                                     func=AF.Silu).then_inc(self.ae_s, 1)
                self.ae_n += 1
                nc.vector.wait_ge(self.pe_s, pe0 + 2 * mg + 2)
                nc.vector.wait_ge(self.ae_s, ae0 + mg + 1)
                nc.vector.tensor_tensor(
                    out=hact[:, mg * 512:(mg + 1) * 512], in0=hsil[b],
                    in1=ps_u[b].ap(), op=OP.mult).then_inc(self.de_s, 1)
                self.de_n += 1
            nc.sync.wait_ge(self.de_s, de0 + MC)
            if th == 0:
                nc.sync.wait_ge(self.sc_w[2], 1)
            self.bar()
            pe0, de0 = self.pe_n, self.de_n
            qs_xs = {}
            for hb in range(NCORES):
                for ms in range(NSUB):
                    j = hb * NSUB + ms
                    b = j % 2
                    if j >= 2:
                        nc.sync.wait_ge(self.pe_s, pe0 + j - 1)
                    blk0 = hb * 16 + ms * 4
                    self.ld(dbuf[b].rearrange("p (r f) -> p r f", r=4),
                            wdv[:, blk0:blk0 + 4, :])
                    qs_after[j] = self.qs_n
                    if ms == NSUB - 1:
                        if hb > 0:
                            nc.sync.wait_ge(self.de_s, de0 + hb)
                        for tc in range(4):
                            r0 = th * TH + tc * 128
                            self.ld(xs[:, tc * 512:(tc + 1) * 512],
                                    src.ap()[r0:r0 + 128,
                                             hb * 512:(hb + 1) * 512])
                        qs_xs[hb] = self.qs_n
                    nc.tensor.wait_ge(self.qs, qs_after[j])
                    if ms == 0 and hb > 0:
                        nc.tensor.wait_ge(self.de_s, de0 + hb)
                    drv = dbuf[b].rearrange("p (r f) -> p r f", r=4)
                    for k in range(28):
                        mg = ms * 28 + k
                        for tc in range(4):
                            mm = nc.tensor.matmul(
                                self.ps_d[tc].ap(),
                                hact[:, mg * 512 + tc * 128:
                                     mg * 512 + (tc + 1) * 128],
                                drv[:, :, k * 128:(k + 1) * 128],
                                start=(ms == 0 and k == 0),
                                stop=(ms == NSUB - 1 and k == 27))
                            if k == 27 and tc == 3:
                                mm.then_inc(self.pe_s, 1)
                    self.pe_n += 1
                nc.vector.wait_ge(self.pe_s, pe0 + NSUB * (hb + 1))
                nc.vector.wait_ge(self.qs, qs_xs[hb])
                if hb > 0:
                    nc.vector.wait_ge(self.qa, self.qa_n)
                for tc in range(4):
                    op_ = nc.vector.scalar_tensor_tensor(
                        out=ot[:, tc * 512:(tc + 1) * 512],
                        in0=self.ps_d[tc].ap(), scalar=0.5,
                        in1=xs[:, tc * 512:(tc + 1) * 512],
                        op0=OP.mult, op1=OP.add)
                    if tc == 3:
                        op_.then_inc(self.de_s, 1)
                self.de_n += 1
                nc.scalar.wait_ge(self.de_s, de0 + hb + 1)
                for tc in range(4):
                    r0 = th * TH + tc * 128
                    self.st(dst.ap()[r0:r0 + 128, hb * 512:(hb + 1) * 512],
                            ot[:, tc * 512:(tc + 1) * 512])
            self.st_wait()
            self.bar()

    def ln_inner(self, h_tiles, w_nm, b_nm, l, out_tiles, silu):
        nc = self.nc
        sq = self.fv(F_MTMP, 1024)
        for half in range(2):
            tgt = self.ps_d[half].ap()[0:1, :]
            for ct in range(4):
                nc.tensor.matmul(tgt, self.ones_f.ap(),
                                 h_tiles[ct][:, half * 512:(half + 1) * 512],
                                 start=(ct == 0), stop=(ct == 3))
        self.bar()
        for ct in range(4):
            nc.scalar.activation(out=sq, in_=h_tiles[ct], func=AF.Square)
            self.bar()
            for half in range(2):
                tgt = self.ps_d[2 + half].ap()[0:1, :]
                nc.tensor.matmul(tgt, self.ones_f.ap(),
                                 sq[:, half * 512:(half + 1) * 512],
                                 start=(ct == 0), stop=(ct == 3))
            self.bar()
        st_ = self.fv(F_STAT, 2048, rows=1)
        for half in range(2):
            nc.vector.tensor_copy(out=st_[:, half * 512:(half + 1) * 512],
                                  in_=self.ps_d[half].ap()[0:1, :])
            nc.vector.tensor_copy(out=st_[:, 1024 + half * 512:1024 + (half + 1) * 512],
                                  in_=self.ps_d[2 + half].ap()[0:1, :])
        nc.vector.tensor_scalar(out=st_[:, 0:1024], in0=st_[:, 0:1024],
                                scalar1=-1.0 / DA, scalar2=None, op0=OP.mult)
        nc.vector.tensor_scalar(out=st_[:, 1024:2048], in0=st_[:, 1024:2048],
                                scalar1=1.0 / DA, scalar2=None, op0=OP.mult)
        self.bar()
        musq = self.fv(F_MTMP, 1024, rows=1)
        nc.scalar.activation(out=musq, in_=st_[:, 0:1024], func=AF.Square)
        self.bar()
        nc.vector.tensor_tensor(out=st_[:, 1024:2048], in0=st_[:, 1024:2048],
                                in1=musq, op=OP.subtract)
        self.bar()
        nc.scalar.activation(out=st_[:, 1024:2048], in_=st_[:, 1024:2048],
                             func=AF.Ln, bias=self.eps_c[0:1, :], scale=1.0)
        self.bar()
        nc.scalar.activation(out=st_[:, 1024:2048], in_=st_[:, 1024:2048],
                             func=AF.Exp, scale=self.mhalf_c[0:1, :])
        self.bar()
        for half in range(2):
            nc.tensor.matmul(self.ps_d[half].ap(), self.ones_r.ap(),
                             st_[:, half * 512:(half + 1) * 512],
                             start=True, stop=True)
            nc.tensor.matmul(self.ps_d[2 + half].ap(), self.ones_r.ap(),
                             st_[:, 1024 + half * 512:1024 + (half + 1) * 512],
                             start=True, stop=True)
        self.bar()
        tmp = self.fv(F_MTMP, 1024)
        for ct in range(4):
            for half in range(2):
                s0 = half * 512
                nc.vector.tensor_tensor(out=tmp[:, s0:s0 + 512],
                                        in0=h_tiles[ct][:, s0:s0 + 512],
                                        in1=self.ps_d[half].ap(), op=OP.add)
                nc.vector.tensor_tensor(out=tmp[:, s0:s0 + 512],
                                        in0=tmp[:, s0:s0 + 512],
                                        in1=self.ps_d[2 + half].ap(), op=OP.mult)
            wc = self.pcol(w_nm, l, ct)
            bc = self.pcol(b_nm, l, ct)
            if silu:
                nc.vector.tensor_scalar(out=tmp, in0=tmp, scalar1=wc, scalar2=bc,
                                        op0=OP.mult, op1=OP.add)
                self.bar()
                nc.scalar.activation(out=out_tiles[ct], in_=tmp, func=AF.Silu)
            else:
                nc.vector.tensor_scalar(out=out_tiles[ct], in0=tmp, scalar1=wc,
                                        scalar2=bc, op0=OP.mult, op1=OP.add)
            self.bar()

    def sconv(self, l, h_tiles, y_tiles):
        nc = self.nc
        u_t = [self.fv(F_U + ct * 1024, 1024) for ct in range(4)]
        v_t = [self.fv(F_V + ct * 1024, 1024) for ct in range(4)]
        C_t = [self.fv(F_C + ct * 1024, 1024) for ct in range(4)]
        S_t = [self.fv(F_S + ct * 1024, 1024) for ct in range(4)]
        P_t = [self.fv(F_P + ct * 1056, 1056) for ct in range(4)]
        rho_t = self.fv(F_RHO, 1024)
        tmod = self.fv(F_TMD, 1024)
        CTH = self.scol(0, 4)
        STH = self.scol(4, 4)
        RH = self.scol(8, 4)
        T0 = self.scol(12, 4)
        T1 = self.scol(16, 4)
        T2 = self.scol(20, 4)
        stage = self.scol(24, 8)
        R1C = self.scol(32, 4)
        R1S = self.scol(36, 4)
        BC = self.scol(40, 4)
        BS = self.scol(44, 4)
        UI0 = self.scol(48, 4)
        VI0 = self.scol(52, 4)
        UIN = self.scol(56, 4)
        VIN = self.scol(60, 4)
        pr4 = self.pcol("phz_re", l, 0, 4)
        pi4 = self.pcol("phz_im", l, 0, 4)
        lcr4 = self.pcol("lc_re", l, 0, 4)
        lci4 = self.pcol("lc_im", l, 0, 4)
        cmask = self.pcol("cmask", 0, 0, 8)
        nc.scalar.activation(out=T0, in_=pr4, func=AF.Square)
        nc.scalar.activation(out=T1, in_=pi4, func=AF.Square)
        self.bar()
        nc.vector.tensor_tensor(out=T0, in0=T0, in1=T1, op=OP.add)
        self.bar()
        nc.scalar.activation(out=T1, in_=T0, func=AF.Sqrt)
        self.bar()
        nc.vector.reciprocal(out=T2, in_=T1)
        self.bar()
        nc.vector.tensor_tensor(out=T2, in0=T0, in1=T2, op=OP.mult)
        self.bar()
        nc.vector.tensor_tensor(out=T1, in0=T1, in1=T2, op=OP.add)
        self.bar()
        nc.vector.tensor_scalar(out=T1, in0=T1, scalar1=0.5, scalar2=None,
                                op0=OP.mult)
        self.bar()
        nc.vector.reciprocal(out=T2, in_=T1)
        nc.scalar.activation(out=RH, in_=T1, func=AF.Tanh)
        self.bar()
        nc.vector.tensor_tensor(out=CTH, in0=pr4, in1=T2, op=OP.mult)
        nc.vector.tensor_tensor(out=STH, in0=pi4, in1=T2, op=OP.mult)
        self.bar()
        nc.vector.tensor_tensor(out=T0, in0=STH, in1=lci4, op=OP.mult)
        nc.vector.tensor_tensor(out=T1, in0=CTH, in1=lci4, op=OP.mult)
        self.bar()
        nc.vector.tensor_tensor(out=UI0, in0=CTH, in1=lcr4, op=OP.mult)
        nc.vector.tensor_tensor(out=VI0, in0=STH, in1=lcr4, op=OP.mult)
        self.bar()
        nc.vector.tensor_tensor(out=UI0, in0=UI0, in1=T0, op=OP.subtract)
        nc.vector.tensor_tensor(out=VI0, in0=VI0, in1=T1, op=OP.add)
        self.bar()
        nc.vector.tensor_scalar(out=VI0, in0=VI0, scalar1=-1.0, scalar2=None,
                                op0=OP.mult)
        for ct in range(4):
            Ct, St, Pt = C_t[ct], S_t[ct], P_t[ct]
            nc.vector.memset(Ct[:, 0:1], 1.0)
            nc.vector.memset(St[:, 0:1], 0.0)
            nc.vector.memset(Pt[:, 0:1], 1.0)
        self.bar()
        for ct in range(4):
            Ct, St, Pt = C_t[ct], S_t[ct], P_t[ct]
            nc.vector.tensor_copy(out=Ct[:, 1:2], in_=CTH[:, ct:ct + 1])
            nc.vector.tensor_copy(out=St[:, 1:2], in_=STH[:, ct:ct + 1])
            nc.vector.tensor_copy(out=Pt[:, 1:2], in_=RH[:, ct:ct + 1])
        self.bar()
        f = 2
        while f < 1056:
            m = min(f - 1, 1056 - f)
            mcs = min(f - 1, 1024 - f) if f < 1024 else 0
            for ct in range(4):
                Ct, St, Pt = C_t[ct], S_t[ct], P_t[ct]
                if mcs > 0:
                    scc = Ct[:, f - 1:f]
                    scs = St[:, f - 1:f]
                    tbuf = self.fv(F_TMD, 1024)[:, 0:mcs]
                    nc.vector.tensor_scalar_mul(out=tbuf, in0=Ct[:, 1:1 + mcs],
                                                scalar1=scs)
                    nc.vector.scalar_tensor_tensor(
                        out=St[:, f:f + mcs], in0=St[:, 1:1 + mcs], scalar=scc,
                        in1=tbuf, op0=OP.mult, op1=OP.add)
                    nc.vector.tensor_scalar_mul(out=tbuf, in0=St[:, 1:1 + mcs],
                                                scalar1=scs)
                    nc.vector.scalar_tensor_tensor(
                        out=Ct[:, f:f + mcs], in0=Ct[:, 1:1 + mcs], scalar=scc,
                        in1=tbuf, op0=OP.mult, op1=OP.subtract)
                nc.vector.tensor_scalar_mul(out=Pt[:, f:f + m],
                                            in0=Pt[:, 1:1 + m],
                                            scalar1=Pt[:, f - 1:f])
            self.bar()
            f += m
        for ct in range(4):
            Ct, St = C_t[ct], S_t[ct]
            nc.vector.tensor_scalar_mul(out=T0[:, ct:ct + 1],
                                        in0=St[:, 1023:1024],
                                        scalar1=STH[:, ct:ct + 1])
            nc.vector.tensor_scalar_mul(out=T1[:, ct:ct + 1],
                                        in0=Ct[:, 1023:1024],
                                        scalar1=STH[:, ct:ct + 1])
        self.bar()
        for ct in range(4):
            Ct, St = C_t[ct], S_t[ct]
            nc.vector.scalar_tensor_tensor(out=R1C[:, ct:ct + 1],
                                           in0=Ct[:, 1023:1024],
                                           scalar=CTH[:, ct:ct + 1],
                                           in1=T0[:, ct:ct + 1],
                                           op0=OP.mult, op1=OP.subtract)
            nc.vector.scalar_tensor_tensor(out=R1S[:, ct:ct + 1],
                                           in0=St[:, 1023:1024],
                                           scalar=CTH[:, ct:ct + 1],
                                           in1=T1[:, ct:ct + 1],
                                           op0=OP.mult, op1=OP.add)
        self.bar()
        pw_c = [self.fv(F_TMD + ct * 16, 8) for ct in range(4)]
        pw_s = [self.fv(F_TMD + ct * 16 + 8, 8) for ct in range(4)]
        for ct in range(4):
            nc.vector.memset(pw_c[ct][:, 0:1], 1.0)
            nc.vector.memset(pw_s[ct][:, 0:1], 0.0)
        self.bar()
        for k in range(1, 8):
            for ct in range(4):
                nc.vector.tensor_scalar_mul(out=T0[:, ct:ct + 1],
                                            in0=pw_s[ct][:, k - 1:k],
                                            scalar1=R1S[:, ct:ct + 1])
                nc.vector.tensor_scalar_mul(out=T1[:, ct:ct + 1],
                                            in0=pw_c[ct][:, k - 1:k],
                                            scalar1=R1S[:, ct:ct + 1])
            self.bar()
            for ct in range(4):
                nc.vector.scalar_tensor_tensor(out=pw_c[ct][:, k:k + 1],
                                               in0=pw_c[ct][:, k - 1:k],
                                               scalar=R1C[:, ct:ct + 1],
                                               in1=T0[:, ct:ct + 1],
                                               op0=OP.mult, op1=OP.subtract)
                nc.vector.scalar_tensor_tensor(out=pw_s[ct][:, k:k + 1],
                                               in0=pw_s[ct][:, k - 1:k],
                                               scalar=R1C[:, ct:ct + 1],
                                               in1=T1[:, ct:ct + 1],
                                               op0=OP.mult, op1=OP.add)
            self.bar()
        cmask8 = self.pcol("cmask8", 0, 0, 64)
        pw_all = self.fv(F_TMD, 64)
        sel64 = self.fv(F_TMD + 64, 64)
        bcs8 = self.fv(F_TMD + 128, 8)
        nc.vector.tensor_tensor(out=sel64, in0=pw_all, in1=cmask8, op=OP.mult)
        self.bar()
        nc.vector.tensor_reduce(
            out=bcs8.unsqueeze(2),
            in_=sel64.rearrange("p (g k) -> p g k", k=8), axis=AXX, op=OP.add)
        self.bar()
        for ct in range(4):
            nc.vector.tensor_copy(out=BC[:, ct:ct + 1],
                                  in_=bcs8[:, 2 * ct:2 * ct + 1])
            nc.vector.tensor_copy(out=BS[:, ct:ct + 1],
                                  in_=bcs8[:, 2 * ct + 1:2 * ct + 2])
        self.bar()
        de0 = self.de_n
        for ct in range(4):
            Ct, St = C_t[ct], S_t[ct]
            bcc = BC[:, ct:ct + 1]
            bsc = BS[:, ct:ct + 1]
            nc.vector.tensor_scalar_mul(out=tmod, in0=St[:, 0:1024],
                                        scalar1=bsc)
            nc.vector.tensor_scalar_mul(out=St[:, 0:1024], in0=St[:, 0:1024],
                                        scalar1=bcc)
            nc.vector.scalar_tensor_tensor(out=St[:, 0:1024],
                                           in0=Ct[:, 0:1024], scalar=bsc,
                                           in1=St[:, 0:1024],
                                           op0=OP.mult, op1=OP.add)
            nc.vector.tensor_scalar_mul(out=Ct[:, 0:1024], in0=Ct[:, 0:1024],
                                        scalar1=bcc)
            nc.vector.tensor_tensor(out=Ct[:, 0:1024], in0=Ct[:, 0:1024],
                                    in1=tmod, op=OP.subtract)
            nc.vector.memset(rho_t, 1.0)
            self.bar()
            nc.vector.tensor_scalar_mul(out=rho_t, in0=rho_t,
                                        scalar1=RH[:, ct:ct + 1])
            self.bar()
            nc.vector.tensor_tensor(out=tmod, in0=Ct[:, 0:1024],
                                    in1=y_tiles[ct], op=OP.mult)
            nc.vector.tensor_tensor_scan(out=u_t[ct], data0=rho_t,
                                         data1=tmod, initial=0.0,
                                         op0=OP.mult, op1=OP.add)
            nc.vector.tensor_tensor(out=tmod, in0=St[:, 0:1024],
                                    in1=y_tiles[ct], op=OP.mult)
            nc.vector.tensor_tensor_scan(out=v_t[ct], data0=rho_t,
                                         data1=tmod, initial=0.0,
                                         op0=OP.mult, op1=OP.add)
            nc.vector.tensor_copy(out=stage[:, 2 * ct:2 * ct + 1],
                                  in_=u_t[ct][:, 1023:1024])
            nc.vector.tensor_copy(out=stage[:, 2 * ct + 1:2 * ct + 2],
                                  in_=v_t[ct][:, 1023:1024])
        self.bar()
        nc.vector.engine_nop().then_inc(self.st_s, 1)
        self.st_n += 1
        nc.gpsimd.wait_ge(self.st_s, self.st_n)
        nc.gpsimd.dma_start(
            out=self.cc_in[l].ap().rearrange("a (p c) -> (a p) c", p=128),
            in_=stage).then_inc(self.qg, 16)
        self.qg_n += 16
        nc.gpsimd.wait_ge(self.qg, self.qg_n)
        nc.gpsimd.collective_compute(
            "AllGather", OP.bypass, replica_groups=[list(range(NCORES))],
            ins=[self.cc_in[l].ap().opt()], outs=[self.cc_out[l].ap().opt()],
        ).then_inc(self.sc_l[l])
        nc.sync.wait_ge(self.sc_l[l], 1)
        eraw = self.fv(F_TMD, 1024, rows=8)
        self.ld(eraw, self.cc_out[l].ap())
        self.ld_wait()
        self.bar()
        ends = self.fv(F_RHO, 64)
        erv = eraw.rearrange("r (p c) -> r p c", c=8)
        for ct in range(4):
            for uv in range(2):
                nc.tensor.transpose(self.ps_m.ap()[:, 8 * (2 * ct + uv):
                                                   8 * (2 * ct + uv) + 8],
                                    erv[:, :, 2 * ct + uv],
                                    self.ident_f.ap()[0:8, 0:8])
        self.bar()
        nc.vector.tensor_copy(out=ends, in_=self.ps_m.ap()[:, 0:64])
        self.bar()
        w8u = [self.fv(F_TMD + ct * 16, 8) for ct in range(4)]
        w8v = [self.fv(F_TMD + ct * 16 + 8, 8) for ct in range(4)]
        for ct in range(4):
            nc.vector.tensor_copy(out=w8u[ct][:, 0:1], in_=UI0[:, ct:ct + 1])
            nc.vector.tensor_copy(out=w8v[ct][:, 0:1], in_=VI0[:, ct:ct + 1])
        self.bar()
        for cx in range(1, 8):
            for ct in range(4):
                E = P_t[ct][:, 1024:1025]
                Ue = ends[:, (ct * 2) * 8:(ct * 2) * 8 + 8]
                Ve = ends[:, (ct * 2 + 1) * 8:(ct * 2 + 1) * 8 + 8]
                nc.vector.scalar_tensor_tensor(
                    out=w8u[ct][:, cx:cx + 1], in0=w8u[ct][:, cx - 1:cx],
                    scalar=E, in1=Ue[:, cx - 1:cx], op0=OP.mult, op1=OP.add)
                nc.vector.scalar_tensor_tensor(
                    out=w8v[ct][:, cx:cx + 1], in0=w8v[ct][:, cx - 1:cx],
                    scalar=E, in1=Ve[:, cx - 1:cx], op0=OP.mult, op1=OP.add)
            self.bar()
        w8all = self.fv(F_TMD, 64)
        selb64 = self.fv(F_TMD + 64, 64)
        uv8 = self.fv(F_TMD + 128, 8)
        nc.vector.tensor_tensor(out=selb64, in0=w8all, in1=cmask8, op=OP.mult)
        self.bar()
        nc.vector.tensor_reduce(
            out=uv8.unsqueeze(2),
            in_=selb64.rearrange("p (g k) -> p g k", k=8), axis=AXX, op=OP.add)
        self.bar()
        for ct in range(4):
            nc.vector.tensor_copy(out=UIN[:, ct:ct + 1],
                                  in_=uv8[:, 2 * ct:2 * ct + 1])
            nc.vector.tensor_copy(out=VIN[:, ct:ct + 1],
                                  in_=uv8[:, 2 * ct + 1:2 * ct + 2])
        self.bar()
        for ct in range(4):
            nc.vector.scalar_tensor_tensor(out=u_t[ct], in0=P_t[ct][:, 1:1025],
                                           scalar=UIN[:, ct:ct + 1],
                                           in1=u_t[ct], op0=OP.mult,
                                           op1=OP.add)
            nc.vector.scalar_tensor_tensor(out=v_t[ct], in0=P_t[ct][:, 1:1025],
                                           scalar=VIN[:, ct:ct + 1],
                                           in1=v_t[ct], op0=OP.mult,
                                           op1=OP.add)
            nc.vector.tensor_tensor(out=u_t[ct], in0=u_t[ct],
                                    in1=C_t[ct][:, 0:1024], op=OP.mult)
            nc.vector.tensor_tensor(out=v_t[ct], in0=v_t[ct],
                                    in1=S_t[ct][:, 0:1024], op=OP.mult)
            nc.vector.tensor_tensor(out=u_t[ct], in0=u_t[ct], in1=v_t[ct],
                                    op=OP.add)
            scl = self.pcol("sc_lin", l, ct)
            nc.vector.scalar_tensor_tensor(out=h_tiles[ct], in0=u_t[ct],
                                           scalar=scl, in1=h_tiles[ct],
                                           op0=OP.mult, op1=OP.add)
        self.bar()

    def attn(self, src, dst):
        nc = self.nc
        h_tiles = [self.fv(F_H + ct * 1024, 1024) for ct in range(4)]
        y_tiles = [self.fv(F_Y + ct * 1024, 1024) for ct in range(4)]
        xfm = self.bv(A_XFM2, 16384)
        wsin = self.bv(A_WIN, 4096)
        winTv = self.winT.ap().rearrange("(s p) f -> p s f", p=128)
        for th in range(2):
            self.ln_big_half(src, th, xfm)
            for dc in range(DC):
                self.ld(wsin, winTv[:, dc, :])
                self.ld_wait()
                self.bar()
                for k in range(VC):
                    nc.tensor.matmul(self.ps_a.ap(),
                                     wsin[:, k * 128:(k + 1) * 128],
                                     xfm[:, k * 512:(k + 1) * 512],
                                     start=(k == 0), stop=(k == VC - 1))
                self.bar()
                nc.vector.tensor_scalar(
                    out=h_tiles[dc][:, th * 512:(th + 1) * 512],
                    in0=self.ps_a.ap(), scalar1=self.pcol("bin", 0, dc),
                    scalar2=None, op0=OP.add)
                self.bar()
        self.dbg_h(0, h_tiles)
        u_mlp = self.bv(A_UMLP, 8192)
        y2 = self.bv(A_Y2, 4096)
        wstr = self.bv(A_WSTR, 8192)
        w1Tv = self.w1T.ap().rearrange("(s p) f -> p s f", p=128)
        w2Tv = self.w2T.ap().rearrange("(s p) f -> p s f", p=128)
        tmp = self.fv(F_MTMP, 512)
        for l in range(NL):
            self.ln_inner(h_tiles, "ln1_w", "ln1_b", l, y_tiles, True)
            if l == 0:
                self.dbg_h(1, y_tiles)
            self.sconv(l, h_tiles, y_tiles)
            if l == 0:
                self.dbg_h(2, h_tiles)
            y2t = [y2[:, ct * 1024:(ct + 1) * 1024] for ct in range(4)]
            self.ln_inner(h_tiles, "ln2_w", "ln2_b", l, y2t, False)

            self.ld(wstr, w1Tv[:, l, :])
            self.ld_wait()
            self.bar()
            for th in range(2):
                for fc in range(FC):
                    for dc in range(DC):
                        nc.tensor.matmul(
                            self.ps_a.ap(),
                            wstr[:, (fc * 4 + dc) * 128:(fc * 4 + dc + 1) * 128],
                            y2t[dc][:, th * 512:(th + 1) * 512],
                            start=(dc == 0), stop=(dc == DC - 1))
                    self.bar()
                    nc.scalar.activation(
                        out=u_mlp[:, fc * 512:(fc + 1) * 512],
                        in_=self.ps_a.ap(), func=AF.Silu,
                        bias=self.pcol("b1", l, fc), scale=1.0)
                    self.bar()
                if th == 0:
                    u_sav = self.bv(A_UMLP + 8192, 8192)
                    nc.vector.tensor_copy(out=u_sav, in_=u_mlp)
                    self.bar()
            self.ld(wstr, w2Tv[:, l, :])
            self.ld_wait()
            self.bar()
            for th in range(2):
                usrc = self.bv(A_UMLP + 8192, 8192) if th == 0 else u_mlp
                for dc in range(DC):
                    for fc in range(FC):
                        nc.tensor.matmul(
                            self.ps_b.ap(),
                            wstr[:, (dc * 16 + fc) * 128:(dc * 16 + fc + 1) * 128],
                            usrc[:, fc * 512:(fc + 1) * 512],
                            start=(fc == 0), stop=(fc == FC - 1))
                    self.bar()
                    nc.vector.tensor_scalar(out=tmp, in0=self.ps_b.ap(),
                                            scalar1=self.pcol("b2", l, dc),
                                            scalar2=None, op0=OP.add)
                    self.bar()
                    nc.vector.tensor_tensor(
                        out=h_tiles[dc][:, th * 512:(th + 1) * 512],
                        in0=h_tiles[dc][:, th * 512:(th + 1) * 512],
                        in1=tmp, op=OP.add)
                    self.bar()

            self.dbg_h(3 + l, h_tiles)
        hln = self.bv(A_HLN, 4096)
        hlnt = [hln[:, ct * 1024:(ct + 1) * 1024] for ct in range(4)]
        self.ln_inner(h_tiles, "lnl_w", "lnl_b", 0, hlnt, False)
        self.dbg_h(7, hlnt)
        wo = [self.bv(A_WO + dc * 4096, 4096) for dc in range(DC)]
        woutTv = self.woutT.ap().rearrange("(s p) f -> p s f", p=128)
        for dc in range(DC):
            self.ld(wo[dc], woutTv[:, dc, :])
        self.ld_wait()
        self.bar()
        boutt = self.fv(F_BOUT, 512)
        x1s = self.fv(F_X1S, 512)
        wtmp = self.fv(F_WTMP, 512)
        for vb in range(8):
            self.ld(boutt, self.bout_rep.ap()[:, vb * 512:(vb + 1) * 512])
            self.ld_wait()
            self.bar()
            for tcg in range(8):
                for dc in range(DC):
                    nc.tensor.matmul(
                        self.ps_d[0].ap(),
                        hln[:, dc * 1024 + tcg * 128:dc * 1024 + (tcg + 1) * 128],
                        wo[dc][:, vb * 512:(vb + 1) * 512],
                        start=(dc == 0), stop=(dc == DC - 1))
                self.bar()
                self.ld(x1s, src.ap()[tcg * 128:(tcg + 1) * 128,
                                      vb * 512:(vb + 1) * 512])
                self.ld_wait()
                self.bar()
                nc.vector.tensor_tensor(out=wtmp, in0=self.ps_d[0].ap(),
                                        in1=boutt, op=OP.add)
                self.bar()
                nc.vector.tensor_tensor(out=wtmp, in0=wtmp, in1=x1s, op=OP.add)
                self.bar()
                self.st(dst.ap()[tcg * 128:(tcg + 1) * 128,
                                 vb * 512:(vb + 1) * 512], wtmp)
                self.st_wait()
                self.bar()

    def dbg_h(self, stage, tiles):
        if not DEBUG:
            return
        self.allbar()
        for ct in range(4):
            dst = self.dbgh.ap()[stage, ct * 128:(ct + 1) * 128, :]
            if tiles[ct].dtype == BF16:
                self.nc.gpsimd.dma_start(out=dst, in_=tiles[ct]).then_inc(self.qg, 16)
                self.qg_n += 16
            else:
                self.st(dst, tiles[ct])
        self.st_wait()
        self.nc.gpsimd.wait_ge(self.qg, self.qg_n)
        self.allbar()

    def dbg_copy(self):
        if not DEBUG:
            return
        buf = self.fv(0, 4096)
        for (srct, dstt) in [(self.x1b, self.dbg1), (self.x2b, self.dbg2)]:
            for tt in range(8):
                self.ld(buf, srct.ap()[tt * 128:(tt + 1) * 128, :])
                self.ld_wait()
                self.bar()
                self.st(dstt.ap()[tt * 128:(tt + 1) * 128, :], buf)
                self.st_wait()
                self.bar()

    def emit(self):
        self.prep_consts()
        self.prep_ffn_weights()
        self.prep_attn_weights()
        self.bar()
        self.ffn(self.x_in, self.x1b)
        self.bar()
        self.attn(self.x1b, self.x2b)
        self.bar()
        self.ffn(self.x2b, self.out)
        self.bar()
        self.dbg_copy()
        self.st_wait()
        self.nc.sync.wait_ge(self.qa, self.qa_n)
        self.bar()


_CACHE = {}


def _build():
    if "nc" not in _CACHE:
        kb = KB()
        kb.emit()
        _CACHE["nc"] = kb.nc
    return _CACHE["nc"]


def _pack_params(inputs, core):
    f32 = np.float32
    prm = np.zeros((128, PC_DATA), f32)

    def put4(nm, arr):
        a = np.asarray(arr, f32).reshape(NL, 4, 128)
        for l in range(NL):
            c = PCMAP[(nm, l)]
            prm[:, c:c + 4] = a[l].T

    put4("ln1_w", inputs["L0_attn_ln1_w"])
    put4("ln1_b", inputs["L0_attn_ln1_b"])
    put4("ln2_w", inputs["L0_attn_ln2_w"])
    put4("ln2_b", inputs["L0_attn_ln2_b"])
    put4("sc_lin", inputs["L0_attn_sc_lin"])
    put4("phz_re", inputs["L0_attn_phz_re"])
    put4("phz_im", inputs["L0_attn_phz_im"])
    put4("lc_re", inputs["L0_attn_lc_re"])
    put4("lc_im", inputs["L0_attn_lc_im"])
    put4("b2", inputs["L0_attn_b2"])
    b1 = np.asarray(inputs["L0_attn_b1"], f32).reshape(NL, 16, 128)
    for l in range(NL):
        c = PCMAP[("b1", l)]
        prm[:, c:c + 16] = b1[l].T
    for nm, key in [("bin", "L0_attn_bin"), ("lnl_w", "L0_attn_lnl_w"),
                    ("lnl_b", "L0_attn_lnl_b")]:
        c = PCMAP[(nm, 0)]
        prm[:, c:c + 4] = np.asarray(inputs[key], f32).reshape(4, 128).T
    c = PCMAP[("cmask", 0)]
    prm[:, c + core] = 1.0
    c = PCMAP[("negt", 0)]
    prm[:, c:c + 14] = -np.arange(1, 15, dtype=f32)[None, :]
    c = PCMAP[("cmask8", 0)]
    m8 = np.zeros(8, f32)
    m8[core] = 1.0
    prm[:, c:c + 64] = np.tile(m8, 8)[None, :]
    return prm


def kernel(**inputs):
    nc = _build()
    f32 = np.float32
    x = np.asarray(inputs["L0_attn_x"], f32).reshape(8192, D)
    gq = np.asarray(inputs["L0_ffn_gate_q"], np.int32)
    uq = np.asarray(inputs["L0_ffn_up_q"], np.int32)
    dq = np.asarray(inputs["L0_ffn_down_q"], np.int32)
    ga = np.asarray(inputs["L0_ffn_gate_absmax"], f32).reshape(M, 64)
    ua = np.asarray(inputs["L0_ffn_up_absmax"], f32).reshape(M, 64)
    da = np.asarray(inputs["L0_ffn_down_absmax"], f32).reshape(D, M // 64)
    bout_rep = np.tile(np.asarray(inputs["L0_attn_bout"], f32)[None, :], (128, 1))
    in_maps = []
    for c in range(NCORES):
        mo, ho = c * MO, c * HO
        in_maps.append({
            "x_in": np.ascontiguousarray(x[c * T:(c + 1) * T]),
            "gq": np.ascontiguousarray(gq[mo:mo + MO]),
            "uq": np.ascontiguousarray(uq[mo:mo + MO]),
            "dq": np.ascontiguousarray(dq[ho:ho + HO]),
            "ga": np.ascontiguousarray(ga[mo:mo + MO]),
            "ua": np.ascontiguousarray(ua[mo:mo + MO]),
            "da": np.ascontiguousarray(da[ho:ho + HO]),
            "win": np.asarray(inputs["L0_attn_Win"], f32),
            "w1": np.asarray(inputs["L0_attn_W1"], f32),
            "w2": np.asarray(inputs["L0_attn_W2"], f32),
            "wout": np.asarray(inputs["L0_attn_Wout"], f32),
            "prm_in": _pack_params(inputs, c),
            "bout_rep": bout_rep,
        })
    res = run_bass_kernel_spmd(nc, in_maps, core_ids=list(range(NCORES)))
    global _LAST_RES
    _LAST_RES = res
    out = np.concatenate([res.results[c]["out"] for c in range(NCORES)], axis=0)
    return out.reshape(1, 8192, D).astype(np.float32)


# version 10
# speedup vs baseline: 1.0539x; 1.0539x over previous
import numpy as np
import ml_dtypes

import concourse.bass as bass
import concourse.mybir as mybir
from concourse.bass_utils import run_bass_kernel_spmd

F32 = mybir.dt.float32
BF16 = mybir.dt.bfloat16
I32 = mybir.dt.int32
AF = mybir.ActivationFunctionType
OP = mybir.AluOpType
AXX = mybir.AxisListType.X

NCORES = 8
T = 1024
TH = 512
D = 4096
M = 14336
DA = 512
DFF = 2048
NL = 4
EPS = 1e-5

MC = M // 128
VC = D // 128
DC = DA // 128
FC = DFF // 128
MO = M // NCORES
HO = D // NCORES
MSUB = 3584
NSUB = M // MSUB

NF4 = [-1.0, -0.6961928009986877, -0.5250730514526367, -0.39491748809814453,
       -0.28444138169288635, -0.18477343022823334, -0.09105003625154495, 0.0,
       0.07958029955625534, 0.16093020141124725, 0.24611230194568634,
       0.33791524171829224, 0.44070982933044434, 0.5626170039176941,
       0.7229568362236023, 1.0]
NF_A = NF4[0]
NF_B = NF4[1] - NF4[0]
NF_G = [NF4[t + 1] - 2 * NF4[t] + NF4[t - 1] for t in range(1, 15)]

_LAST_RES = None
DEBUG = False

PQ = ["ln1_w", "ln1_b", "ln2_w", "ln2_b", "sc_lin",
      "phz_re", "phz_im", "lc_re", "lc_im", "b2"]
PCMAP = {}
_col = 0
for _l in range(NL):
    for _nm in PQ:
        PCMAP[(_nm, _l)] = _col
        _col += 4
for _l in range(NL):
    PCMAP[("b1", _l)] = _col
    _col += 16
for _nm in ["bin", "lnl_w", "lnl_b"]:
    PCMAP[(_nm, 0)] = _col
    _col += 4
PCMAP[("cmask", 0)] = _col
_col += 8
PCMAP[("negt", 0)] = _col
_col += 14
PCMAP[("cmask8", 0)] = _col
_col += 64
PC_DATA = _col
PC_SCR = PC_DATA
PCOLS = PC_DATA + 76

AR = 102400
A_XFM = 0
A_GB = 16384
A_UB = 24576
A_HACT = 32768
A_DBUF = 16384
A_HSIL = 98816
F_XS = 0
F_OT = 2048
F_LNX = 16384
F_LNSQ = 20480
A_LNB = 49152
F_DQ_QI = 0
F_DQ_QF = 4096
F_DQ_RL = 8192
F_DQ_AC = 12288
F_DQ_AM = 16384
A_DQ_WB = 40960
A_DQ_ST = 49152
F_H = 0
F_Y = 4096
F_U = 16384
F_V = 20480
F_C = 24576
F_S = 28672
F_P = 32768
F_RHO = 37120
F_TMD = 38144
F_LN2 = 7168
A_UMLP = 78336
A_WO = 78336
A_XFM2 = 78336
A_Y2 = 94720
A_HLN = 94720
A_WSTR = 32768
A_WIN = 65536
F_STAT = 39168
F_MTMP = 14336
F_BOUT = 14336
F_X1S = 14848
F_WTMP = 15360


class KB:
    def __init__(self):
        nc = bass.Bass("TRN2", target_bir_lowering=False)
        self.nc = nc
        self.qs_n = 0
        self.qa_n = 0
        self.qg_n = 0

        dt = nc.dram_tensor
        self.x_in = dt("x_in", [T, D], F32, kind="ExternalInput")
        self.gq = dt("gq", [MO, D], I32, kind="ExternalInput")
        self.uq = dt("uq", [MO, D], I32, kind="ExternalInput")
        self.dq = dt("dq", [HO, M], I32, kind="ExternalInput")
        self.ga = dt("ga", [MO, 64], F32, kind="ExternalInput")
        self.ua = dt("ua", [MO, 64], F32, kind="ExternalInput")
        self.da = dt("da", [HO, M // 64], F32, kind="ExternalInput")
        self.win = dt("win", [DA, D], F32, kind="ExternalInput")
        self.w1 = dt("w1", [NL, DFF, DA], F32, kind="ExternalInput")
        self.w2 = dt("w2", [NL, DA, DFF], F32, kind="ExternalInput")
        self.wout = dt("wout", [D, DA], F32, kind="ExternalInput")
        self.prm_in = dt("prm_in", [128, PC_DATA], F32, kind="ExternalInput")
        self.bout_rep = dt("bout_rep", [128, D], F32, kind="ExternalInput")
        self.out = dt("out", [T, D], F32, kind="ExternalOutput")

        self.wg_own = dt("wg_own", [MO, D], BF16)
        self.wu_own = dt("wu_own", [MO, D], BF16)
        self.wd_own = dt("wd_own", [4 * NSUB * 128, MSUB], BF16)
        self.wg_gath = dt("wg_gath", [M, D], BF16)
        self.wu_gath = dt("wu_gath", [M, D], BF16)
        self.wd_gath = dt("wd_gath", [NCORES * 4 * NSUB * 128, MSUB], BF16)
        self.winT = dt("winT", [DC * 128, D], BF16)
        self.w1T = dt("w1T", [NL * 128, FC * DC * 128], BF16)
        self.w2T = dt("w2T", [NL * 128, FC * DC * 128], BF16)
        self.woutT = dt("woutT", [DC * 128, D], BF16)
        self.x1b = dt("x1b", [T, D], F32)
        self.x2b = dt("x2b", [T, D], F32)
        self.dbg1 = dt("dbg1", [T, D], F32, kind="ExternalOutput") if DEBUG else None
        self.dbg2 = dt("dbg2", [T, D], F32, kind="ExternalOutput") if DEBUG else None
        self.dbgh = dt("dbgh", [8, DA, T], F32, kind="ExternalOutput") if DEBUG else None
        self.dbgp = dt("dbgp", [128, 64], F32, kind="ExternalOutput") if DEBUG else None
        self.cc_in = [dt(f"cc_in{l}", [1, 1024], F32) for l in range(NL)]
        self.cc_out = [dt(f"cc_out{l}", [NCORES, 1024], F32) for l in range(NL)]

        self.ident_b = nc.alloc_sbuf_tensor("ident_b", [128, 128], BF16)
        self.ident_f = nc.alloc_sbuf_tensor("ident_f", [128, 128], F32)
        self.ones_f = nc.alloc_sbuf_tensor("ones_f", [128, 1], F32)
        self.ones_r = nc.alloc_sbuf_tensor("ones_r", [1, 128], F32)
        self.prm = nc.alloc_sbuf_tensor("prm", [128, PCOLS], F32)
        self.arena = nc.alloc_sbuf_tensor("arena", [128, AR], BF16)
        self.ar_b = self.arena.ap()
        self.ar_f = self.arena.ap().bitcast(F32)

        ps = nc.alloc_psum_tensor
        self.ps_a = ps("ps_a", [128, 512], F32)
        self.ps_b = ps("ps_b", [128, 512], F32)
        self.ps_d = [ps(f"ps_d{i}", [128, 512], F32) for i in range(4)]
        self.ps_t = ps("ps_t", [128, 1024], BF16)
        self.ps_m = ps("ps_m", [128, 512], F32)

        self.qs = nc.alloc_semaphore(name="qs")
        self.qa = nc.alloc_semaphore(name="qa")
        self.qg = nc.alloc_semaphore(name="qg")
        self.sc_w = [nc.alloc_semaphore(name=f"scw{i}") for i in range(3)]
        self.sc_l = [nc.alloc_semaphore(name=f"scl{i}") for i in range(NL)]
        self.pe_s = nc.alloc_semaphore(name="pe_s")
        self.ae_s = nc.alloc_semaphore(name="ae_s")
        self.de_s = nc.alloc_semaphore(name="de_s")
        self.pe_n = 0
        self.ae_n = 0
        self.de_n = 0

        ET = mybir.EngineType
        self._bar_engines = [ET.PE, ET.Activation, ET.DVE, ET.SP]
        self.st_s = nc.alloc_semaphore(name="st_s")
        self.st_n = 0
        self.pt_s = nc.alloc_semaphore(name="pt_s")
        self.ev_s = nc.alloc_semaphore(name="ev_s")
        self.pt_n = 0
        self.ev_n = 0
        self.c_ident_b = nc.inline_tensor(
            np.eye(128, dtype=np.float32).astype(ml_dtypes.bfloat16), name="cib")
        self.c_ident_f = nc.inline_tensor(np.eye(128, dtype=np.float32), name="cif")

    def bar(self):
        self.nc.multi_engine_barrier(self._bar_engines)

    def allbar(self):
        self.nc.all_engine_barrier()

    def ld(self, out, in_):
        self.nc.sync.dma_start(out=out, in_=in_).then_inc(self.qs, 16)
        self.qs_n += 16

    def ld_wait(self):
        self.nc.sync.wait_ge(self.qs, self.qs_n)

    def st(self, out, in_):
        self.nc.scalar.dma_start(out=out, in_=in_).then_inc(self.qa, 16)
        self.qa_n += 16

    def st_wait(self):
        self.nc.scalar.wait_ge(self.qa, self.qa_n)

    def fv(self, off, cols, rows=128):
        return self.ar_f[0:rows, off:off + cols]

    def bv(self, off, cols):
        return self.ar_b[:, off:off + cols]

    def pcol(self, nm, l=0, tile=0, n=1):
        c = PCMAP[(nm, l)] + tile
        return self.prm.ap()[:, c:c + n]

    def scol(self, i, n=1):
        c = PC_SCR + i
        return self.prm.ap()[:, c:c + n]

    def prep_consts(self):
        nc = self.nc
        self.ld(self.ident_b.ap(), self.c_ident_b.ap())
        self.ld(self.ident_f.ap(), self.c_ident_f.ap())
        self.ld(self.prm.ap()[:, 0:PC_DATA], self.prm_in.ap())
        nc.vector.memset(self.ones_f.ap(), 1.0)
        nc.vector.memset(self.ones_r.ap(), 1.0)
        self.eps_c = self.scol(70)
        self.mhalf_c = self.scol(71)
        nc.vector.memset(self.eps_c, EPS)
        nc.vector.memset(self.mhalf_c, -0.5)
        self.ld_wait()
        self.bar()

    def dq_ld(self, k, q_src, am_src, fcols):
        b = k % 2
        qi = self.ar_f[:, b * 4096:b * 4096 + fcols].bitcast(I32)
        amax = self.fv(22528 + b * 64, fcols // 64)
        self.ld(qi, q_src)
        self.ld(amax, am_src)
        return self.qs_n

    def dequant_tile(self, k, fcols, qs_need):
        nc = self.nc
        b = k % 2
        qi = self.ar_f[:, b * 4096:b * 4096 + fcols].bitcast(I32)
        qf = self.fv(8192, fcols)
        rl = [self.fv(12288, fcols), self.fv(26624, fcols)]
        acc = self.fv(16384, fcols)
        nb = fcols // 64
        amax = self.fv(22528 + b * 64, nb)
        wb = self.bv(A_DQ_WB, fcols)
        ae0, de0 = self.ae_n, self.de_n
        nc.vector.wait_ge(self.qs, qs_need)
        nc.vector.tensor_copy(out=qf, in_=qi).then_inc(self.de_s, 1)
        self.de_n += 1
        nc.vector.tensor_scalar(out=acc, in0=qf, scalar1=NF_B, scalar2=NF_A,
                                op0=OP.mult, op1=OP.add)
        nc.scalar.wait_ge(self.de_s, de0 + 1)
        for t in range(1, 15):
            rb = t % 2
            if t > 2:
                nc.scalar.wait_ge(self.de_s, de0 + 1 + (t - 2))
            nc.scalar.activation(out=rl[rb], in_=qf, func=AF.Relu,
                                 bias=self.pcol("negt", 0, t - 1),
                                 scale=1.0).then_inc(self.ae_s, 1)
            self.ae_n += 1
            nc.vector.wait_ge(self.ae_s, ae0 + t)
            op_ = nc.vector.scalar_tensor_tensor(
                out=acc, in0=rl[rb], scalar=NF_G[t - 1], in1=acc,
                op0=OP.mult, op1=OP.add)
            op_.then_inc(self.de_s, 1)
            self.de_n += 1
        amv = amax.unsqueeze(2).broadcast_to([128, nb, 64])
        accv = acc.rearrange("p (nb b) -> p nb b", b=64)
        wv = wb.rearrange("p (nb b) -> p nb b", b=64)
        nc.vector.tensor_tensor(out=wv, in0=accv, in1=amv, op=OP.mult)
        self.bar()
        return wb

    def transpose_pipe(self, srcs, writer):
        nc = self.nc
        banks = [self.ps_t.ap(), self.ps_d[2].ap().bitcast(BF16)]
        pt0, ev0 = self.pt_n, self.ev_n
        ngroups = (len(srcs) + 7) // 8
        for g in range(ngroups):
            g0 = g * 8
            kn = min(8, len(srcs) - g0)
            bk = banks[g % 2]
            if g >= 2:
                nc.tensor.wait_ge(self.ev_s, ev0 + g - 1)
            for i in range(kn):
                mm = nc.tensor.transpose(bk[:, i * 128:(i + 1) * 128],
                                         srcs[g0 + i], self.ident_b.ap())
                if i == kn - 1:
                    mm.then_inc(self.pt_s, 1)
            self.pt_n += 1
            nc.vector.wait_ge(self.pt_s, pt0 + g + 1)
            last = writer(g0, kn, bk)
            last.then_inc(self.ev_s, 1)
            self.ev_n += 1

    def transpose_wait(self):
        self.nc.scalar.wait_ge(self.ev_s, self.ev_n)
        self.bar()

    def transpose_to_strip(self, src_b, strip, nk):
        srcs = [src_b[:, k * 128:(k + 1) * 128] for k in range(nk)]

        def wr(g0, kn, bk):
            return self.nc.vector.tensor_copy(
                out=strip[:, g0 * 128:(g0 + kn) * 128], in_=bk[:, 0:kn * 128])
        self.transpose_pipe(srcs, wr)
        self.transpose_wait()

    def prep_ffn_weights(self):
        nc = self.nc
        strip = self.bv(A_DQ_ST, 4096)
        tiles = []
        for (qsrc, asrc, own) in [(self.gq, self.ga, self.wg_own),
                                  (self.uq, self.ua, self.wu_own)]:
            ownv = own.ap().rearrange("(s p) f -> p s f", p=128)
            for mc in range(MO // 128):
                tiles.append((qsrc.ap()[mc * 128:(mc + 1) * 128, :],
                              asrc.ap()[mc * 128:(mc + 1) * 128, :],
                              D, ownv[:, mc, :], 32))
        wdv = self.wd_own.ap().rearrange("(s p) f -> p s f", p=128)
        for hc in range(4):
            for ms in range(NSUB):
                tiles.append((
                    self.dq.ap()[hc * 128:(hc + 1) * 128,
                                 ms * MSUB:(ms + 1) * MSUB],
                    self.da.ap()[hc * 128:(hc + 1) * 128,
                                 ms * (MSUB // 64):(ms + 1) * (MSUB // 64)],
                    MSUB, wdv[:, ms * 4 + hc, :], 28))
        ags = {13: 0, 27: 1, 43: 2}
        ag_args = [(self.wg_own, self.wg_gath, self.sc_w[0]),
                   (self.wu_own, self.wu_gath, self.sc_w[1]),
                   (self.wd_own, self.wd_gath, self.sc_w[2])]
        qs_need = self.dq_ld(0, *tiles[0][:3])
        for k, (qsrc, asrc, fcols, dst, nk) in enumerate(tiles):
            if k + 1 < len(tiles):
                nxt = self.dq_ld(k + 1, *tiles[k + 1][:3])
            wb = self.dequant_tile(k, fcols, qs_need)
            qs_need = nxt if k + 1 < len(tiles) else qs_need
            self.transpose_to_strip(wb, strip, nk)
            self.st(dst, strip[:, 0:fcols])
            self.st_wait()
            self.bar()
            if k in ags:
                own, gath, sem = ag_args[ags[k]]
                nc.gpsimd.wait_ge(self.qa, self.qa_n)
                nc.gpsimd.collective_compute(
                    "AllGather", OP.bypass,
                    replica_groups=[list(range(NCORES))],
                    ins=[own.ap().opt()], outs=[gath.ap().opt()],
                ).then_inc(sem)

    def prep_attn_weights(self):
        nc = self.nc
        src16 = self.fv(0, 16384)
        cast16 = self.bv(32768, 16384)
        strip8 = self.bv(49152, 8192)
        wstr = [self.bv(57344 + i * 4096, 4096) for i in range(4)]

        def group_transpose(cast_v, pairs, dst_writer):
            srcs = [cast_v[:, c:c + 128] for c in pairs]
            self.transpose_pipe(srcs, dst_writer)
            self.transpose_wait()

        self.ld(src16.rearrange("p (dc v) -> p dc v", dc=4),
                self.win.ap().rearrange("(dc p) v -> p dc v", p=128))
        self.ld_wait()
        self.bar()
        nc.vector.tensor_copy(out=cast16, in_=src16)
        self.bar()
        winTv = self.winT.ap().rearrange("(s p) f -> p s f", p=128)
        for dc in range(DC):
            pairs = [dc * 4096 + vc * 128 for vc in range(VC)]

            def wr(g0, kn, bk, dc=dc):
                return nc.vector.tensor_copy(
                    out=wstr[dc][:, g0 * 128:(g0 + kn) * 128],
                    in_=bk[:, 0:kn * 128])
            group_transpose(cast16, pairs, wr)
        for dc in range(DC):
            self.st(winTv[:, dc, :], wstr[dc])
        self.st_wait()
        self.bar()
        w1Tv = self.w1T.ap().rearrange("(s p) f -> p s f", p=128)
        w2Tv = self.w2T.ap().rearrange("(s p) f -> p s f", p=128)
        for l in range(NL):
            self.ld(src16[:, 0:8192].rearrange("p (fc d) -> p fc d", fc=FC),
                    self.w1.ap()[l].rearrange("(fc p) d -> p fc d", p=128))
            self.ld_wait()
            self.bar()
            nc.vector.tensor_copy(out=cast16[:, 0:8192], in_=src16[:, 0:8192])
            self.bar()
            pairs = [fc * 512 + dc * 128 for fc in range(FC) for dc in range(DC)]

            def wr1(g0, kn, bk):
                return nc.vector.tensor_copy(
                    out=strip8[:, g0 * 128:(g0 + kn) * 128],
                    in_=bk[:, 0:kn * 128])
            group_transpose(cast16, pairs, wr1)
            self.st(w1Tv[:, l, :], strip8)
            self.st_wait()
            self.bar()
            self.ld(src16[:, 0:8192].rearrange("p (dc f) -> p dc f", dc=DC),
                    self.w2.ap()[l].rearrange("(dc p) f -> p dc f", p=128))
            self.ld_wait()
            self.bar()
            nc.vector.tensor_copy(out=cast16[:, 0:8192], in_=src16[:, 0:8192])
            self.bar()
            pairs = [dc * 2048 + fc * 128 for dc in range(DC) for fc in range(FC)]
            group_transpose(cast16, pairs, wr1)
            self.st(w2Tv[:, l, :], strip8)
            self.st_wait()
            self.bar()
        self.ld(src16.rearrange("p (vc d) -> p vc d", vc=VC),
                self.wout.ap().rearrange("(vc p) d -> p vc d", p=128))
        self.ld_wait()
        self.bar()
        nc.vector.tensor_copy(out=cast16, in_=src16)
        self.bar()
        woutTv = self.woutT.ap().rearrange("(s p) f -> p s f", p=128)
        pairs = [vc * 512 + dc * 128 for vc in range(VC) for dc in range(DC)]

        def wrout(g0, kn, bk):
            vc0 = g0 // 4
            last = None
            for dc in range(DC):
                sv = bk.rearrange("p (v d f) -> p v d f",
                                  v=2, d=4)[:, 0:kn // 4, dc, :]
                dv = wstr[dc][:, vc0 * 128:(vc0 + kn // 4) * 128]
                last = nc.vector.tensor_copy(out=dv, in_=sv)
            return last
        group_transpose(cast16, pairs, wrout)
        for dc in range(DC):
            self.st(woutTv[:, dc, :], wstr[dc])
        self.st_wait()
        self.bar()

    def ln_big_half(self, src, th, xfm):
        nc = self.nc
        xt = [self.fv(F_LNX + tt * 4096, 4096) for tt in range(4)]
        sq = self.fv(32768, 4096)
        xln = self.bv(73728, 4096)
        ssum = self.scol(0, 4)
        nmu = self.scol(4, 4)
        msq = self.scol(8, 4)
        rstd = self.scol(12, 4)
        for tt in range(4):
            r0 = th * TH + tt * 128
            self.ld(xt[tt], src.ap()[r0:r0 + 128, :])
        self.ld_wait()
        self.bar()
        for tt in range(4):
            nc.vector.tensor_reduce(out=ssum[:, tt:tt + 1], in_=xt[tt],
                                    axis=AXX, op=OP.add)
            nc.scalar.activation(out=sq, in_=xt[tt], func=AF.Square)
            self.bar()
            nc.vector.tensor_reduce(out=msq[:, tt:tt + 1], in_=sq,
                                    axis=AXX, op=OP.add)
            self.bar()
        nc.vector.tensor_scalar(out=nmu, in0=ssum, scalar1=-1.0 / D,
                                scalar2=None, op0=OP.mult)
        nc.vector.tensor_scalar(out=msq, in0=msq, scalar1=1.0 / D,
                                scalar2=None, op0=OP.mult)
        self.bar()
        nc.scalar.activation(out=rstd, in_=nmu, func=AF.Square)
        self.bar()
        nc.vector.tensor_tensor(out=msq, in0=msq, in1=rstd, op=OP.subtract)
        self.bar()
        nc.scalar.activation(out=msq, in_=msq, func=AF.Ln,
                             bias=self.eps_c, scale=1.0)
        self.bar()
        nc.scalar.activation(out=rstd, in_=msq, func=AF.Exp,
                             scale=self.mhalf_c)
        self.bar()
        for tt in range(4):
            nc.vector.tensor_scalar(out=xln, in0=xt[tt],
                                    scalar1=nmu[:, tt:tt + 1],
                                    scalar2=rstd[:, tt:tt + 1],
                                    op0=OP.add, op1=OP.mult)
            self.bar()
            srcs = [xln[:, k * 128:(k + 1) * 128] for k in range(32)]

            def wrln(g0, kn, bk, tt=tt):
                dstv = xfm.rearrange("p (v t) -> p v t", t=512)[
                    :, g0:g0 + kn, tt * 128:(tt + 1) * 128]
                srcv = bk.rearrange("p (v t) -> p v t", t=128)[:, 0:kn, :]
                return nc.vector.tensor_copy(out=dstv, in_=srcv)
            self.transpose_pipe(srcs, wrln)
            self.transpose_wait()

    def ffn(self, src, dst):
        nc = self.nc
        nc.sync.wait_ge(self.sc_w[0], 1)
        nc.sync.wait_ge(self.sc_w[1], 1)
        self.bar()
        xfm = self.bv(A_XFM, 16384)
        gbuf = [self.bv(A_GB + i * 4096, 4096) for i in range(2)]
        ubuf = [self.bv(A_UB + i * 4096, 4096) for i in range(2)]
        hact = self.bv(A_HACT, MC * 512)
        dbuf = [self.bv(0, 4 * MSUB), self.bv(16384, 4 * MSUB)]
        hsil = [self.bv(A_HSIL + i * 512, 512) for i in range(2)]
        wgv = self.wg_gath.ap().rearrange("(s p) f -> p s f", p=128)
        wuv = self.wu_gath.ap().rearrange("(s p) f -> p s f", p=128)
        wdv = self.wd_gath.ap().rearrange("(s p) f -> p s f", p=128)
        xs = self.fv(45056, 2048)
        ot = self.fv(47104, 2048)
        ps_g = [self.ps_a, self.ps_d[0]]
        ps_u = [self.ps_b, self.ps_d[1]]
        for th in range(2):
            self.ln_big_half(src, th, xfm)
            pe0, ae0, de0 = self.pe_n, self.ae_n, self.de_n
            qs_after = {}
            for mg in range(MC):
                b = mg % 2
                if mg >= 2:
                    nc.sync.wait_ge(self.pe_s, pe0 + 2 * (mg - 2) + 2)
                self.ld(gbuf[b], wgv[:, mg, :])
                self.ld(ubuf[b], wuv[:, mg, :])
                qs_after[mg] = self.qs_n
                nc.tensor.wait_ge(self.qs, qs_after[mg])
                if mg >= 2:
                    nc.tensor.wait_ge(self.ae_s, ae0 + (mg - 2) + 1)
                    nc.tensor.wait_ge(self.de_s, de0 + (mg - 2) + 1)
                for k in range(VC):
                    mm = nc.tensor.matmul(ps_g[b].ap(),
                                          gbuf[b][:, k * 128:(k + 1) * 128],
                                          xfm[:, k * 512:(k + 1) * 512],
                                          start=(k == 0), stop=(k == VC - 1))
                    if k == VC - 1:
                        mm.then_inc(self.pe_s, 1)
                self.pe_n += 1
                for k in range(VC):
                    mm = nc.tensor.matmul(ps_u[b].ap(),
                                          ubuf[b][:, k * 128:(k + 1) * 128],
                                          xfm[:, k * 512:(k + 1) * 512],
                                          start=(k == 0), stop=(k == VC - 1))
                    if k == VC - 1:
                        mm.then_inc(self.pe_s, 1)
                self.pe_n += 1
                nc.scalar.wait_ge(self.pe_s, pe0 + 2 * mg + 1)
                if mg >= 2:
                    nc.scalar.wait_ge(self.de_s, de0 + (mg - 2) + 1)
                nc.scalar.activation(out=hsil[b], in_=ps_g[b].ap(),
                                     func=AF.Silu).then_inc(self.ae_s, 1)
                self.ae_n += 1
                nc.vector.wait_ge(self.pe_s, pe0 + 2 * mg + 2)
                nc.vector.wait_ge(self.ae_s, ae0 + mg + 1)
                nc.vector.tensor_tensor(
                    out=hact[:, mg * 512:(mg + 1) * 512], in0=hsil[b],
                    in1=ps_u[b].ap(), op=OP.mult).then_inc(self.de_s, 1)
                self.de_n += 1
            nc.sync.wait_ge(self.de_s, de0 + MC)
            if th == 0:
                nc.sync.wait_ge(self.sc_w[2], 1)
            self.bar()
            pe0, de0 = self.pe_n, self.de_n
            qs_xs = {}
            for hb in range(NCORES):
                for ms in range(NSUB):
                    j = hb * NSUB + ms
                    b = j % 2
                    if j >= 2:
                        nc.sync.wait_ge(self.pe_s, pe0 + j - 1)
                    blk0 = hb * 16 + ms * 4
                    self.ld(dbuf[b].rearrange("p (r f) -> p r f", r=4),
                            wdv[:, blk0:blk0 + 4, :])
                    qs_after[j] = self.qs_n
                    if ms == NSUB - 1:
                        if hb > 0:
                            nc.sync.wait_ge(self.de_s, de0 + hb)
                        for tc in range(4):
                            r0 = th * TH + tc * 128
                            self.ld(xs[:, tc * 512:(tc + 1) * 512],
                                    src.ap()[r0:r0 + 128,
                                             hb * 512:(hb + 1) * 512])
                        qs_xs[hb] = self.qs_n
                    nc.tensor.wait_ge(self.qs, qs_after[j])
                    if ms == 0 and hb > 0:
                        nc.tensor.wait_ge(self.de_s, de0 + hb)
                    drv = dbuf[b].rearrange("p (r f) -> p r f", r=4)
                    for k in range(28):
                        mg = ms * 28 + k
                        for tc in range(4):
                            mm = nc.tensor.matmul(
                                self.ps_d[tc].ap(),
                                hact[:, mg * 512 + tc * 128:
                                     mg * 512 + (tc + 1) * 128],
                                drv[:, :, k * 128:(k + 1) * 128],
                                start=(ms == 0 and k == 0),
                                stop=(ms == NSUB - 1 and k == 27))
                            if k == 27 and tc == 3:
                                mm.then_inc(self.pe_s, 1)
                    self.pe_n += 1
                nc.vector.wait_ge(self.pe_s, pe0 + NSUB * (hb + 1))
                nc.vector.wait_ge(self.qs, qs_xs[hb])
                if hb > 0:
                    nc.vector.wait_ge(self.qa, self.qa_n)
                for tc in range(4):
                    op_ = nc.vector.scalar_tensor_tensor(
                        out=ot[:, tc * 512:(tc + 1) * 512],
                        in0=self.ps_d[tc].ap(), scalar=0.5,
                        in1=xs[:, tc * 512:(tc + 1) * 512],
                        op0=OP.mult, op1=OP.add)
                    if tc == 3:
                        op_.then_inc(self.de_s, 1)
                self.de_n += 1
                nc.scalar.wait_ge(self.de_s, de0 + hb + 1)
                for tc in range(4):
                    r0 = th * TH + tc * 128
                    self.st(dst.ap()[r0:r0 + 128, hb * 512:(hb + 1) * 512],
                            ot[:, tc * 512:(tc + 1) * 512])
            self.st_wait()
            self.bar()

    def ln_inner(self, h_tiles, w_nm, b_nm, l, out_tiles, silu):
        nc = self.nc
        sq = self.fv(F_MTMP, 1024)
        for half in range(2):
            tgt = self.ps_d[half].ap()[0:1, :]
            for ct in range(4):
                nc.tensor.matmul(tgt, self.ones_f.ap(),
                                 h_tiles[ct][:, half * 512:(half + 1) * 512],
                                 start=(ct == 0), stop=(ct == 3))
        self.bar()
        for ct in range(4):
            nc.scalar.activation(out=sq, in_=h_tiles[ct], func=AF.Square)
            self.bar()
            for half in range(2):
                tgt = self.ps_d[2 + half].ap()[0:1, :]
                nc.tensor.matmul(tgt, self.ones_f.ap(),
                                 sq[:, half * 512:(half + 1) * 512],
                                 start=(ct == 0), stop=(ct == 3))
            self.bar()
        st_ = self.fv(F_STAT, 2048, rows=1)
        for half in range(2):
            nc.vector.tensor_copy(out=st_[:, half * 512:(half + 1) * 512],
                                  in_=self.ps_d[half].ap()[0:1, :])
            nc.vector.tensor_copy(out=st_[:, 1024 + half * 512:1024 + (half + 1) * 512],
                                  in_=self.ps_d[2 + half].ap()[0:1, :])
        nc.vector.tensor_scalar(out=st_[:, 0:1024], in0=st_[:, 0:1024],
                                scalar1=-1.0 / DA, scalar2=None, op0=OP.mult)
        nc.vector.tensor_scalar(out=st_[:, 1024:2048], in0=st_[:, 1024:2048],
                                scalar1=1.0 / DA, scalar2=None, op0=OP.mult)
        self.bar()
        musq = self.fv(F_MTMP, 1024, rows=1)
        nc.scalar.activation(out=musq, in_=st_[:, 0:1024], func=AF.Square)
        self.bar()
        nc.vector.tensor_tensor(out=st_[:, 1024:2048], in0=st_[:, 1024:2048],
                                in1=musq, op=OP.subtract)
        self.bar()
        nc.scalar.activation(out=st_[:, 1024:2048], in_=st_[:, 1024:2048],
                             func=AF.Ln, bias=self.eps_c[0:1, :], scale=1.0)
        self.bar()
        nc.scalar.activation(out=st_[:, 1024:2048], in_=st_[:, 1024:2048],
                             func=AF.Exp, scale=self.mhalf_c[0:1, :])
        self.bar()
        for half in range(2):
            nc.tensor.matmul(self.ps_d[half].ap(), self.ones_r.ap(),
                             st_[:, half * 512:(half + 1) * 512],
                             start=True, stop=True)
            nc.tensor.matmul(self.ps_d[2 + half].ap(), self.ones_r.ap(),
                             st_[:, 1024 + half * 512:1024 + (half + 1) * 512],
                             start=True, stop=True)
        self.bar()
        tmp = self.fv(F_MTMP, 1024)
        for ct in range(4):
            for half in range(2):
                s0 = half * 512
                nc.vector.tensor_tensor(out=tmp[:, s0:s0 + 512],
                                        in0=h_tiles[ct][:, s0:s0 + 512],
                                        in1=self.ps_d[half].ap(), op=OP.add)
                nc.vector.tensor_tensor(out=tmp[:, s0:s0 + 512],
                                        in0=tmp[:, s0:s0 + 512],
                                        in1=self.ps_d[2 + half].ap(), op=OP.mult)
            wc = self.pcol(w_nm, l, ct)
            bc = self.pcol(b_nm, l, ct)
            if silu:
                nc.vector.tensor_scalar(out=tmp, in0=tmp, scalar1=wc, scalar2=bc,
                                        op0=OP.mult, op1=OP.add)
                self.bar()
                nc.scalar.activation(out=out_tiles[ct], in_=tmp, func=AF.Silu)
            else:
                nc.vector.tensor_scalar(out=out_tiles[ct], in0=tmp, scalar1=wc,
                                        scalar2=bc, op0=OP.mult, op1=OP.add)
            self.bar()

    def sconv(self, l, h_tiles, y_tiles):
        nc = self.nc
        u_t = [self.fv(F_U + ct * 1024, 1024) for ct in range(4)]
        v_t = [self.fv(F_V + ct * 1024, 1024) for ct in range(4)]
        C_t = [self.fv(F_C + ct * 1024, 1024) for ct in range(4)]
        S_t = [self.fv(F_S + ct * 1024, 1024) for ct in range(4)]
        P_t = [self.fv(F_P + ct * 1056, 1056) for ct in range(4)]
        rho_t = self.fv(F_RHO, 1024)
        tmod = self.fv(F_TMD, 1024)
        CTH = self.scol(0, 4)
        STH = self.scol(4, 4)
        RH = self.scol(8, 4)
        T0 = self.scol(12, 4)
        T1 = self.scol(16, 4)
        T2 = self.scol(20, 4)
        stage = self.scol(24, 8)
        R1C = self.scol(32, 4)
        R1S = self.scol(36, 4)
        BC = self.scol(40, 4)
        BS = self.scol(44, 4)
        UI0 = self.scol(48, 4)
        VI0 = self.scol(52, 4)
        UIN = self.scol(56, 4)
        VIN = self.scol(60, 4)
        pr4 = self.pcol("phz_re", l, 0, 4)
        pi4 = self.pcol("phz_im", l, 0, 4)
        lcr4 = self.pcol("lc_re", l, 0, 4)
        lci4 = self.pcol("lc_im", l, 0, 4)
        cmask = self.pcol("cmask", 0, 0, 8)
        nc.scalar.activation(out=T0, in_=pr4, func=AF.Square)
        nc.scalar.activation(out=T1, in_=pi4, func=AF.Square)
        self.bar()
        nc.vector.tensor_tensor(out=T0, in0=T0, in1=T1, op=OP.add)
        self.bar()
        nc.scalar.activation(out=T1, in_=T0, func=AF.Sqrt)
        self.bar()
        nc.vector.reciprocal(out=T2, in_=T1)
        self.bar()
        nc.vector.tensor_tensor(out=T2, in0=T0, in1=T2, op=OP.mult)
        self.bar()
        nc.vector.tensor_tensor(out=T1, in0=T1, in1=T2, op=OP.add)
        self.bar()
        nc.vector.tensor_scalar(out=T1, in0=T1, scalar1=0.5, scalar2=None,
                                op0=OP.mult)
        self.bar()
        nc.vector.reciprocal(out=T2, in_=T1)
        nc.scalar.activation(out=RH, in_=T1, func=AF.Tanh)
        self.bar()
        nc.vector.tensor_tensor(out=CTH, in0=pr4, in1=T2, op=OP.mult)
        nc.vector.tensor_tensor(out=STH, in0=pi4, in1=T2, op=OP.mult)
        self.bar()
        nc.vector.tensor_tensor(out=T0, in0=STH, in1=lci4, op=OP.mult)
        nc.vector.tensor_tensor(out=T1, in0=CTH, in1=lci4, op=OP.mult)
        self.bar()
        nc.vector.tensor_tensor(out=UI0, in0=CTH, in1=lcr4, op=OP.mult)
        nc.vector.tensor_tensor(out=VI0, in0=STH, in1=lcr4, op=OP.mult)
        self.bar()
        nc.vector.tensor_tensor(out=UI0, in0=UI0, in1=T0, op=OP.subtract)
        nc.vector.tensor_tensor(out=VI0, in0=VI0, in1=T1, op=OP.add)
        self.bar()
        nc.vector.tensor_scalar(out=VI0, in0=VI0, scalar1=-1.0, scalar2=None,
                                op0=OP.mult)
        for ct in range(4):
            Ct, St, Pt = C_t[ct], S_t[ct], P_t[ct]
            nc.vector.memset(Ct[:, 0:1], 1.0)
            nc.vector.memset(St[:, 0:1], 0.0)
            nc.vector.memset(Pt[:, 0:1], 1.0)
        self.bar()
        for ct in range(4):
            Ct, St, Pt = C_t[ct], S_t[ct], P_t[ct]
            nc.vector.tensor_copy(out=Ct[:, 1:2], in_=CTH[:, ct:ct + 1])
            nc.vector.tensor_copy(out=St[:, 1:2], in_=STH[:, ct:ct + 1])
            nc.vector.tensor_copy(out=Pt[:, 1:2], in_=RH[:, ct:ct + 1])
        self.bar()
        C4 = self.fv(F_C, 4096).rearrange("p (t j) -> p t j", t=4)
        S4 = self.fv(F_S, 4096).rearrange("p (t j) -> p t j", t=4)
        P4 = self.fv(F_P, 4224).rearrange("p (t j) -> p t j", t=4)
        t4a = self.fv(F_TMD, 4096).rearrange("p (t j) -> p t j", t=4)

        def bcol(V, j, n):
            return V[:, :, j:j + 1].broadcast_to([128, 4, n])
        f = 2
        while f < 1056:
            m = min(f - 1, 1056 - f)
            mcs = min(f - 1, 1024 - f) if f < 1024 else 0
            if mcs > 0:
                nc.vector.tensor_tensor(out=t4a[:, :, 0:mcs],
                                        in0=C4[:, :, 1:1 + mcs],
                                        in1=bcol(S4, f - 1, mcs), op=OP.mult)
                nc.vector.tensor_tensor(out=S4[:, :, f:f + mcs],
                                        in0=S4[:, :, 1:1 + mcs],
                                        in1=bcol(C4, f - 1, mcs), op=OP.mult)
                nc.vector.tensor_tensor(out=t4a[:, :, 512:512 + mcs],
                                        in0=S4[:, :, 1:1 + mcs],
                                        in1=bcol(S4, f - 1, mcs), op=OP.mult)
                nc.vector.tensor_tensor(out=C4[:, :, f:f + mcs],
                                        in0=C4[:, :, 1:1 + mcs],
                                        in1=bcol(C4, f - 1, mcs), op=OP.mult)
                self.bar()
                nc.vector.tensor_tensor(out=S4[:, :, f:f + mcs],
                                        in0=S4[:, :, f:f + mcs],
                                        in1=t4a[:, :, 0:mcs], op=OP.add)
                nc.vector.tensor_tensor(out=C4[:, :, f:f + mcs],
                                        in0=C4[:, :, f:f + mcs],
                                        in1=t4a[:, :, 512:512 + mcs],
                                        op=OP.subtract)
            nc.vector.tensor_tensor(out=P4[:, :, f:f + m],
                                    in0=P4[:, :, 1:1 + m],
                                    in1=bcol(P4, f - 1, m), op=OP.mult)
            self.bar()
            f += m
        for ct in range(4):
            Ct, St = C_t[ct], S_t[ct]
            nc.vector.tensor_scalar_mul(out=T0[:, ct:ct + 1],
                                        in0=St[:, 1023:1024],
                                        scalar1=STH[:, ct:ct + 1])
            nc.vector.tensor_scalar_mul(out=T1[:, ct:ct + 1],
                                        in0=Ct[:, 1023:1024],
                                        scalar1=STH[:, ct:ct + 1])
        self.bar()
        for ct in range(4):
            Ct, St = C_t[ct], S_t[ct]
            nc.vector.scalar_tensor_tensor(out=R1C[:, ct:ct + 1],
                                           in0=Ct[:, 1023:1024],
                                           scalar=CTH[:, ct:ct + 1],
                                           in1=T0[:, ct:ct + 1],
                                           op0=OP.mult, op1=OP.subtract)
            nc.vector.scalar_tensor_tensor(out=R1S[:, ct:ct + 1],
                                           in0=St[:, 1023:1024],
                                           scalar=CTH[:, ct:ct + 1],
                                           in1=T1[:, ct:ct + 1],
                                           op0=OP.mult, op1=OP.add)
        self.bar()
        pw_c = [self.fv(F_TMD + ct * 16, 8) for ct in range(4)]
        pw_s = [self.fv(F_TMD + ct * 16 + 8, 8) for ct in range(4)]
        for ct in range(4):
            nc.vector.memset(pw_c[ct][:, 0:1], 1.0)
            nc.vector.memset(pw_s[ct][:, 0:1], 0.0)
        self.bar()
        for k in range(1, 8):
            for ct in range(4):
                nc.vector.tensor_scalar_mul(out=T0[:, ct:ct + 1],
                                            in0=pw_s[ct][:, k - 1:k],
                                            scalar1=R1S[:, ct:ct + 1])
                nc.vector.tensor_scalar_mul(out=T1[:, ct:ct + 1],
                                            in0=pw_c[ct][:, k - 1:k],
                                            scalar1=R1S[:, ct:ct + 1])
            self.bar()
            for ct in range(4):
                nc.vector.scalar_tensor_tensor(out=pw_c[ct][:, k:k + 1],
                                               in0=pw_c[ct][:, k - 1:k],
                                               scalar=R1C[:, ct:ct + 1],
                                               in1=T0[:, ct:ct + 1],
                                               op0=OP.mult, op1=OP.subtract)
                nc.vector.scalar_tensor_tensor(out=pw_s[ct][:, k:k + 1],
                                               in0=pw_s[ct][:, k - 1:k],
                                               scalar=R1C[:, ct:ct + 1],
                                               in1=T1[:, ct:ct + 1],
                                               op0=OP.mult, op1=OP.add)
            self.bar()
        cmask8 = self.pcol("cmask8", 0, 0, 64)
        pw_all = self.fv(F_TMD, 64)
        sel64 = self.fv(F_TMD + 64, 64)
        bcs8 = self.fv(F_TMD + 128, 8)
        nc.vector.tensor_tensor(out=sel64, in0=pw_all, in1=cmask8, op=OP.mult)
        self.bar()
        nc.vector.tensor_reduce(
            out=bcs8.unsqueeze(2),
            in_=sel64.rearrange("p (g k) -> p g k", k=8), axis=AXX, op=OP.add)
        self.bar()
        for ct in range(4):
            nc.vector.tensor_copy(out=BC[:, ct:ct + 1],
                                  in_=bcs8[:, 2 * ct:2 * ct + 1])
            nc.vector.tensor_copy(out=BS[:, ct:ct + 1],
                                  in_=bcs8[:, 2 * ct + 1:2 * ct + 2])
        self.bar()
        de0 = self.de_n
        for ct in range(4):
            Ct, St = C_t[ct], S_t[ct]
            bcc = BC[:, ct:ct + 1]
            bsc = BS[:, ct:ct + 1]
            nc.vector.tensor_scalar_mul(out=tmod, in0=St[:, 0:1024],
                                        scalar1=bsc)
            nc.vector.tensor_scalar_mul(out=St[:, 0:1024], in0=St[:, 0:1024],
                                        scalar1=bcc)
            nc.vector.scalar_tensor_tensor(out=St[:, 0:1024],
                                           in0=Ct[:, 0:1024], scalar=bsc,
                                           in1=St[:, 0:1024],
                                           op0=OP.mult, op1=OP.add)
            nc.vector.tensor_scalar_mul(out=Ct[:, 0:1024], in0=Ct[:, 0:1024],
                                        scalar1=bcc)
            nc.vector.tensor_tensor(out=Ct[:, 0:1024], in0=Ct[:, 0:1024],
                                    in1=tmod, op=OP.subtract)
            nc.vector.memset(rho_t, 1.0)
            self.bar()
            nc.vector.tensor_scalar_mul(out=rho_t, in0=rho_t,
                                        scalar1=RH[:, ct:ct + 1])
            self.bar()
            nc.vector.tensor_tensor(out=tmod, in0=Ct[:, 0:1024],
                                    in1=y_tiles[ct], op=OP.mult)
            nc.vector.tensor_tensor_scan(out=u_t[ct], data0=rho_t,
                                         data1=tmod, initial=0.0,
                                         op0=OP.mult, op1=OP.add)
            nc.vector.tensor_tensor(out=tmod, in0=St[:, 0:1024],
                                    in1=y_tiles[ct], op=OP.mult)
            nc.vector.tensor_tensor_scan(out=v_t[ct], data0=rho_t,
                                         data1=tmod, initial=0.0,
                                         op0=OP.mult, op1=OP.add)
            nc.vector.tensor_copy(out=stage[:, 2 * ct:2 * ct + 1],
                                  in_=u_t[ct][:, 1023:1024])
            nc.vector.tensor_copy(out=stage[:, 2 * ct + 1:2 * ct + 2],
                                  in_=v_t[ct][:, 1023:1024])
        self.bar()
        nc.vector.engine_nop().then_inc(self.st_s, 1)
        self.st_n += 1
        nc.gpsimd.wait_ge(self.st_s, self.st_n)
        nc.gpsimd.dma_start(
            out=self.cc_in[l].ap().rearrange("a (p c) -> (a p) c", p=128),
            in_=stage).then_inc(self.qg, 16)
        self.qg_n += 16
        nc.gpsimd.wait_ge(self.qg, self.qg_n)
        nc.gpsimd.collective_compute(
            "AllGather", OP.bypass, replica_groups=[list(range(NCORES))],
            ins=[self.cc_in[l].ap().opt()], outs=[self.cc_out[l].ap().opt()],
        ).then_inc(self.sc_l[l])
        nc.sync.wait_ge(self.sc_l[l], 1)
        eraw = self.fv(F_TMD, 1024, rows=8)
        self.ld(eraw, self.cc_out[l].ap())
        self.ld_wait()
        self.bar()
        ends = self.fv(F_RHO, 64)
        erv = eraw.rearrange("r (p c) -> r p c", c=8)
        for ct in range(4):
            for uv in range(2):
                nc.tensor.transpose(self.ps_m.ap()[:, 8 * (2 * ct + uv):
                                                   8 * (2 * ct + uv) + 8],
                                    erv[:, :, 2 * ct + uv],
                                    self.ident_f.ap()[0:8, 0:8])
        self.bar()
        nc.vector.tensor_copy(out=ends, in_=self.ps_m.ap()[:, 0:64])
        self.bar()
        w8u = [self.fv(F_TMD + ct * 16, 8) for ct in range(4)]
        w8v = [self.fv(F_TMD + ct * 16 + 8, 8) for ct in range(4)]
        for ct in range(4):
            nc.vector.tensor_copy(out=w8u[ct][:, 0:1], in_=UI0[:, ct:ct + 1])
            nc.vector.tensor_copy(out=w8v[ct][:, 0:1], in_=VI0[:, ct:ct + 1])
        self.bar()
        for cx in range(1, 8):
            for ct in range(4):
                E = P_t[ct][:, 1024:1025]
                Ue = ends[:, (ct * 2) * 8:(ct * 2) * 8 + 8]
                Ve = ends[:, (ct * 2 + 1) * 8:(ct * 2 + 1) * 8 + 8]
                nc.vector.scalar_tensor_tensor(
                    out=w8u[ct][:, cx:cx + 1], in0=w8u[ct][:, cx - 1:cx],
                    scalar=E, in1=Ue[:, cx - 1:cx], op0=OP.mult, op1=OP.add)
                nc.vector.scalar_tensor_tensor(
                    out=w8v[ct][:, cx:cx + 1], in0=w8v[ct][:, cx - 1:cx],
                    scalar=E, in1=Ve[:, cx - 1:cx], op0=OP.mult, op1=OP.add)
            self.bar()
        w8all = self.fv(F_TMD, 64)
        selb64 = self.fv(F_TMD + 64, 64)
        uv8 = self.fv(F_TMD + 128, 8)
        nc.vector.tensor_tensor(out=selb64, in0=w8all, in1=cmask8, op=OP.mult)
        self.bar()
        nc.vector.tensor_reduce(
            out=uv8.unsqueeze(2),
            in_=selb64.rearrange("p (g k) -> p g k", k=8), axis=AXX, op=OP.add)
        self.bar()
        for ct in range(4):
            nc.vector.tensor_copy(out=UIN[:, ct:ct + 1],
                                  in_=uv8[:, 2 * ct:2 * ct + 1])
            nc.vector.tensor_copy(out=VIN[:, ct:ct + 1],
                                  in_=uv8[:, 2 * ct + 1:2 * ct + 2])
        self.bar()
        for ct in range(4):
            nc.vector.scalar_tensor_tensor(out=u_t[ct], in0=P_t[ct][:, 1:1025],
                                           scalar=UIN[:, ct:ct + 1],
                                           in1=u_t[ct], op0=OP.mult,
                                           op1=OP.add)
            nc.vector.scalar_tensor_tensor(out=v_t[ct], in0=P_t[ct][:, 1:1025],
                                           scalar=VIN[:, ct:ct + 1],
                                           in1=v_t[ct], op0=OP.mult,
                                           op1=OP.add)
            nc.vector.tensor_tensor(out=u_t[ct], in0=u_t[ct],
                                    in1=C_t[ct][:, 0:1024], op=OP.mult)
            nc.vector.tensor_tensor(out=v_t[ct], in0=v_t[ct],
                                    in1=S_t[ct][:, 0:1024], op=OP.mult)
            nc.vector.tensor_tensor(out=u_t[ct], in0=u_t[ct], in1=v_t[ct],
                                    op=OP.add)
            scl = self.pcol("sc_lin", l, ct)
            nc.vector.scalar_tensor_tensor(out=h_tiles[ct], in0=u_t[ct],
                                           scalar=scl, in1=h_tiles[ct],
                                           op0=OP.mult, op1=OP.add)
        self.bar()

    def attn(self, src, dst):
        nc = self.nc
        h_tiles = [self.fv(F_H + ct * 1024, 1024) for ct in range(4)]
        y_tiles = [self.fv(F_Y + ct * 1024, 1024) for ct in range(4)]
        xfm = self.bv(A_XFM2, 16384)
        wsin = self.bv(A_WIN, 4096)
        winTv = self.winT.ap().rearrange("(s p) f -> p s f", p=128)
        for th in range(2):
            self.ln_big_half(src, th, xfm)
            for dc in range(DC):
                self.ld(wsin, winTv[:, dc, :])
                self.ld_wait()
                self.bar()
                for k in range(VC):
                    nc.tensor.matmul(self.ps_a.ap(),
                                     wsin[:, k * 128:(k + 1) * 128],
                                     xfm[:, k * 512:(k + 1) * 512],
                                     start=(k == 0), stop=(k == VC - 1))
                self.bar()
                nc.vector.tensor_scalar(
                    out=h_tiles[dc][:, th * 512:(th + 1) * 512],
                    in0=self.ps_a.ap(), scalar1=self.pcol("bin", 0, dc),
                    scalar2=None, op0=OP.add)
                self.bar()
        self.dbg_h(0, h_tiles)
        u_mlp = self.bv(A_UMLP, 8192)
        y2 = self.bv(A_Y2, 4096)
        wstr = self.bv(A_WSTR, 8192)
        w1Tv = self.w1T.ap().rearrange("(s p) f -> p s f", p=128)
        w2Tv = self.w2T.ap().rearrange("(s p) f -> p s f", p=128)
        tmp = self.fv(F_MTMP, 512)
        for l in range(NL):
            self.ln_inner(h_tiles, "ln1_w", "ln1_b", l, y_tiles, True)
            if l == 0:
                self.dbg_h(1, y_tiles)
            self.sconv(l, h_tiles, y_tiles)
            if l == 0:
                self.dbg_h(2, h_tiles)
            y2t = [y2[:, ct * 1024:(ct + 1) * 1024] for ct in range(4)]
            self.ln_inner(h_tiles, "ln2_w", "ln2_b", l, y2t, False)

            self.ld(wstr, w1Tv[:, l, :])
            self.ld_wait()
            self.bar()
            for th in range(2):
                for fc in range(FC):
                    for dc in range(DC):
                        nc.tensor.matmul(
                            self.ps_a.ap(),
                            wstr[:, (fc * 4 + dc) * 128:(fc * 4 + dc + 1) * 128],
                            y2t[dc][:, th * 512:(th + 1) * 512],
                            start=(dc == 0), stop=(dc == DC - 1))
                    self.bar()
                    nc.scalar.activation(
                        out=u_mlp[:, fc * 512:(fc + 1) * 512],
                        in_=self.ps_a.ap(), func=AF.Silu,
                        bias=self.pcol("b1", l, fc), scale=1.0)
                    self.bar()
                if th == 0:
                    u_sav = self.bv(A_UMLP + 8192, 8192)
                    nc.vector.tensor_copy(out=u_sav, in_=u_mlp)
                    self.bar()
            self.ld(wstr, w2Tv[:, l, :])
            self.ld_wait()
            self.bar()
            for th in range(2):
                usrc = self.bv(A_UMLP + 8192, 8192) if th == 0 else u_mlp
                for dc in range(DC):
                    for fc in range(FC):
                        nc.tensor.matmul(
                            self.ps_b.ap(),
                            wstr[:, (dc * 16 + fc) * 128:(dc * 16 + fc + 1) * 128],
                            usrc[:, fc * 512:(fc + 1) * 512],
                            start=(fc == 0), stop=(fc == FC - 1))
                    self.bar()
                    nc.vector.tensor_scalar(out=tmp, in0=self.ps_b.ap(),
                                            scalar1=self.pcol("b2", l, dc),
                                            scalar2=None, op0=OP.add)
                    self.bar()
                    nc.vector.tensor_tensor(
                        out=h_tiles[dc][:, th * 512:(th + 1) * 512],
                        in0=h_tiles[dc][:, th * 512:(th + 1) * 512],
                        in1=tmp, op=OP.add)
                    self.bar()

            self.dbg_h(3 + l, h_tiles)
        hln = self.bv(A_HLN, 4096)
        hlnt = [hln[:, ct * 1024:(ct + 1) * 1024] for ct in range(4)]
        self.ln_inner(h_tiles, "lnl_w", "lnl_b", 0, hlnt, False)
        self.dbg_h(7, hlnt)
        wo = [self.bv(A_WO + dc * 4096, 4096) for dc in range(DC)]
        woutTv = self.woutT.ap().rearrange("(s p) f -> p s f", p=128)
        for dc in range(DC):
            self.ld(wo[dc], woutTv[:, dc, :])
        self.ld_wait()
        self.bar()
        boutt = self.fv(F_BOUT, 512)
        x1s = self.fv(F_X1S, 512)
        wtmp = self.fv(F_WTMP, 512)
        for vb in range(8):
            self.ld(boutt, self.bout_rep.ap()[:, vb * 512:(vb + 1) * 512])
            self.ld_wait()
            self.bar()
            for tcg in range(8):
                for dc in range(DC):
                    nc.tensor.matmul(
                        self.ps_d[0].ap(),
                        hln[:, dc * 1024 + tcg * 128:dc * 1024 + (tcg + 1) * 128],
                        wo[dc][:, vb * 512:(vb + 1) * 512],
                        start=(dc == 0), stop=(dc == DC - 1))
                self.bar()
                self.ld(x1s, src.ap()[tcg * 128:(tcg + 1) * 128,
                                      vb * 512:(vb + 1) * 512])
                self.ld_wait()
                self.bar()
                nc.vector.tensor_tensor(out=wtmp, in0=self.ps_d[0].ap(),
                                        in1=boutt, op=OP.add)
                self.bar()
                nc.vector.tensor_tensor(out=wtmp, in0=wtmp, in1=x1s, op=OP.add)
                self.bar()
                self.st(dst.ap()[tcg * 128:(tcg + 1) * 128,
                                 vb * 512:(vb + 1) * 512], wtmp)
                self.st_wait()
                self.bar()

    def dbg_h(self, stage, tiles):
        if not DEBUG:
            return
        self.allbar()
        for ct in range(4):
            dst = self.dbgh.ap()[stage, ct * 128:(ct + 1) * 128, :]
            if tiles[ct].dtype == BF16:
                self.nc.gpsimd.dma_start(out=dst, in_=tiles[ct]).then_inc(self.qg, 16)
                self.qg_n += 16
            else:
                self.st(dst, tiles[ct])
        self.st_wait()
        self.nc.gpsimd.wait_ge(self.qg, self.qg_n)
        self.allbar()

    def dbg_copy(self):
        if not DEBUG:
            return
        buf = self.fv(0, 4096)
        for (srct, dstt) in [(self.x1b, self.dbg1), (self.x2b, self.dbg2)]:
            for tt in range(8):
                self.ld(buf, srct.ap()[tt * 128:(tt + 1) * 128, :])
                self.ld_wait()
                self.bar()
                self.st(dstt.ap()[tt * 128:(tt + 1) * 128, :], buf)
                self.st_wait()
                self.bar()

    def emit(self):
        self.prep_consts()
        self.prep_ffn_weights()
        self.prep_attn_weights()
        self.bar()
        self.ffn(self.x_in, self.x1b)
        self.bar()
        self.attn(self.x1b, self.x2b)
        self.bar()
        self.ffn(self.x2b, self.out)
        self.bar()
        self.dbg_copy()
        self.st_wait()
        self.nc.sync.wait_ge(self.qa, self.qa_n)
        self.bar()


_CACHE = {}


def _build():
    if "nc" not in _CACHE:
        kb = KB()
        kb.emit()
        _CACHE["nc"] = kb.nc
    return _CACHE["nc"]


def _pack_params(inputs, core):
    f32 = np.float32
    prm = np.zeros((128, PC_DATA), f32)

    def put4(nm, arr):
        a = np.asarray(arr, f32).reshape(NL, 4, 128)
        for l in range(NL):
            c = PCMAP[(nm, l)]
            prm[:, c:c + 4] = a[l].T

    put4("ln1_w", inputs["L0_attn_ln1_w"])
    put4("ln1_b", inputs["L0_attn_ln1_b"])
    put4("ln2_w", inputs["L0_attn_ln2_w"])
    put4("ln2_b", inputs["L0_attn_ln2_b"])
    put4("sc_lin", inputs["L0_attn_sc_lin"])
    put4("phz_re", inputs["L0_attn_phz_re"])
    put4("phz_im", inputs["L0_attn_phz_im"])
    put4("lc_re", inputs["L0_attn_lc_re"])
    put4("lc_im", inputs["L0_attn_lc_im"])
    put4("b2", inputs["L0_attn_b2"])
    b1 = np.asarray(inputs["L0_attn_b1"], f32).reshape(NL, 16, 128)
    for l in range(NL):
        c = PCMAP[("b1", l)]
        prm[:, c:c + 16] = b1[l].T
    for nm, key in [("bin", "L0_attn_bin"), ("lnl_w", "L0_attn_lnl_w"),
                    ("lnl_b", "L0_attn_lnl_b")]:
        c = PCMAP[(nm, 0)]
        prm[:, c:c + 4] = np.asarray(inputs[key], f32).reshape(4, 128).T
    c = PCMAP[("cmask", 0)]
    prm[:, c + core] = 1.0
    c = PCMAP[("negt", 0)]
    prm[:, c:c + 14] = -np.arange(1, 15, dtype=f32)[None, :]
    c = PCMAP[("cmask8", 0)]
    m8 = np.zeros(8, f32)
    m8[core] = 1.0
    prm[:, c:c + 64] = np.tile(m8, 8)[None, :]
    return prm


def kernel(**inputs):
    nc = _build()
    f32 = np.float32
    x = np.asarray(inputs["L0_attn_x"], f32).reshape(8192, D)
    gq = np.asarray(inputs["L0_ffn_gate_q"], np.int32)
    uq = np.asarray(inputs["L0_ffn_up_q"], np.int32)
    dq = np.asarray(inputs["L0_ffn_down_q"], np.int32)
    ga = np.asarray(inputs["L0_ffn_gate_absmax"], f32).reshape(M, 64)
    ua = np.asarray(inputs["L0_ffn_up_absmax"], f32).reshape(M, 64)
    da = np.asarray(inputs["L0_ffn_down_absmax"], f32).reshape(D, M // 64)
    bout_rep = np.tile(np.asarray(inputs["L0_attn_bout"], f32)[None, :], (128, 1))
    in_maps = []
    for c in range(NCORES):
        mo, ho = c * MO, c * HO
        in_maps.append({
            "x_in": np.ascontiguousarray(x[c * T:(c + 1) * T]),
            "gq": np.ascontiguousarray(gq[mo:mo + MO]),
            "uq": np.ascontiguousarray(uq[mo:mo + MO]),
            "dq": np.ascontiguousarray(dq[ho:ho + HO]),
            "ga": np.ascontiguousarray(ga[mo:mo + MO]),
            "ua": np.ascontiguousarray(ua[mo:mo + MO]),
            "da": np.ascontiguousarray(da[ho:ho + HO]),
            "win": np.asarray(inputs["L0_attn_Win"], f32),
            "w1": np.asarray(inputs["L0_attn_W1"], f32),
            "w2": np.asarray(inputs["L0_attn_W2"], f32),
            "wout": np.asarray(inputs["L0_attn_Wout"], f32),
            "prm_in": _pack_params(inputs, c),
            "bout_rep": bout_rep,
        })
    res = run_bass_kernel_spmd(nc, in_maps, core_ids=list(range(NCORES)))
    global _LAST_RES
    _LAST_RES = res
    out = np.concatenate([res.results[c]["out"] for c in range(NCORES)], axis=0)
    return out.reshape(1, 8192, D).astype(np.float32)


# version 11
# speedup vs baseline: 1.0570x; 1.0029x over previous
import numpy as np
import ml_dtypes

import concourse.bass as bass
import concourse.mybir as mybir
from concourse.bass_utils import run_bass_kernel_spmd

F32 = mybir.dt.float32
BF16 = mybir.dt.bfloat16
I32 = mybir.dt.int32
AF = mybir.ActivationFunctionType
OP = mybir.AluOpType
AXX = mybir.AxisListType.X

NCORES = 8
T = 1024
TH = 512
D = 4096
M = 14336
DA = 512
DFF = 2048
NL = 4
EPS = 1e-5

MC = M // 128
VC = D // 128
DC = DA // 128
FC = DFF // 128
MO = M // NCORES
HO = D // NCORES
MSUB = 3584
NSUB = M // MSUB

NF4 = [-1.0, -0.6961928009986877, -0.5250730514526367, -0.39491748809814453,
       -0.28444138169288635, -0.18477343022823334, -0.09105003625154495, 0.0,
       0.07958029955625534, 0.16093020141124725, 0.24611230194568634,
       0.33791524171829224, 0.44070982933044434, 0.5626170039176941,
       0.7229568362236023, 1.0]
NF_A = NF4[0]
NF_B = NF4[1] - NF4[0]
NF_G = [NF4[t + 1] - 2 * NF4[t] + NF4[t - 1] for t in range(1, 15)]

_LAST_RES = None
DEBUG = False

PQ = ["ln1_w", "ln1_b", "ln2_w", "ln2_b", "sc_lin",
      "phz_re", "phz_im", "lc_re", "lc_im", "b2"]
PCMAP = {}
_col = 0
for _l in range(NL):
    for _nm in PQ:
        PCMAP[(_nm, _l)] = _col
        _col += 4
for _l in range(NL):
    PCMAP[("b1", _l)] = _col
    _col += 16
for _nm in ["bin", "lnl_w", "lnl_b"]:
    PCMAP[(_nm, 0)] = _col
    _col += 4
PCMAP[("cmask", 0)] = _col
_col += 8
PCMAP[("negt", 0)] = _col
_col += 14
PCMAP[("cmask8", 0)] = _col
_col += 64
PC_DATA = _col
PC_SCR = PC_DATA
PCOLS = PC_DATA + 76

AR = 102400
A_XFM = 0
A_GB = 16384
A_UB = 24576
A_HACT = 32768
A_DBUF = 16384
A_HSIL = 98816
F_XS = 0
F_OT = 2048
F_LNX = 16384
F_LNSQ = 20480
A_LNB = 49152
F_DQ_QI = 0
F_DQ_QF = 4096
F_DQ_RL = 8192
F_DQ_AC = 12288
F_DQ_AM = 16384
A_DQ_WB = 40960
A_DQ_ST = 49152
F_H = 0
F_Y = 4096
F_U = 16384
F_V = 20480
F_C = 24576
F_S = 28672
F_P = 32768
F_RHO = 37120
F_TMD = 38144
F_LN2 = 7168
A_UMLP = 78336
A_WO = 78336
A_XFM2 = 78336
A_Y2 = 94720
A_HLN = 94720
A_WSTR = 32768
A_WIN = 65536
F_STAT = 39168
F_MTMP = 14336
F_BOUT = 14336
F_X1S = 14848
F_WTMP = 15360


class KB:
    def __init__(self):
        nc = bass.Bass("TRN2", target_bir_lowering=False)
        self.nc = nc
        self.qs_n = 0
        self.qa_n = 0
        self.qg_n = 0

        dt = nc.dram_tensor
        self.x_in = dt("x_in", [T, D], F32, kind="ExternalInput")
        self.gq = dt("gq", [MO, D], I32, kind="ExternalInput")
        self.uq = dt("uq", [MO, D], I32, kind="ExternalInput")
        self.dq = dt("dq", [HO, M], I32, kind="ExternalInput")
        self.ga = dt("ga", [MO, 64], F32, kind="ExternalInput")
        self.ua = dt("ua", [MO, 64], F32, kind="ExternalInput")
        self.da = dt("da", [HO, M // 64], F32, kind="ExternalInput")
        self.win = dt("win", [DA, D], F32, kind="ExternalInput")
        self.w1 = dt("w1", [NL, DFF, DA], F32, kind="ExternalInput")
        self.w2 = dt("w2", [NL, DA, DFF], F32, kind="ExternalInput")
        self.wout = dt("wout", [D, DA], F32, kind="ExternalInput")
        self.prm_in = dt("prm_in", [128, PC_DATA], F32, kind="ExternalInput")
        self.bout_rep = dt("bout_rep", [128, D], F32, kind="ExternalInput")
        self.out = dt("out", [T, D], F32, kind="ExternalOutput")

        self.wg_own = dt("wg_own", [MO, D], BF16)
        self.wu_own = dt("wu_own", [MO, D], BF16)
        self.wd_own = dt("wd_own", [4 * NSUB * 128, MSUB], BF16)
        self.wg_gath = dt("wg_gath", [M, D], BF16)
        self.wu_gath = dt("wu_gath", [M, D], BF16)
        self.wd_gath = dt("wd_gath", [NCORES * 4 * NSUB * 128, MSUB], BF16)
        self.winT = dt("winT", [DC * 128, D], BF16)
        self.w1T = dt("w1T", [NL * 128, FC * DC * 128], BF16)
        self.w2T = dt("w2T", [NL * 128, FC * DC * 128], BF16)
        self.woutT = dt("woutT", [DC * 128, D], BF16)
        self.x1b = dt("x1b", [T, D], F32)
        self.x2b = dt("x2b", [T, D], F32)
        self.dbg1 = dt("dbg1", [T, D], F32, kind="ExternalOutput") if DEBUG else None
        self.dbg2 = dt("dbg2", [T, D], F32, kind="ExternalOutput") if DEBUG else None
        self.dbgh = dt("dbgh", [8, DA, T], F32, kind="ExternalOutput") if DEBUG else None
        self.dbgp = dt("dbgp", [128, 64], F32, kind="ExternalOutput") if DEBUG else None
        self.cc_in = [dt(f"cc_in{l}", [1, 1024], F32) for l in range(NL)]
        self.cc_out = [dt(f"cc_out{l}", [NCORES, 1024], F32) for l in range(NL)]

        self.ident_b = nc.alloc_sbuf_tensor("ident_b", [128, 128], BF16)
        self.ident_f = nc.alloc_sbuf_tensor("ident_f", [128, 128], F32)
        self.ones_f = nc.alloc_sbuf_tensor("ones_f", [128, 1], F32)
        self.ones_r = nc.alloc_sbuf_tensor("ones_r", [1, 128], F32)
        self.prm = nc.alloc_sbuf_tensor("prm", [128, PCOLS], F32)
        self.arena = nc.alloc_sbuf_tensor("arena", [128, AR], BF16)
        self.ar_b = self.arena.ap()
        self.ar_f = self.arena.ap().bitcast(F32)

        ps = nc.alloc_psum_tensor
        self.ps_a = ps("ps_a", [128, 512], F32)
        self.ps_b = ps("ps_b", [128, 512], F32)
        self.ps_d = [ps(f"ps_d{i}", [128, 512], F32) for i in range(4)]
        self.ps_t = ps("ps_t", [128, 1024], BF16)
        self.ps_m = ps("ps_m", [128, 512], F32)

        self.qs = nc.alloc_semaphore(name="qs")
        self.qa = nc.alloc_semaphore(name="qa")
        self.qg = nc.alloc_semaphore(name="qg")
        self.sc_w = [nc.alloc_semaphore(name=f"scw{i}") for i in range(3)]
        self.sc_l = [nc.alloc_semaphore(name=f"scl{i}") for i in range(NL)]
        self.pe_s = nc.alloc_semaphore(name="pe_s")
        self.ae_s = nc.alloc_semaphore(name="ae_s")
        self.de_s = nc.alloc_semaphore(name="de_s")
        self.pe_n = 0
        self.ae_n = 0
        self.de_n = 0

        ET = mybir.EngineType
        self._bar_engines = [ET.PE, ET.Activation, ET.DVE, ET.SP]
        self._vbar_engines = [ET.Activation, ET.DVE]
        self.st_s = nc.alloc_semaphore(name="st_s")
        self.st_n = 0
        self.pt_s = nc.alloc_semaphore(name="pt_s")
        self.ev_s = nc.alloc_semaphore(name="ev_s")
        self.pt_n = 0
        self.ev_n = 0
        self.c_ident_b = nc.inline_tensor(
            np.eye(128, dtype=np.float32).astype(ml_dtypes.bfloat16), name="cib")
        self.c_ident_f = nc.inline_tensor(np.eye(128, dtype=np.float32), name="cif")

    def bar(self):
        self.nc.multi_engine_barrier(self._bar_engines)

    def allbar(self):
        self.nc.all_engine_barrier()

    def vbar(self):
        self.nc.multi_engine_barrier(self._vbar_engines)

    def ld(self, out, in_):
        self.nc.sync.dma_start(out=out, in_=in_).then_inc(self.qs, 16)
        self.qs_n += 16

    def ld_wait(self):
        self.nc.sync.wait_ge(self.qs, self.qs_n)

    def st(self, out, in_):
        self.nc.scalar.dma_start(out=out, in_=in_).then_inc(self.qa, 16)
        self.qa_n += 16

    def st_wait(self):
        self.nc.scalar.wait_ge(self.qa, self.qa_n)

    def fv(self, off, cols, rows=128):
        return self.ar_f[0:rows, off:off + cols]

    def bv(self, off, cols):
        return self.ar_b[:, off:off + cols]

    def pcol(self, nm, l=0, tile=0, n=1):
        c = PCMAP[(nm, l)] + tile
        return self.prm.ap()[:, c:c + n]

    def scol(self, i, n=1):
        c = PC_SCR + i
        return self.prm.ap()[:, c:c + n]

    def prep_consts(self):
        nc = self.nc
        self.ld(self.ident_b.ap(), self.c_ident_b.ap())
        self.ld(self.ident_f.ap(), self.c_ident_f.ap())
        self.ld(self.prm.ap()[:, 0:PC_DATA], self.prm_in.ap())
        nc.vector.memset(self.ones_f.ap(), 1.0)
        nc.vector.memset(self.ones_r.ap(), 1.0)
        self.eps_c = self.scol(70)
        self.mhalf_c = self.scol(71)
        nc.vector.memset(self.eps_c, EPS)
        nc.vector.memset(self.mhalf_c, -0.5)
        self.ld_wait()
        self.bar()

    def dq_ld(self, k, q_src, am_src, fcols):
        b = k % 2
        qi = self.ar_f[:, b * 4096:b * 4096 + fcols].bitcast(I32)
        amax = self.fv(22528 + b * 64, fcols // 64)
        self.ld(qi, q_src)
        self.ld(amax, am_src)
        return self.qs_n

    def dequant_tile(self, k, fcols, qs_need):
        nc = self.nc
        b = k % 2
        qi = self.ar_f[:, b * 4096:b * 4096 + fcols].bitcast(I32)
        qf = self.fv(8192, fcols)
        rl = [self.fv(12288, fcols), self.fv(26624, fcols)]
        acc = self.fv(16384, fcols)
        nb = fcols // 64
        amax = self.fv(22528 + b * 64, nb)
        wb = self.bv(A_DQ_WB, fcols)
        ae0, de0 = self.ae_n, self.de_n
        nc.vector.wait_ge(self.qs, qs_need)
        nc.vector.tensor_copy(out=qf, in_=qi).then_inc(self.de_s, 1)
        self.de_n += 1
        nc.vector.tensor_scalar(out=acc, in0=qf, scalar1=NF_B, scalar2=NF_A,
                                op0=OP.mult, op1=OP.add)
        nc.scalar.wait_ge(self.de_s, de0 + 1)
        for t in range(1, 15):
            rb = t % 2
            if t > 2:
                nc.scalar.wait_ge(self.de_s, de0 + 1 + (t - 2))
            nc.scalar.activation(out=rl[rb], in_=qf, func=AF.Relu,
                                 bias=self.pcol("negt", 0, t - 1),
                                 scale=1.0).then_inc(self.ae_s, 1)
            self.ae_n += 1
            nc.vector.wait_ge(self.ae_s, ae0 + t)
            op_ = nc.vector.scalar_tensor_tensor(
                out=acc, in0=rl[rb], scalar=NF_G[t - 1], in1=acc,
                op0=OP.mult, op1=OP.add)
            op_.then_inc(self.de_s, 1)
            self.de_n += 1
        amv = amax.unsqueeze(2).broadcast_to([128, nb, 64])
        accv = acc.rearrange("p (nb b) -> p nb b", b=64)
        wv = wb.rearrange("p (nb b) -> p nb b", b=64)
        nc.vector.tensor_tensor(out=wv, in0=accv, in1=amv, op=OP.mult)
        self.bar()
        return wb

    def transpose_pipe(self, srcs, writer):
        nc = self.nc
        banks = [self.ps_t.ap(), self.ps_d[2].ap().bitcast(BF16)]
        pt0, ev0 = self.pt_n, self.ev_n
        ngroups = (len(srcs) + 7) // 8
        for g in range(ngroups):
            g0 = g * 8
            kn = min(8, len(srcs) - g0)
            bk = banks[g % 2]
            if g >= 2:
                nc.tensor.wait_ge(self.ev_s, ev0 + g - 1)
            for i in range(kn):
                mm = nc.tensor.transpose(bk[:, i * 128:(i + 1) * 128],
                                         srcs[g0 + i], self.ident_b.ap())
                if i == kn - 1:
                    mm.then_inc(self.pt_s, 1)
            self.pt_n += 1
            nc.vector.wait_ge(self.pt_s, pt0 + g + 1)
            last = writer(g0, kn, bk)
            last.then_inc(self.ev_s, 1)
            self.ev_n += 1

    def transpose_wait(self):
        self.nc.scalar.wait_ge(self.ev_s, self.ev_n)
        self.bar()

    def transpose_to_strip(self, src_b, strip, nk):
        srcs = [src_b[:, k * 128:(k + 1) * 128] for k in range(nk)]

        def wr(g0, kn, bk):
            return self.nc.vector.tensor_copy(
                out=strip[:, g0 * 128:(g0 + kn) * 128], in_=bk[:, 0:kn * 128])
        self.transpose_pipe(srcs, wr)
        self.transpose_wait()

    def prep_ffn_weights(self):
        nc = self.nc
        strip = self.bv(A_DQ_ST, 4096)
        tiles = []
        for (qsrc, asrc, own) in [(self.gq, self.ga, self.wg_own),
                                  (self.uq, self.ua, self.wu_own)]:
            ownv = own.ap().rearrange("(s p) f -> p s f", p=128)
            for mc in range(MO // 128):
                tiles.append((qsrc.ap()[mc * 128:(mc + 1) * 128, :],
                              asrc.ap()[mc * 128:(mc + 1) * 128, :],
                              D, ownv[:, mc, :], 32))
        wdv = self.wd_own.ap().rearrange("(s p) f -> p s f", p=128)
        for hc in range(4):
            for ms in range(NSUB):
                tiles.append((
                    self.dq.ap()[hc * 128:(hc + 1) * 128,
                                 ms * MSUB:(ms + 1) * MSUB],
                    self.da.ap()[hc * 128:(hc + 1) * 128,
                                 ms * (MSUB // 64):(ms + 1) * (MSUB // 64)],
                    MSUB, wdv[:, ms * 4 + hc, :], 28))
        ags = {13: 0, 27: 1, 43: 2}
        ag_args = [(self.wg_own, self.wg_gath, self.sc_w[0]),
                   (self.wu_own, self.wu_gath, self.sc_w[1]),
                   (self.wd_own, self.wd_gath, self.sc_w[2])]
        qs_need = self.dq_ld(0, *tiles[0][:3])
        for k, (qsrc, asrc, fcols, dst, nk) in enumerate(tiles):
            if k + 1 < len(tiles):
                nxt = self.dq_ld(k + 1, *tiles[k + 1][:3])
            wb = self.dequant_tile(k, fcols, qs_need)
            qs_need = nxt if k + 1 < len(tiles) else qs_need
            self.transpose_to_strip(wb, strip, nk)
            self.st(dst, strip[:, 0:fcols])
            self.st_wait()
            self.bar()
            if k in ags:
                own, gath, sem = ag_args[ags[k]]
                nc.gpsimd.wait_ge(self.qa, self.qa_n)
                nc.gpsimd.collective_compute(
                    "AllGather", OP.bypass,
                    replica_groups=[list(range(NCORES))],
                    ins=[own.ap().opt()], outs=[gath.ap().opt()],
                ).then_inc(sem)

    def prep_attn_weights(self):
        nc = self.nc
        src16 = self.fv(0, 16384)
        cast16 = self.bv(32768, 16384)
        strip8 = self.bv(49152, 8192)
        wstr = [self.bv(57344 + i * 4096, 4096) for i in range(4)]

        def group_transpose(cast_v, pairs, dst_writer):
            srcs = [cast_v[:, c:c + 128] for c in pairs]
            self.transpose_pipe(srcs, dst_writer)
            self.transpose_wait()

        self.ld(src16.rearrange("p (dc v) -> p dc v", dc=4),
                self.win.ap().rearrange("(dc p) v -> p dc v", p=128))
        self.ld_wait()
        self.bar()
        nc.vector.tensor_copy(out=cast16, in_=src16)
        self.bar()
        winTv = self.winT.ap().rearrange("(s p) f -> p s f", p=128)
        for dc in range(DC):
            pairs = [dc * 4096 + vc * 128 for vc in range(VC)]

            def wr(g0, kn, bk, dc=dc):
                return nc.vector.tensor_copy(
                    out=wstr[dc][:, g0 * 128:(g0 + kn) * 128],
                    in_=bk[:, 0:kn * 128])
            group_transpose(cast16, pairs, wr)
        for dc in range(DC):
            self.st(winTv[:, dc, :], wstr[dc])
        self.st_wait()
        self.bar()
        w1Tv = self.w1T.ap().rearrange("(s p) f -> p s f", p=128)
        w2Tv = self.w2T.ap().rearrange("(s p) f -> p s f", p=128)
        for l in range(NL):
            self.ld(src16[:, 0:8192].rearrange("p (fc d) -> p fc d", fc=FC),
                    self.w1.ap()[l].rearrange("(fc p) d -> p fc d", p=128))
            self.ld_wait()
            self.bar()
            nc.vector.tensor_copy(out=cast16[:, 0:8192], in_=src16[:, 0:8192])
            self.bar()
            pairs = [fc * 512 + dc * 128 for fc in range(FC) for dc in range(DC)]

            def wr1(g0, kn, bk):
                return nc.vector.tensor_copy(
                    out=strip8[:, g0 * 128:(g0 + kn) * 128],
                    in_=bk[:, 0:kn * 128])
            group_transpose(cast16, pairs, wr1)
            self.st(w1Tv[:, l, :], strip8)
            self.st_wait()
            self.bar()
            self.ld(src16[:, 0:8192].rearrange("p (dc f) -> p dc f", dc=DC),
                    self.w2.ap()[l].rearrange("(dc p) f -> p dc f", p=128))
            self.ld_wait()
            self.bar()
            nc.vector.tensor_copy(out=cast16[:, 0:8192], in_=src16[:, 0:8192])
            self.bar()
            pairs = [dc * 2048 + fc * 128 for dc in range(DC) for fc in range(FC)]
            group_transpose(cast16, pairs, wr1)
            self.st(w2Tv[:, l, :], strip8)
            self.st_wait()
            self.bar()
        self.ld(src16.rearrange("p (vc d) -> p vc d", vc=VC),
                self.wout.ap().rearrange("(vc p) d -> p vc d", p=128))
        self.ld_wait()
        self.bar()
        nc.vector.tensor_copy(out=cast16, in_=src16)
        self.bar()
        woutTv = self.woutT.ap().rearrange("(s p) f -> p s f", p=128)
        pairs = [vc * 512 + dc * 128 for vc in range(VC) for dc in range(DC)]

        def wrout(g0, kn, bk):
            vc0 = g0 // 4
            last = None
            for dc in range(DC):
                sv = bk.rearrange("p (v d f) -> p v d f",
                                  v=2, d=4)[:, 0:kn // 4, dc, :]
                dv = wstr[dc][:, vc0 * 128:(vc0 + kn // 4) * 128]
                last = nc.vector.tensor_copy(out=dv, in_=sv)
            return last
        group_transpose(cast16, pairs, wrout)
        for dc in range(DC):
            self.st(woutTv[:, dc, :], wstr[dc])
        self.st_wait()
        self.bar()

    def ln_big_half(self, src, th, xfm):
        nc = self.nc
        xt = [self.fv(F_LNX + tt * 4096, 4096) for tt in range(4)]
        sq = self.fv(32768, 4096)
        xln = self.bv(73728, 4096)
        ssum = self.scol(0, 4)
        nmu = self.scol(4, 4)
        msq = self.scol(8, 4)
        rstd = self.scol(12, 4)
        for tt in range(4):
            r0 = th * TH + tt * 128
            self.ld(xt[tt], src.ap()[r0:r0 + 128, :])
        self.ld_wait()
        self.bar()
        for tt in range(4):
            nc.vector.tensor_reduce(out=ssum[:, tt:tt + 1], in_=xt[tt],
                                    axis=AXX, op=OP.add)
            nc.scalar.activation(out=sq, in_=xt[tt], func=AF.Square)
            self.vbar()
            nc.vector.tensor_reduce(out=msq[:, tt:tt + 1], in_=sq,
                                    axis=AXX, op=OP.add)
            self.vbar()
        nc.vector.tensor_scalar(out=nmu, in0=ssum, scalar1=-1.0 / D,
                                scalar2=None, op0=OP.mult)
        nc.vector.tensor_scalar(out=msq, in0=msq, scalar1=1.0 / D,
                                scalar2=None, op0=OP.mult)
        self.vbar()
        nc.scalar.activation(out=rstd, in_=nmu, func=AF.Square)
        self.vbar()
        nc.vector.tensor_tensor(out=msq, in0=msq, in1=rstd, op=OP.subtract)
        self.vbar()
        nc.scalar.activation(out=msq, in_=msq, func=AF.Ln,
                             bias=self.eps_c, scale=1.0)
        self.vbar()
        nc.scalar.activation(out=rstd, in_=msq, func=AF.Exp,
                             scale=self.mhalf_c)
        self.vbar()
        for tt in range(4):
            nc.vector.tensor_scalar(out=xln, in0=xt[tt],
                                    scalar1=nmu[:, tt:tt + 1],
                                    scalar2=rstd[:, tt:tt + 1],
                                    op0=OP.add, op1=OP.mult)
            self.bar()
            srcs = [xln[:, k * 128:(k + 1) * 128] for k in range(32)]

            def wrln(g0, kn, bk, tt=tt):
                dstv = xfm.rearrange("p (v t) -> p v t", t=512)[
                    :, g0:g0 + kn, tt * 128:(tt + 1) * 128]
                srcv = bk.rearrange("p (v t) -> p v t", t=128)[:, 0:kn, :]
                return nc.vector.tensor_copy(out=dstv, in_=srcv)
            self.transpose_pipe(srcs, wrln)
            self.transpose_wait()

    def ffn(self, src, dst):
        nc = self.nc
        nc.sync.wait_ge(self.sc_w[0], 1)
        nc.sync.wait_ge(self.sc_w[1], 1)
        self.bar()
        xfm = self.bv(A_XFM, 16384)
        gbuf = [self.bv(A_GB + i * 4096, 4096) for i in range(2)]
        ubuf = [self.bv(A_UB + i * 4096, 4096) for i in range(2)]
        hact = self.bv(A_HACT, MC * 512)
        dbuf = [self.bv(0, 4 * MSUB), self.bv(16384, 4 * MSUB)]
        hsil = [self.bv(A_HSIL + i * 512, 512) for i in range(2)]
        wgv = self.wg_gath.ap().rearrange("(s p) f -> p s f", p=128)
        wuv = self.wu_gath.ap().rearrange("(s p) f -> p s f", p=128)
        wdv = self.wd_gath.ap().rearrange("(s p) f -> p s f", p=128)
        xs = self.fv(45056, 2048)
        ot = self.fv(47104, 2048)
        ps_g = [self.ps_a, self.ps_d[0]]
        ps_u = [self.ps_b, self.ps_d[1]]
        for th in range(2):
            self.ln_big_half(src, th, xfm)
            pe0, ae0, de0 = self.pe_n, self.ae_n, self.de_n
            qs_after = {}
            for mg in range(MC):
                b = mg % 2
                if mg >= 2:
                    nc.sync.wait_ge(self.pe_s, pe0 + 2 * (mg - 2) + 2)
                self.ld(gbuf[b], wgv[:, mg, :])
                self.ld(ubuf[b], wuv[:, mg, :])
                qs_after[mg] = self.qs_n
                nc.tensor.wait_ge(self.qs, qs_after[mg])
                if mg >= 2:
                    nc.tensor.wait_ge(self.ae_s, ae0 + (mg - 2) + 1)
                    nc.tensor.wait_ge(self.de_s, de0 + (mg - 2) + 1)
                for k in range(VC):
                    mm = nc.tensor.matmul(ps_g[b].ap(),
                                          gbuf[b][:, k * 128:(k + 1) * 128],
                                          xfm[:, k * 512:(k + 1) * 512],
                                          start=(k == 0), stop=(k == VC - 1))
                    if k == VC - 1:
                        mm.then_inc(self.pe_s, 1)
                self.pe_n += 1
                for k in range(VC):
                    mm = nc.tensor.matmul(ps_u[b].ap(),
                                          ubuf[b][:, k * 128:(k + 1) * 128],
                                          xfm[:, k * 512:(k + 1) * 512],
                                          start=(k == 0), stop=(k == VC - 1))
                    if k == VC - 1:
                        mm.then_inc(self.pe_s, 1)
                self.pe_n += 1
                nc.scalar.wait_ge(self.pe_s, pe0 + 2 * mg + 1)
                if mg >= 2:
                    nc.scalar.wait_ge(self.de_s, de0 + (mg - 2) + 1)
                nc.scalar.activation(out=hsil[b], in_=ps_g[b].ap(),
                                     func=AF.Silu).then_inc(self.ae_s, 1)
                self.ae_n += 1
                nc.vector.wait_ge(self.pe_s, pe0 + 2 * mg + 2)
                nc.vector.wait_ge(self.ae_s, ae0 + mg + 1)
                nc.vector.tensor_tensor(
                    out=hact[:, mg * 512:(mg + 1) * 512], in0=hsil[b],
                    in1=ps_u[b].ap(), op=OP.mult).then_inc(self.de_s, 1)
                self.de_n += 1
            nc.sync.wait_ge(self.de_s, de0 + MC)
            if th == 0:
                nc.sync.wait_ge(self.sc_w[2], 1)
            self.bar()
            pe0, de0 = self.pe_n, self.de_n
            qs_xs = {}
            for hb in range(NCORES):
                for ms in range(NSUB):
                    j = hb * NSUB + ms
                    b = j % 2
                    if j >= 2:
                        nc.sync.wait_ge(self.pe_s, pe0 + j - 1)
                    blk0 = hb * 16 + ms * 4
                    self.ld(dbuf[b].rearrange("p (r f) -> p r f", r=4),
                            wdv[:, blk0:blk0 + 4, :])
                    qs_after[j] = self.qs_n
                    if ms == NSUB - 1:
                        if hb > 0:
                            nc.sync.wait_ge(self.de_s, de0 + hb)
                        for tc in range(4):
                            r0 = th * TH + tc * 128
                            self.ld(xs[:, tc * 512:(tc + 1) * 512],
                                    src.ap()[r0:r0 + 128,
                                             hb * 512:(hb + 1) * 512])
                        qs_xs[hb] = self.qs_n
                    nc.tensor.wait_ge(self.qs, qs_after[j])
                    if ms == 0 and hb > 0:
                        nc.tensor.wait_ge(self.de_s, de0 + hb)
                    drv = dbuf[b].rearrange("p (r f) -> p r f", r=4)
                    for k in range(28):
                        mg = ms * 28 + k
                        for tc in range(4):
                            mm = nc.tensor.matmul(
                                self.ps_d[tc].ap(),
                                hact[:, mg * 512 + tc * 128:
                                     mg * 512 + (tc + 1) * 128],
                                drv[:, :, k * 128:(k + 1) * 128],
                                start=(ms == 0 and k == 0),
                                stop=(ms == NSUB - 1 and k == 27))
                            if k == 27 and tc == 3:
                                mm.then_inc(self.pe_s, 1)
                    self.pe_n += 1
                nc.vector.wait_ge(self.pe_s, pe0 + NSUB * (hb + 1))
                nc.vector.wait_ge(self.qs, qs_xs[hb])
                if hb > 0:
                    nc.vector.wait_ge(self.qa, self.qa_n)
                for tc in range(4):
                    op_ = nc.vector.scalar_tensor_tensor(
                        out=ot[:, tc * 512:(tc + 1) * 512],
                        in0=self.ps_d[tc].ap(), scalar=0.5,
                        in1=xs[:, tc * 512:(tc + 1) * 512],
                        op0=OP.mult, op1=OP.add)
                    if tc == 3:
                        op_.then_inc(self.de_s, 1)
                self.de_n += 1
                nc.scalar.wait_ge(self.de_s, de0 + hb + 1)
                for tc in range(4):
                    r0 = th * TH + tc * 128
                    self.st(dst.ap()[r0:r0 + 128, hb * 512:(hb + 1) * 512],
                            ot[:, tc * 512:(tc + 1) * 512])
            self.st_wait()
            self.bar()

    def ln_inner(self, h_tiles, w_nm, b_nm, l, out_tiles, silu):
        nc = self.nc
        sq = self.fv(F_MTMP, 1024)
        for half in range(2):
            tgt = self.ps_d[half].ap()[0:1, :]
            for ct in range(4):
                nc.tensor.matmul(tgt, self.ones_f.ap(),
                                 h_tiles[ct][:, half * 512:(half + 1) * 512],
                                 start=(ct == 0), stop=(ct == 3))
        self.bar()
        for ct in range(4):
            nc.scalar.activation(out=sq, in_=h_tiles[ct], func=AF.Square)
            self.bar()
            for half in range(2):
                tgt = self.ps_d[2 + half].ap()[0:1, :]
                nc.tensor.matmul(tgt, self.ones_f.ap(),
                                 sq[:, half * 512:(half + 1) * 512],
                                 start=(ct == 0), stop=(ct == 3))
            self.bar()
        st_ = self.fv(F_STAT, 2048, rows=1)
        for half in range(2):
            nc.vector.tensor_copy(out=st_[:, half * 512:(half + 1) * 512],
                                  in_=self.ps_d[half].ap()[0:1, :])
            nc.vector.tensor_copy(out=st_[:, 1024 + half * 512:1024 + (half + 1) * 512],
                                  in_=self.ps_d[2 + half].ap()[0:1, :])
        nc.vector.tensor_scalar(out=st_[:, 0:1024], in0=st_[:, 0:1024],
                                scalar1=-1.0 / DA, scalar2=None, op0=OP.mult)
        nc.vector.tensor_scalar(out=st_[:, 1024:2048], in0=st_[:, 1024:2048],
                                scalar1=1.0 / DA, scalar2=None, op0=OP.mult)
        self.bar()
        musq = self.fv(F_MTMP, 1024, rows=1)
        nc.scalar.activation(out=musq, in_=st_[:, 0:1024], func=AF.Square)
        self.bar()
        nc.vector.tensor_tensor(out=st_[:, 1024:2048], in0=st_[:, 1024:2048],
                                in1=musq, op=OP.subtract)
        self.bar()
        nc.scalar.activation(out=st_[:, 1024:2048], in_=st_[:, 1024:2048],
                             func=AF.Ln, bias=self.eps_c[0:1, :], scale=1.0)
        self.bar()
        nc.scalar.activation(out=st_[:, 1024:2048], in_=st_[:, 1024:2048],
                             func=AF.Exp, scale=self.mhalf_c[0:1, :])
        self.bar()
        for half in range(2):
            nc.tensor.matmul(self.ps_d[half].ap(), self.ones_r.ap(),
                             st_[:, half * 512:(half + 1) * 512],
                             start=True, stop=True)
            nc.tensor.matmul(self.ps_d[2 + half].ap(), self.ones_r.ap(),
                             st_[:, 1024 + half * 512:1024 + (half + 1) * 512],
                             start=True, stop=True)
        self.bar()
        tmp = self.fv(F_MTMP, 1024)
        for ct in range(4):
            for half in range(2):
                s0 = half * 512
                nc.vector.tensor_tensor(out=tmp[:, s0:s0 + 512],
                                        in0=h_tiles[ct][:, s0:s0 + 512],
                                        in1=self.ps_d[half].ap(), op=OP.add)
                nc.vector.tensor_tensor(out=tmp[:, s0:s0 + 512],
                                        in0=tmp[:, s0:s0 + 512],
                                        in1=self.ps_d[2 + half].ap(), op=OP.mult)
            wc = self.pcol(w_nm, l, ct)
            bc = self.pcol(b_nm, l, ct)
            if silu:
                nc.vector.tensor_scalar(out=tmp, in0=tmp, scalar1=wc, scalar2=bc,
                                        op0=OP.mult, op1=OP.add)
                self.bar()
                nc.scalar.activation(out=out_tiles[ct], in_=tmp, func=AF.Silu)
            else:
                nc.vector.tensor_scalar(out=out_tiles[ct], in0=tmp, scalar1=wc,
                                        scalar2=bc, op0=OP.mult, op1=OP.add)
            self.bar()

    def sconv(self, l, h_tiles, y_tiles):
        nc = self.nc
        u_t = [self.fv(F_U + ct * 1024, 1024) for ct in range(4)]
        v_t = [self.fv(F_V + ct * 1024, 1024) for ct in range(4)]
        C_t = [self.fv(F_C + ct * 1024, 1024) for ct in range(4)]
        S_t = [self.fv(F_S + ct * 1024, 1024) for ct in range(4)]
        P_t = [self.fv(F_P + ct * 1056, 1056) for ct in range(4)]
        rho_t = self.fv(F_RHO, 1024)
        tmod = self.fv(F_TMD, 1024)
        CTH = self.scol(0, 4)
        STH = self.scol(4, 4)
        RH = self.scol(8, 4)
        T0 = self.scol(12, 4)
        T1 = self.scol(16, 4)
        T2 = self.scol(20, 4)
        stage = self.scol(24, 8)
        R1C = self.scol(32, 4)
        R1S = self.scol(36, 4)
        BC = self.scol(40, 4)
        BS = self.scol(44, 4)
        UI0 = self.scol(48, 4)
        VI0 = self.scol(52, 4)
        UIN = self.scol(56, 4)
        VIN = self.scol(60, 4)
        pr4 = self.pcol("phz_re", l, 0, 4)
        pi4 = self.pcol("phz_im", l, 0, 4)
        lcr4 = self.pcol("lc_re", l, 0, 4)
        lci4 = self.pcol("lc_im", l, 0, 4)
        cmask = self.pcol("cmask", 0, 0, 8)
        nc.scalar.activation(out=T0, in_=pr4, func=AF.Square)
        nc.scalar.activation(out=T1, in_=pi4, func=AF.Square)
        self.vbar()
        nc.vector.tensor_tensor(out=T0, in0=T0, in1=T1, op=OP.add)
        self.vbar()
        nc.scalar.activation(out=T1, in_=T0, func=AF.Sqrt)
        self.vbar()
        nc.vector.reciprocal(out=T2, in_=T1)
        self.vbar()
        nc.vector.tensor_tensor(out=T2, in0=T0, in1=T2, op=OP.mult)
        self.vbar()
        nc.vector.tensor_tensor(out=T1, in0=T1, in1=T2, op=OP.add)
        self.vbar()
        nc.vector.tensor_scalar(out=T1, in0=T1, scalar1=0.5, scalar2=None,
                                op0=OP.mult)
        self.vbar()
        nc.vector.reciprocal(out=T2, in_=T1)
        nc.scalar.activation(out=RH, in_=T1, func=AF.Tanh)
        self.vbar()
        nc.vector.tensor_tensor(out=CTH, in0=pr4, in1=T2, op=OP.mult)
        nc.vector.tensor_tensor(out=STH, in0=pi4, in1=T2, op=OP.mult)
        self.vbar()
        nc.vector.tensor_tensor(out=T0, in0=STH, in1=lci4, op=OP.mult)
        nc.vector.tensor_tensor(out=T1, in0=CTH, in1=lci4, op=OP.mult)
        self.vbar()
        nc.vector.tensor_tensor(out=UI0, in0=CTH, in1=lcr4, op=OP.mult)
        nc.vector.tensor_tensor(out=VI0, in0=STH, in1=lcr4, op=OP.mult)
        self.vbar()
        nc.vector.tensor_tensor(out=UI0, in0=UI0, in1=T0, op=OP.subtract)
        nc.vector.tensor_tensor(out=VI0, in0=VI0, in1=T1, op=OP.add)
        self.vbar()
        nc.vector.tensor_scalar(out=VI0, in0=VI0, scalar1=-1.0, scalar2=None,
                                op0=OP.mult)
        for ct in range(4):
            Ct, St, Pt = C_t[ct], S_t[ct], P_t[ct]
            nc.vector.memset(Ct[:, 0:1], 1.0)
            nc.vector.memset(St[:, 0:1], 0.0)
            nc.vector.memset(Pt[:, 0:1], 1.0)
        self.vbar()
        for ct in range(4):
            Ct, St, Pt = C_t[ct], S_t[ct], P_t[ct]
            nc.vector.tensor_copy(out=Ct[:, 1:2], in_=CTH[:, ct:ct + 1])
            nc.vector.tensor_copy(out=St[:, 1:2], in_=STH[:, ct:ct + 1])
            nc.vector.tensor_copy(out=Pt[:, 1:2], in_=RH[:, ct:ct + 1])
        self.vbar()
        C4 = self.fv(F_C, 4096).rearrange("p (t j) -> p t j", t=4)
        S4 = self.fv(F_S, 4096).rearrange("p (t j) -> p t j", t=4)
        P4 = self.fv(F_P, 4224).rearrange("p (t j) -> p t j", t=4)
        t4a = self.fv(F_TMD, 4096).rearrange("p (t j) -> p t j", t=4)

        def bcol(V, j, n):
            return V[:, :, j:j + 1].broadcast_to([128, 4, n])
        f = 2
        while f < 1056:
            m = min(f - 1, 1056 - f)
            mcs = min(f - 1, 1024 - f) if f < 1024 else 0
            if mcs > 0:
                nc.vector.tensor_tensor(out=t4a[:, :, 0:mcs],
                                        in0=C4[:, :, 1:1 + mcs],
                                        in1=bcol(S4, f - 1, mcs), op=OP.mult)
                nc.vector.tensor_tensor(out=S4[:, :, f:f + mcs],
                                        in0=S4[:, :, 1:1 + mcs],
                                        in1=bcol(C4, f - 1, mcs), op=OP.mult)
                nc.vector.tensor_tensor(out=t4a[:, :, 512:512 + mcs],
                                        in0=S4[:, :, 1:1 + mcs],
                                        in1=bcol(S4, f - 1, mcs), op=OP.mult)
                nc.vector.tensor_tensor(out=C4[:, :, f:f + mcs],
                                        in0=C4[:, :, 1:1 + mcs],
                                        in1=bcol(C4, f - 1, mcs), op=OP.mult)
                self.vbar()
                nc.vector.tensor_tensor(out=S4[:, :, f:f + mcs],
                                        in0=S4[:, :, f:f + mcs],
                                        in1=t4a[:, :, 0:mcs], op=OP.add)
                nc.vector.tensor_tensor(out=C4[:, :, f:f + mcs],
                                        in0=C4[:, :, f:f + mcs],
                                        in1=t4a[:, :, 512:512 + mcs],
                                        op=OP.subtract)
            nc.vector.tensor_tensor(out=P4[:, :, f:f + m],
                                    in0=P4[:, :, 1:1 + m],
                                    in1=bcol(P4, f - 1, m), op=OP.mult)
            self.vbar()
            f += m
        for ct in range(4):
            Ct, St = C_t[ct], S_t[ct]
            nc.vector.tensor_scalar_mul(out=T0[:, ct:ct + 1],
                                        in0=St[:, 1023:1024],
                                        scalar1=STH[:, ct:ct + 1])
            nc.vector.tensor_scalar_mul(out=T1[:, ct:ct + 1],
                                        in0=Ct[:, 1023:1024],
                                        scalar1=STH[:, ct:ct + 1])
        self.vbar()
        for ct in range(4):
            Ct, St = C_t[ct], S_t[ct]
            nc.vector.scalar_tensor_tensor(out=R1C[:, ct:ct + 1],
                                           in0=Ct[:, 1023:1024],
                                           scalar=CTH[:, ct:ct + 1],
                                           in1=T0[:, ct:ct + 1],
                                           op0=OP.mult, op1=OP.subtract)
            nc.vector.scalar_tensor_tensor(out=R1S[:, ct:ct + 1],
                                           in0=St[:, 1023:1024],
                                           scalar=CTH[:, ct:ct + 1],
                                           in1=T1[:, ct:ct + 1],
                                           op0=OP.mult, op1=OP.add)
        self.vbar()
        pw_c = [self.fv(F_TMD + ct * 16, 8) for ct in range(4)]
        pw_s = [self.fv(F_TMD + ct * 16 + 8, 8) for ct in range(4)]
        for ct in range(4):
            nc.vector.memset(pw_c[ct][:, 0:1], 1.0)
            nc.vector.memset(pw_s[ct][:, 0:1], 0.0)
        self.vbar()
        for k in range(1, 8):
            for ct in range(4):
                nc.vector.tensor_scalar_mul(out=T0[:, ct:ct + 1],
                                            in0=pw_s[ct][:, k - 1:k],
                                            scalar1=R1S[:, ct:ct + 1])
                nc.vector.tensor_scalar_mul(out=T1[:, ct:ct + 1],
                                            in0=pw_c[ct][:, k - 1:k],
                                            scalar1=R1S[:, ct:ct + 1])
            self.vbar()
            for ct in range(4):
                nc.vector.scalar_tensor_tensor(out=pw_c[ct][:, k:k + 1],
                                               in0=pw_c[ct][:, k - 1:k],
                                               scalar=R1C[:, ct:ct + 1],
                                               in1=T0[:, ct:ct + 1],
                                               op0=OP.mult, op1=OP.subtract)
                nc.vector.scalar_tensor_tensor(out=pw_s[ct][:, k:k + 1],
                                               in0=pw_s[ct][:, k - 1:k],
                                               scalar=R1C[:, ct:ct + 1],
                                               in1=T1[:, ct:ct + 1],
                                               op0=OP.mult, op1=OP.add)
            self.vbar()
        cmask8 = self.pcol("cmask8", 0, 0, 64)
        pw_all = self.fv(F_TMD, 64)
        sel64 = self.fv(F_TMD + 64, 64)
        bcs8 = self.fv(F_TMD + 128, 8)
        nc.vector.tensor_tensor(out=sel64, in0=pw_all, in1=cmask8, op=OP.mult)
        self.vbar()
        nc.vector.tensor_reduce(
            out=bcs8.unsqueeze(2),
            in_=sel64.rearrange("p (g k) -> p g k", k=8), axis=AXX, op=OP.add)
        self.vbar()
        for ct in range(4):
            nc.vector.tensor_copy(out=BC[:, ct:ct + 1],
                                  in_=bcs8[:, 2 * ct:2 * ct + 1])
            nc.vector.tensor_copy(out=BS[:, ct:ct + 1],
                                  in_=bcs8[:, 2 * ct + 1:2 * ct + 2])
        self.vbar()
        de0 = self.de_n
        for ct in range(4):
            Ct, St = C_t[ct], S_t[ct]
            bcc = BC[:, ct:ct + 1]
            bsc = BS[:, ct:ct + 1]
            nc.vector.tensor_scalar_mul(out=tmod, in0=St[:, 0:1024],
                                        scalar1=bsc)
            nc.vector.tensor_scalar_mul(out=St[:, 0:1024], in0=St[:, 0:1024],
                                        scalar1=bcc)
            nc.vector.scalar_tensor_tensor(out=St[:, 0:1024],
                                           in0=Ct[:, 0:1024], scalar=bsc,
                                           in1=St[:, 0:1024],
                                           op0=OP.mult, op1=OP.add)
            nc.vector.tensor_scalar_mul(out=Ct[:, 0:1024], in0=Ct[:, 0:1024],
                                        scalar1=bcc)
            nc.vector.tensor_tensor(out=Ct[:, 0:1024], in0=Ct[:, 0:1024],
                                    in1=tmod, op=OP.subtract)
            nc.vector.memset(rho_t, 1.0)
            self.vbar()
            nc.vector.tensor_scalar_mul(out=rho_t, in0=rho_t,
                                        scalar1=RH[:, ct:ct + 1])
            self.vbar()
            nc.vector.tensor_tensor(out=tmod, in0=Ct[:, 0:1024],
                                    in1=y_tiles[ct], op=OP.mult)
            nc.vector.tensor_tensor_scan(out=u_t[ct], data0=rho_t,
                                         data1=tmod, initial=0.0,
                                         op0=OP.mult, op1=OP.add)
            nc.vector.tensor_tensor(out=tmod, in0=St[:, 0:1024],
                                    in1=y_tiles[ct], op=OP.mult)
            nc.vector.tensor_tensor_scan(out=v_t[ct], data0=rho_t,
                                         data1=tmod, initial=0.0,
                                         op0=OP.mult, op1=OP.add)
            nc.vector.tensor_copy(out=stage[:, 2 * ct:2 * ct + 1],
                                  in_=u_t[ct][:, 1023:1024])
            nc.vector.tensor_copy(out=stage[:, 2 * ct + 1:2 * ct + 2],
                                  in_=v_t[ct][:, 1023:1024])
        self.vbar()
        nc.vector.engine_nop().then_inc(self.st_s, 1)
        self.st_n += 1
        nc.gpsimd.wait_ge(self.st_s, self.st_n)
        nc.gpsimd.dma_start(
            out=self.cc_in[l].ap().rearrange("a (p c) -> (a p) c", p=128),
            in_=stage).then_inc(self.qg, 16)
        self.qg_n += 16
        nc.gpsimd.wait_ge(self.qg, self.qg_n)
        nc.gpsimd.collective_compute(
            "AllGather", OP.bypass, replica_groups=[list(range(NCORES))],
            ins=[self.cc_in[l].ap().opt()], outs=[self.cc_out[l].ap().opt()],
        ).then_inc(self.sc_l[l])
        nc.sync.wait_ge(self.sc_l[l], 1)
        eraw = self.fv(F_TMD, 1024, rows=8)
        self.ld(eraw, self.cc_out[l].ap())
        self.ld_wait()
        self.bar()
        ends = self.fv(F_RHO, 64)
        erv = eraw.rearrange("r (p c) -> r p c", c=8)
        for ct in range(4):
            for uv in range(2):
                nc.tensor.transpose(self.ps_m.ap()[:, 8 * (2 * ct + uv):
                                                   8 * (2 * ct + uv) + 8],
                                    erv[:, :, 2 * ct + uv],
                                    self.ident_f.ap()[0:8, 0:8])
        self.bar()
        nc.vector.tensor_copy(out=ends, in_=self.ps_m.ap()[:, 0:64])
        self.bar()
        w8u = [self.fv(F_TMD + ct * 16, 8) for ct in range(4)]
        w8v = [self.fv(F_TMD + ct * 16 + 8, 8) for ct in range(4)]
        for ct in range(4):
            nc.vector.tensor_copy(out=w8u[ct][:, 0:1], in_=UI0[:, ct:ct + 1])
            nc.vector.tensor_copy(out=w8v[ct][:, 0:1], in_=VI0[:, ct:ct + 1])
        self.vbar()
        for cx in range(1, 8):
            for ct in range(4):
                E = P_t[ct][:, 1024:1025]
                Ue = ends[:, (ct * 2) * 8:(ct * 2) * 8 + 8]
                Ve = ends[:, (ct * 2 + 1) * 8:(ct * 2 + 1) * 8 + 8]
                nc.vector.scalar_tensor_tensor(
                    out=w8u[ct][:, cx:cx + 1], in0=w8u[ct][:, cx - 1:cx],
                    scalar=E, in1=Ue[:, cx - 1:cx], op0=OP.mult, op1=OP.add)
                nc.vector.scalar_tensor_tensor(
                    out=w8v[ct][:, cx:cx + 1], in0=w8v[ct][:, cx - 1:cx],
                    scalar=E, in1=Ve[:, cx - 1:cx], op0=OP.mult, op1=OP.add)
            self.vbar()
        w8all = self.fv(F_TMD, 64)
        selb64 = self.fv(F_TMD + 64, 64)
        uv8 = self.fv(F_TMD + 128, 8)
        nc.vector.tensor_tensor(out=selb64, in0=w8all, in1=cmask8, op=OP.mult)
        self.vbar()
        nc.vector.tensor_reduce(
            out=uv8.unsqueeze(2),
            in_=selb64.rearrange("p (g k) -> p g k", k=8), axis=AXX, op=OP.add)
        self.vbar()
        for ct in range(4):
            nc.vector.tensor_copy(out=UIN[:, ct:ct + 1],
                                  in_=uv8[:, 2 * ct:2 * ct + 1])
            nc.vector.tensor_copy(out=VIN[:, ct:ct + 1],
                                  in_=uv8[:, 2 * ct + 1:2 * ct + 2])
        self.vbar()
        for ct in range(4):
            nc.vector.scalar_tensor_tensor(out=u_t[ct], in0=P_t[ct][:, 1:1025],
                                           scalar=UIN[:, ct:ct + 1],
                                           in1=u_t[ct], op0=OP.mult,
                                           op1=OP.add)
            nc.vector.scalar_tensor_tensor(out=v_t[ct], in0=P_t[ct][:, 1:1025],
                                           scalar=VIN[:, ct:ct + 1],
                                           in1=v_t[ct], op0=OP.mult,
                                           op1=OP.add)
            nc.vector.tensor_tensor(out=u_t[ct], in0=u_t[ct],
                                    in1=C_t[ct][:, 0:1024], op=OP.mult)
            nc.vector.tensor_tensor(out=v_t[ct], in0=v_t[ct],
                                    in1=S_t[ct][:, 0:1024], op=OP.mult)
            nc.vector.tensor_tensor(out=u_t[ct], in0=u_t[ct], in1=v_t[ct],
                                    op=OP.add)
            scl = self.pcol("sc_lin", l, ct)
            nc.vector.scalar_tensor_tensor(out=h_tiles[ct], in0=u_t[ct],
                                           scalar=scl, in1=h_tiles[ct],
                                           op0=OP.mult, op1=OP.add)
        self.bar()

    def attn(self, src, dst):
        nc = self.nc
        h_tiles = [self.fv(F_H + ct * 1024, 1024) for ct in range(4)]
        y_tiles = [self.fv(F_Y + ct * 1024, 1024) for ct in range(4)]
        xfm = self.bv(A_XFM2, 16384)
        wsin = self.bv(A_WIN, 4096)
        winTv = self.winT.ap().rearrange("(s p) f -> p s f", p=128)
        for th in range(2):
            self.ln_big_half(src, th, xfm)
            for dc in range(DC):
                self.ld(wsin, winTv[:, dc, :])
                self.ld_wait()
                self.bar()
                for k in range(VC):
                    nc.tensor.matmul(self.ps_a.ap(),
                                     wsin[:, k * 128:(k + 1) * 128],
                                     xfm[:, k * 512:(k + 1) * 512],
                                     start=(k == 0), stop=(k == VC - 1))
                self.bar()
                nc.vector.tensor_scalar(
                    out=h_tiles[dc][:, th * 512:(th + 1) * 512],
                    in0=self.ps_a.ap(), scalar1=self.pcol("bin", 0, dc),
                    scalar2=None, op0=OP.add)
                self.bar()
        self.dbg_h(0, h_tiles)
        u_mlp = self.bv(A_UMLP, 8192)
        y2 = self.bv(A_Y2, 4096)
        wstr = self.bv(A_WSTR, 8192)
        w1Tv = self.w1T.ap().rearrange("(s p) f -> p s f", p=128)
        w2Tv = self.w2T.ap().rearrange("(s p) f -> p s f", p=128)
        tmp = self.fv(F_MTMP, 512)
        for l in range(NL):
            self.ln_inner(h_tiles, "ln1_w", "ln1_b", l, y_tiles, True)
            if l == 0:
                self.dbg_h(1, y_tiles)
            self.sconv(l, h_tiles, y_tiles)
            if l == 0:
                self.dbg_h(2, h_tiles)
            y2t = [y2[:, ct * 1024:(ct + 1) * 1024] for ct in range(4)]
            self.ln_inner(h_tiles, "ln2_w", "ln2_b", l, y2t, False)

            self.ld(wstr, w1Tv[:, l, :])
            self.ld_wait()
            self.bar()
            for th in range(2):
                for fc in range(FC):
                    for dc in range(DC):
                        nc.tensor.matmul(
                            self.ps_a.ap(),
                            wstr[:, (fc * 4 + dc) * 128:(fc * 4 + dc + 1) * 128],
                            y2t[dc][:, th * 512:(th + 1) * 512],
                            start=(dc == 0), stop=(dc == DC - 1))
                    self.bar()
                    nc.scalar.activation(
                        out=u_mlp[:, fc * 512:(fc + 1) * 512],
                        in_=self.ps_a.ap(), func=AF.Silu,
                        bias=self.pcol("b1", l, fc), scale=1.0)
                    self.bar()
                if th == 0:
                    u_sav = self.bv(A_UMLP + 8192, 8192)
                    nc.vector.tensor_copy(out=u_sav, in_=u_mlp)
                    self.bar()
            self.ld(wstr, w2Tv[:, l, :])
            self.ld_wait()
            self.bar()
            for th in range(2):
                usrc = self.bv(A_UMLP + 8192, 8192) if th == 0 else u_mlp
                for dc in range(DC):
                    for fc in range(FC):
                        nc.tensor.matmul(
                            self.ps_b.ap(),
                            wstr[:, (dc * 16 + fc) * 128:(dc * 16 + fc + 1) * 128],
                            usrc[:, fc * 512:(fc + 1) * 512],
                            start=(fc == 0), stop=(fc == FC - 1))
                    self.bar()
                    nc.vector.tensor_scalar(out=tmp, in0=self.ps_b.ap(),
                                            scalar1=self.pcol("b2", l, dc),
                                            scalar2=None, op0=OP.add)
                    self.bar()
                    nc.vector.tensor_tensor(
                        out=h_tiles[dc][:, th * 512:(th + 1) * 512],
                        in0=h_tiles[dc][:, th * 512:(th + 1) * 512],
                        in1=tmp, op=OP.add)
                    self.bar()

            self.dbg_h(3 + l, h_tiles)
        hln = self.bv(A_HLN, 4096)
        hlnt = [hln[:, ct * 1024:(ct + 1) * 1024] for ct in range(4)]
        self.ln_inner(h_tiles, "lnl_w", "lnl_b", 0, hlnt, False)
        self.dbg_h(7, hlnt)
        wo = [self.bv(A_WO + dc * 4096, 4096) for dc in range(DC)]
        woutTv = self.woutT.ap().rearrange("(s p) f -> p s f", p=128)
        for dc in range(DC):
            self.ld(wo[dc], woutTv[:, dc, :])
        self.ld_wait()
        self.bar()
        boutt = self.fv(F_BOUT, 512)
        x1s = self.fv(F_X1S, 512)
        wtmp = self.fv(F_WTMP, 512)
        for vb in range(8):
            self.ld(boutt, self.bout_rep.ap()[:, vb * 512:(vb + 1) * 512])
            self.ld_wait()
            self.bar()
            for tcg in range(8):
                for dc in range(DC):
                    nc.tensor.matmul(
                        self.ps_d[0].ap(),
                        hln[:, dc * 1024 + tcg * 128:dc * 1024 + (tcg + 1) * 128],
                        wo[dc][:, vb * 512:(vb + 1) * 512],
                        start=(dc == 0), stop=(dc == DC - 1))
                self.bar()
                self.ld(x1s, src.ap()[tcg * 128:(tcg + 1) * 128,
                                      vb * 512:(vb + 1) * 512])
                self.ld_wait()
                self.bar()
                nc.vector.tensor_tensor(out=wtmp, in0=self.ps_d[0].ap(),
                                        in1=boutt, op=OP.add)
                self.bar()
                nc.vector.tensor_tensor(out=wtmp, in0=wtmp, in1=x1s, op=OP.add)
                self.bar()
                self.st(dst.ap()[tcg * 128:(tcg + 1) * 128,
                                 vb * 512:(vb + 1) * 512], wtmp)
                self.st_wait()
                self.bar()

    def dbg_h(self, stage, tiles):
        if not DEBUG:
            return
        self.allbar()
        for ct in range(4):
            dst = self.dbgh.ap()[stage, ct * 128:(ct + 1) * 128, :]
            if tiles[ct].dtype == BF16:
                self.nc.gpsimd.dma_start(out=dst, in_=tiles[ct]).then_inc(self.qg, 16)
                self.qg_n += 16
            else:
                self.st(dst, tiles[ct])
        self.st_wait()
        self.nc.gpsimd.wait_ge(self.qg, self.qg_n)
        self.allbar()

    def dbg_copy(self):
        if not DEBUG:
            return
        buf = self.fv(0, 4096)
        for (srct, dstt) in [(self.x1b, self.dbg1), (self.x2b, self.dbg2)]:
            for tt in range(8):
                self.ld(buf, srct.ap()[tt * 128:(tt + 1) * 128, :])
                self.ld_wait()
                self.bar()
                self.st(dstt.ap()[tt * 128:(tt + 1) * 128, :], buf)
                self.st_wait()
                self.bar()

    def emit(self):
        self.prep_consts()
        self.prep_ffn_weights()
        self.prep_attn_weights()
        self.bar()
        self.ffn(self.x_in, self.x1b)
        self.bar()
        self.attn(self.x1b, self.x2b)
        self.bar()
        self.ffn(self.x2b, self.out)
        self.bar()
        self.dbg_copy()
        self.st_wait()
        self.nc.sync.wait_ge(self.qa, self.qa_n)
        self.bar()


_CACHE = {}


def _build():
    if "nc" not in _CACHE:
        kb = KB()
        kb.emit()
        _CACHE["nc"] = kb.nc
    return _CACHE["nc"]


def _pack_params(inputs, core):
    f32 = np.float32
    prm = np.zeros((128, PC_DATA), f32)

    def put4(nm, arr):
        a = np.asarray(arr, f32).reshape(NL, 4, 128)
        for l in range(NL):
            c = PCMAP[(nm, l)]
            prm[:, c:c + 4] = a[l].T

    put4("ln1_w", inputs["L0_attn_ln1_w"])
    put4("ln1_b", inputs["L0_attn_ln1_b"])
    put4("ln2_w", inputs["L0_attn_ln2_w"])
    put4("ln2_b", inputs["L0_attn_ln2_b"])
    put4("sc_lin", inputs["L0_attn_sc_lin"])
    put4("phz_re", inputs["L0_attn_phz_re"])
    put4("phz_im", inputs["L0_attn_phz_im"])
    put4("lc_re", inputs["L0_attn_lc_re"])
    put4("lc_im", inputs["L0_attn_lc_im"])
    put4("b2", inputs["L0_attn_b2"])
    b1 = np.asarray(inputs["L0_attn_b1"], f32).reshape(NL, 16, 128)
    for l in range(NL):
        c = PCMAP[("b1", l)]
        prm[:, c:c + 16] = b1[l].T
    for nm, key in [("bin", "L0_attn_bin"), ("lnl_w", "L0_attn_lnl_w"),
                    ("lnl_b", "L0_attn_lnl_b")]:
        c = PCMAP[(nm, 0)]
        prm[:, c:c + 4] = np.asarray(inputs[key], f32).reshape(4, 128).T
    c = PCMAP[("cmask", 0)]
    prm[:, c + core] = 1.0
    c = PCMAP[("negt", 0)]
    prm[:, c:c + 14] = -np.arange(1, 15, dtype=f32)[None, :]
    c = PCMAP[("cmask8", 0)]
    m8 = np.zeros(8, f32)
    m8[core] = 1.0
    prm[:, c:c + 64] = np.tile(m8, 8)[None, :]
    return prm


def kernel(**inputs):
    nc = _build()
    f32 = np.float32
    x = np.asarray(inputs["L0_attn_x"], f32).reshape(8192, D)
    gq = np.asarray(inputs["L0_ffn_gate_q"], np.int32)
    uq = np.asarray(inputs["L0_ffn_up_q"], np.int32)
    dq = np.asarray(inputs["L0_ffn_down_q"], np.int32)
    ga = np.asarray(inputs["L0_ffn_gate_absmax"], f32).reshape(M, 64)
    ua = np.asarray(inputs["L0_ffn_up_absmax"], f32).reshape(M, 64)
    da = np.asarray(inputs["L0_ffn_down_absmax"], f32).reshape(D, M // 64)
    bout_rep = np.tile(np.asarray(inputs["L0_attn_bout"], f32)[None, :], (128, 1))
    in_maps = []
    for c in range(NCORES):
        mo, ho = c * MO, c * HO
        in_maps.append({
            "x_in": np.ascontiguousarray(x[c * T:(c + 1) * T]),
            "gq": np.ascontiguousarray(gq[mo:mo + MO]),
            "uq": np.ascontiguousarray(uq[mo:mo + MO]),
            "dq": np.ascontiguousarray(dq[ho:ho + HO]),
            "ga": np.ascontiguousarray(ga[mo:mo + MO]),
            "ua": np.ascontiguousarray(ua[mo:mo + MO]),
            "da": np.ascontiguousarray(da[ho:ho + HO]),
            "win": np.asarray(inputs["L0_attn_Win"], f32),
            "w1": np.asarray(inputs["L0_attn_W1"], f32),
            "w2": np.asarray(inputs["L0_attn_W2"], f32),
            "wout": np.asarray(inputs["L0_attn_Wout"], f32),
            "prm_in": _pack_params(inputs, c),
            "bout_rep": bout_rep,
        })
    res = run_bass_kernel_spmd(nc, in_maps, core_ids=list(range(NCORES)))
    global _LAST_RES
    _LAST_RES = res
    out = np.concatenate([res.results[c]["out"] for c in range(NCORES)], axis=0)
    return out.reshape(1, 8192, D).astype(np.float32)


# version 15 (final)
# speedup vs baseline: 1.0840x; 1.0255x over previous
import numpy as np
import ml_dtypes

import concourse.bass as bass
import concourse.mybir as mybir
from concourse.bass_utils import run_bass_kernel_spmd

F32 = mybir.dt.float32
BF16 = mybir.dt.bfloat16
I32 = mybir.dt.int32
AF = mybir.ActivationFunctionType
OP = mybir.AluOpType
AXX = mybir.AxisListType.X

NCORES = 8
T = 1024
TH = 512
D = 4096
M = 14336
DA = 512
DFF = 2048
NL = 4
EPS = 1e-5

MC = M // 128
VC = D // 128
DC = DA // 128
FC = DFF // 128
MO = M // NCORES
HO = D // NCORES
MSUB = 3584
NSUB = M // MSUB

NF4 = [-1.0, -0.6961928009986877, -0.5250730514526367, -0.39491748809814453,
       -0.28444138169288635, -0.18477343022823334, -0.09105003625154495, 0.0,
       0.07958029955625534, 0.16093020141124725, 0.24611230194568634,
       0.33791524171829224, 0.44070982933044434, 0.5626170039176941,
       0.7229568362236023, 1.0]
NF_A = NF4[0]
NF_B = NF4[1] - NF4[0]
NF_G = [NF4[t + 1] - 2 * NF4[t] + NF4[t - 1] for t in range(1, 15)]

_LAST_RES = None
DEBUG = False

PQ = ["ln1_w", "ln1_b", "ln2_w", "ln2_b", "sc_lin",
      "phz_re", "phz_im", "lc_re", "lc_im", "b2"]
PCMAP = {}
_col = 0
for _l in range(NL):
    for _nm in PQ:
        PCMAP[(_nm, _l)] = _col
        _col += 4
for _l in range(NL):
    PCMAP[("b1", _l)] = _col
    _col += 16
for _nm in ["bin", "lnl_w", "lnl_b"]:
    PCMAP[(_nm, 0)] = _col
    _col += 4
PCMAP[("cmask", 0)] = _col
_col += 8
PCMAP[("negt", 0)] = _col
_col += 14
PCMAP[("cmask8", 0)] = _col
_col += 64
PC_DATA = _col
PC_SCR = PC_DATA
PCOLS = PC_DATA + 76

AR = 102400
A_XFM = 0
A_GB = 16384
A_UB = 24576
A_HACT = 32768
A_DBUF = 16384
A_HSIL = 98816
F_XS = 0
F_OT = 2048
F_LNX = 16384
F_LNSQ = 20480
A_LNB = 49152
F_DQ_QI = 0
F_DQ_QF = 4096
F_DQ_RL = 8192
F_DQ_AC = 12288
F_DQ_AM = 16384
A_DQ_WB = 40960
A_DQ_ST = 49152
A_DQ_ST2 = 61440
F_H = 0
F_Y = 4096
F_U = 16384
F_V = 20480
F_C = 24576
F_S = 28672
F_P = 32768
F_RHO = 37120
F_TMD = 38144
F_LN2 = 7168
A_UMLP = 78336
A_WO = 78336
A_XFM2 = 78336
A_Y2 = 94720
A_HLN = 94720
A_WSTR = 32768
A_WIN = 65536
F_STAT = 39168
F_MTMP = 14336
F_BOUT = 14336
F_X1S = 14848
F_WTMP = 15360


class KB:
    def __init__(self):
        nc = bass.Bass("TRN2", target_bir_lowering=False)
        self.nc = nc
        self.qs_n = 0
        self.qa_n = 0
        self.qg_n = 0

        dt = nc.dram_tensor
        self.x_in = dt("x_in", [T, D], F32, kind="ExternalInput")
        self.gq = dt("gq", [MO, D], I32, kind="ExternalInput")
        self.uq = dt("uq", [MO, D], I32, kind="ExternalInput")
        self.dq = dt("dq", [HO, M], I32, kind="ExternalInput")
        self.ga = dt("ga", [MO, 64], F32, kind="ExternalInput")
        self.ua = dt("ua", [MO, 64], F32, kind="ExternalInput")
        self.da = dt("da", [HO, M // 64], F32, kind="ExternalInput")
        self.win = dt("win", [DA, D], F32, kind="ExternalInput")
        self.w1 = dt("w1", [NL, DFF, DA], F32, kind="ExternalInput")
        self.w2 = dt("w2", [NL, DA, DFF], F32, kind="ExternalInput")
        self.wout = dt("wout", [D, DA], F32, kind="ExternalInput")
        self.prm_in = dt("prm_in", [128, PC_DATA], F32, kind="ExternalInput")
        self.bout_rep = dt("bout_rep", [128, D], F32, kind="ExternalInput")
        self.out = dt("out", [T, D], F32, kind="ExternalOutput")

        self.wg_own = dt("wg_own", [MO, D], BF16)
        self.wu_own = dt("wu_own", [MO, D], BF16)
        self.wd_own = dt("wd_own", [4 * NSUB * 128, MSUB], BF16)
        self.wg_gath = dt("wg_gath", [M, D], BF16)
        self.wu_gath = dt("wu_gath", [M, D], BF16)
        self.wd_gath = dt("wd_gath", [NCORES * 4 * NSUB * 128, MSUB], BF16)
        self.winT = dt("winT", [DC * 128, D], BF16)
        self.w1T = dt("w1T", [NL * 128, FC * DC * 128], BF16)
        self.w2T = dt("w2T", [NL * 128, FC * DC * 128], BF16)
        self.woutT = dt("woutT", [DC * 128, D], BF16)
        self.x1b = dt("x1b", [T, D], F32)
        self.x2b = dt("x2b", [T, D], F32)
        self.dbg1 = dt("dbg1", [T, D], F32, kind="ExternalOutput") if DEBUG else None
        self.dbg2 = dt("dbg2", [T, D], F32, kind="ExternalOutput") if DEBUG else None
        self.dbgh = dt("dbgh", [8, DA, T], F32, kind="ExternalOutput") if DEBUG else None
        self.dbgp = dt("dbgp", [128, 64], F32, kind="ExternalOutput") if DEBUG else None
        self.cc_in = [dt(f"cc_in{l}", [1, 1024], F32) for l in range(NL)]
        self.cc_out = [dt(f"cc_out{l}", [NCORES, 1024], F32) for l in range(NL)]

        self.ident_b = nc.alloc_sbuf_tensor("ident_b", [128, 128], BF16)
        self.ident_f = nc.alloc_sbuf_tensor("ident_f", [128, 128], F32)
        self.ones_f = nc.alloc_sbuf_tensor("ones_f", [128, 1], F32)
        self.ones_r = nc.alloc_sbuf_tensor("ones_r", [1, 128], F32)
        self.prm = nc.alloc_sbuf_tensor("prm", [128, PCOLS], F32)
        self.arena = nc.alloc_sbuf_tensor("arena", [128, AR], BF16)
        self.ar_b = self.arena.ap()
        self.ar_f = self.arena.ap().bitcast(F32)

        ps = nc.alloc_psum_tensor
        self.ps_a = ps("ps_a", [128, 512], F32)
        self.ps_b = ps("ps_b", [128, 512], F32)
        self.ps_d = [ps(f"ps_d{i}", [128, 512], F32) for i in range(4)]
        self.ps_t = ps("ps_t", [128, 1024], BF16)
        self.ps_m = ps("ps_m", [128, 512], F32)

        self.qs = nc.alloc_semaphore(name="qs")
        self.qa = nc.alloc_semaphore(name="qa")
        self.qg = nc.alloc_semaphore(name="qg")
        self.sc_w = [nc.alloc_semaphore(name=f"scw{i}") for i in range(3)]
        self.sc_l = [nc.alloc_semaphore(name=f"scl{i}") for i in range(NL)]
        self.pe_s = nc.alloc_semaphore(name="pe_s")
        self.ae_s = nc.alloc_semaphore(name="ae_s")
        self.de_s = nc.alloc_semaphore(name="de_s")
        self.pe_n = 0
        self.ae_n = 0
        self.de_n = 0

        ET = mybir.EngineType
        self._bar_engines = [ET.PE, ET.Activation, ET.DVE, ET.SP]
        self._vbar_engines = [ET.Activation, ET.DVE]
        self.st_s = nc.alloc_semaphore(name="st_s")
        self.st_n = 0
        self.pt_s = nc.alloc_semaphore(name="pt_s")
        self.ev_s = nc.alloc_semaphore(name="ev_s")
        self.pt_n = 0
        self.ev_n = 0
        self.c_ident_b = nc.inline_tensor(
            np.eye(128, dtype=np.float32).astype(ml_dtypes.bfloat16), name="cib")
        self.c_ident_f = nc.inline_tensor(np.eye(128, dtype=np.float32), name="cif")

    def bar(self):
        self.nc.multi_engine_barrier(self._bar_engines)

    def allbar(self):
        self.nc.all_engine_barrier()

    def vbar(self):
        self.nc.multi_engine_barrier(self._vbar_engines)

    def ld(self, out, in_):
        self.nc.sync.dma_start(out=out, in_=in_).then_inc(self.qs, 16)
        self.qs_n += 16

    def ld_wait(self):
        self.nc.sync.wait_ge(self.qs, self.qs_n)

    def st(self, out, in_):
        self.nc.scalar.dma_start(out=out, in_=in_).then_inc(self.qa, 16)
        self.qa_n += 16

    def st_wait(self):
        self.nc.scalar.wait_ge(self.qa, self.qa_n)

    def fv(self, off, cols, rows=128):
        return self.ar_f[0:rows, off:off + cols]

    def bv(self, off, cols):
        return self.ar_b[:, off:off + cols]

    def pcol(self, nm, l=0, tile=0, n=1):
        c = PCMAP[(nm, l)] + tile
        return self.prm.ap()[:, c:c + n]

    def scol(self, i, n=1):
        c = PC_SCR + i
        return self.prm.ap()[:, c:c + n]

    def prep_consts(self):
        nc = self.nc
        self.ld(self.ident_b.ap(), self.c_ident_b.ap())
        self.ld(self.ident_f.ap(), self.c_ident_f.ap())
        self.ld(self.prm.ap()[:, 0:PC_DATA], self.prm_in.ap())
        nc.vector.memset(self.ones_f.ap(), 1.0)
        nc.vector.memset(self.ones_r.ap(), 1.0)
        self.eps_c = self.scol(70)
        self.mhalf_c = self.scol(71)
        nc.vector.memset(self.eps_c, EPS)
        nc.vector.memset(self.mhalf_c, -0.5)
        self.ld_wait()
        self.bar()

    def dq_ld(self, k, q_src, am_src, fcols):
        b = k % 2
        qi = self.ar_f[:, b * 4096:b * 4096 + fcols].bitcast(I32)
        amax = self.fv(22528 + b * 64, fcols // 64)
        self.ld(qi, q_src)
        self.ld(amax, am_src)
        return self.qs_n

    def dequant_tile(self, k, fcols, qs_need):
        nc = self.nc
        b = k % 2
        qi = self.ar_f[:, b * 4096:b * 4096 + fcols].bitcast(I32)
        qf = self.fv(8192, fcols)
        rl = [self.fv(12288, fcols), self.fv(26624, fcols)]
        acc = self.fv(16384, fcols)
        nb = fcols // 64
        amax = self.fv(22528 + b * 64, nb)
        wb = self.bv(A_DQ_WB, fcols)
        ae0, de0 = self.ae_n, self.de_n
        nc.vector.wait_ge(self.qs, qs_need)
        nc.vector.tensor_copy(out=qf, in_=qi).then_inc(self.de_s, 1)
        self.de_n += 1
        nc.vector.tensor_scalar(out=acc, in0=qf, scalar1=NF_B, scalar2=NF_A,
                                op0=OP.mult, op1=OP.add)
        nc.scalar.wait_ge(self.de_s, de0 + 1)
        for t in range(1, 15):
            rb = t % 2
            if t > 2:
                nc.scalar.wait_ge(self.de_s, de0 + 1 + (t - 2))
            nc.scalar.activation(out=rl[rb], in_=qf, func=AF.Relu,
                                 bias=self.pcol("negt", 0, t - 1),
                                 scale=1.0).then_inc(self.ae_s, 1)
            self.ae_n += 1
            nc.vector.wait_ge(self.ae_s, ae0 + t)
            op_ = nc.vector.scalar_tensor_tensor(
                out=acc, in0=rl[rb], scalar=NF_G[t - 1], in1=acc,
                op0=OP.mult, op1=OP.add)
            op_.then_inc(self.de_s, 1)
            self.de_n += 1
        amv = amax.unsqueeze(2).broadcast_to([128, nb, 64])
        accv = acc.rearrange("p (nb b) -> p nb b", b=64)
        wv = wb.rearrange("p (nb b) -> p nb b", b=64)
        nc.vector.tensor_tensor(out=wv, in0=accv, in1=amv, op=OP.mult)
        self.bar()
        return wb

    def transpose_pipe(self, srcs, writer):
        nc = self.nc
        banks = [self.ps_t.ap(), self.ps_d[2].ap().bitcast(BF16)]
        pt0, ev0 = self.pt_n, self.ev_n
        ngroups = (len(srcs) + 7) // 8
        for g in range(ngroups):
            g0 = g * 8
            kn = min(8, len(srcs) - g0)
            bk = banks[g % 2]
            if g >= 2:
                nc.tensor.wait_ge(self.ev_s, ev0 + g - 1)
            for i in range(kn):
                mm = nc.tensor.transpose(bk[:, i * 128:(i + 1) * 128],
                                         srcs[g0 + i], self.ident_b.ap())
                if i == kn - 1:
                    mm.then_inc(self.pt_s, 1)
            self.pt_n += 1
            nc.vector.wait_ge(self.pt_s, pt0 + g + 1)
            last = writer(g0, kn, bk)
            last.then_inc(self.ev_s, 1)
            self.ev_n += 1

    def transpose_wait(self):
        self.nc.scalar.wait_ge(self.ev_s, self.ev_n)
        self.bar()

    def transpose_to_strip(self, src_b, strip, nk):
        srcs = [src_b[:, k * 128:(k + 1) * 128] for k in range(nk)]

        def wr(g0, kn, bk):
            return self.nc.vector.tensor_copy(
                out=strip[:, g0 * 128:(g0 + kn) * 128], in_=bk[:, 0:kn * 128])
        self.transpose_pipe(srcs, wr)
        self.transpose_wait()

    def prep_ffn_weights(self):
        nc = self.nc
        strips = [self.bv(A_DQ_ST, 4096), self.bv(A_DQ_ST2, 4096)]
        tiles = []
        for (qsrc, asrc, own) in [(self.gq, self.ga, self.wg_own),
                                  (self.uq, self.ua, self.wu_own)]:
            ownv = own.ap().rearrange("(s p) f -> p s f", p=128)
            for mc in range(MO // 128):
                tiles.append((qsrc.ap()[mc * 128:(mc + 1) * 128, :],
                              asrc.ap()[mc * 128:(mc + 1) * 128, :],
                              D, ownv[:, mc, :], 32))
        wdv = self.wd_own.ap().rearrange("(s p) f -> p s f", p=128)
        for hc in range(4):
            for ms in range(NSUB):
                tiles.append((
                    self.dq.ap()[hc * 128:(hc + 1) * 128,
                                 ms * MSUB:(ms + 1) * MSUB],
                    self.da.ap()[hc * 128:(hc + 1) * 128,
                                 ms * (MSUB // 64):(ms + 1) * (MSUB // 64)],
                    MSUB, wdv[:, ms * 4 + hc, :], 28))
        ags = {13: 0, 27: 1, 43: 2}
        ag_args = [(self.wg_own, self.wg_gath, self.sc_w[0]),
                   (self.wu_own, self.wu_gath, self.sc_w[1]),
                   (self.wd_own, self.wd_gath, self.sc_w[2])]
        qs_need = self.dq_ld(0, *tiles[0][:3])
        store_qa = {}
        for k, (qsrc, asrc, fcols, dst, nk) in enumerate(tiles):
            if k + 1 < len(tiles):
                nxt = self.dq_ld(k + 1, *tiles[k + 1][:3])
            wb = self.dequant_tile(k, fcols, qs_need)
            qs_need = nxt if k + 1 < len(tiles) else qs_need
            if k >= 2:
                nc.vector.wait_ge(self.qa, store_qa[k - 2])
            self.transpose_to_strip(wb, strips[k % 2], nk)
            self.st(dst, strips[k % 2][:, 0:fcols])
            store_qa[k] = self.qa_n
            self.bar()
            if k in ags:
                own, gath, sem = ag_args[ags[k]]
                nc.gpsimd.wait_ge(self.qa, self.qa_n)
                nc.gpsimd.collective_compute(
                    "AllGather", OP.bypass,
                    replica_groups=[list(range(NCORES))],
                    ins=[own.ap().opt()], outs=[gath.ap().opt()],
                ).then_inc(sem)
        self.st_wait()
        self.bar()

    def prep_attn_weights(self):
        nc = self.nc
        src16 = self.fv(0, 16384)
        cast16 = self.bv(32768, 16384)
        strip8 = self.bv(49152, 8192)
        wstr = [self.bv(57344 + i * 4096, 4096) for i in range(4)]

        def group_transpose(cast_v, pairs, dst_writer):
            srcs = [cast_v[:, c:c + 128] for c in pairs]
            self.transpose_pipe(srcs, dst_writer)
            self.transpose_wait()

        self.ld(src16.rearrange("p (dc v) -> p dc v", dc=4),
                self.win.ap().rearrange("(dc p) v -> p dc v", p=128))
        self.ld_wait()
        self.bar()
        nc.vector.tensor_copy(out=cast16, in_=src16)
        self.bar()
        winTv = self.winT.ap().rearrange("(s p) f -> p s f", p=128)
        for dc in range(DC):
            pairs = [dc * 4096 + vc * 128 for vc in range(VC)]

            def wr(g0, kn, bk, dc=dc):
                return nc.vector.tensor_copy(
                    out=wstr[dc][:, g0 * 128:(g0 + kn) * 128],
                    in_=bk[:, 0:kn * 128])
            group_transpose(cast16, pairs, wr)
        for dc in range(DC):
            self.st(winTv[:, dc, :], wstr[dc])
        self.st_wait()
        self.bar()
        w1Tv = self.w1T.ap().rearrange("(s p) f -> p s f", p=128)
        w2Tv = self.w2T.ap().rearrange("(s p) f -> p s f", p=128)
        for l in range(NL):
            self.ld(src16[:, 0:8192].rearrange("p (fc d) -> p fc d", fc=FC),
                    self.w1.ap()[l].rearrange("(fc p) d -> p fc d", p=128))
            self.ld_wait()
            self.bar()
            nc.vector.tensor_copy(out=cast16[:, 0:8192], in_=src16[:, 0:8192])
            self.bar()
            pairs = [fc * 512 + dc * 128 for fc in range(FC) for dc in range(DC)]

            def wr1(g0, kn, bk):
                return nc.vector.tensor_copy(
                    out=strip8[:, g0 * 128:(g0 + kn) * 128],
                    in_=bk[:, 0:kn * 128])
            group_transpose(cast16, pairs, wr1)
            self.st(w1Tv[:, l, :], strip8)
            self.st_wait()
            self.bar()
            self.ld(src16[:, 0:8192].rearrange("p (dc f) -> p dc f", dc=DC),
                    self.w2.ap()[l].rearrange("(dc p) f -> p dc f", p=128))
            self.ld_wait()
            self.bar()
            nc.vector.tensor_copy(out=cast16[:, 0:8192], in_=src16[:, 0:8192])
            self.bar()
            pairs = [dc * 2048 + fc * 128 for dc in range(DC) for fc in range(FC)]
            group_transpose(cast16, pairs, wr1)
            self.st(w2Tv[:, l, :], strip8)
            self.st_wait()
            self.bar()
        self.ld(src16.rearrange("p (vc d) -> p vc d", vc=VC),
                self.wout.ap().rearrange("(vc p) d -> p vc d", p=128))
        self.ld_wait()
        self.bar()
        nc.vector.tensor_copy(out=cast16, in_=src16)
        self.bar()
        woutTv = self.woutT.ap().rearrange("(s p) f -> p s f", p=128)
        pairs = [vc * 512 + dc * 128 for vc in range(VC) for dc in range(DC)]

        def wrout(g0, kn, bk):
            vc0 = g0 // 4
            last = None
            for dc in range(DC):
                sv = bk.rearrange("p (v d f) -> p v d f",
                                  v=2, d=4)[:, 0:kn // 4, dc, :]
                dv = wstr[dc][:, vc0 * 128:(vc0 + kn // 4) * 128]
                last = nc.vector.tensor_copy(out=dv, in_=sv)
            return last
        group_transpose(cast16, pairs, wrout)
        for dc in range(DC):
            self.st(woutTv[:, dc, :], wstr[dc])
        self.st_wait()
        self.bar()

    def ln_big_half(self, src, th, xfm):
        nc = self.nc
        xt = [self.fv(F_LNX + tt * 4096, 4096) for tt in range(4)]
        sq = self.fv(32768, 4096)
        xln = self.bv(73728, 4096)
        ssum = self.scol(0, 4)
        nmu = self.scol(4, 4)
        msq = self.scol(8, 4)
        rstd = self.scol(12, 4)
        for tt in range(4):
            r0 = th * TH + tt * 128
            self.ld(xt[tt], src.ap()[r0:r0 + 128, :])
        self.ld_wait()
        self.bar()
        for tt in range(4):
            nc.vector.tensor_reduce(out=ssum[:, tt:tt + 1], in_=xt[tt],
                                    axis=AXX, op=OP.add)
            nc.scalar.activation(out=sq, in_=xt[tt], func=AF.Square)
            self.vbar()
            nc.vector.tensor_reduce(out=msq[:, tt:tt + 1], in_=sq,
                                    axis=AXX, op=OP.add)
            self.vbar()
        nc.vector.tensor_scalar(out=nmu, in0=ssum, scalar1=-1.0 / D,
                                scalar2=None, op0=OP.mult)
        nc.vector.tensor_scalar(out=msq, in0=msq, scalar1=1.0 / D,
                                scalar2=None, op0=OP.mult)
        self.vbar()
        nc.scalar.activation(out=rstd, in_=nmu, func=AF.Square)
        self.vbar()
        nc.vector.tensor_tensor(out=msq, in0=msq, in1=rstd, op=OP.subtract)
        self.vbar()
        nc.scalar.activation(out=msq, in_=msq, func=AF.Ln,
                             bias=self.eps_c, scale=1.0)
        self.vbar()
        nc.scalar.activation(out=rstd, in_=msq, func=AF.Exp,
                             scale=self.mhalf_c)
        self.vbar()
        for tt in range(4):
            nc.vector.tensor_scalar(out=xln, in0=xt[tt],
                                    scalar1=nmu[:, tt:tt + 1],
                                    scalar2=rstd[:, tt:tt + 1],
                                    op0=OP.add, op1=OP.mult)
            self.bar()
            srcs = [xln[:, k * 128:(k + 1) * 128] for k in range(32)]

            def wrln(g0, kn, bk, tt=tt):
                dstv = xfm.rearrange("p (v t) -> p v t", t=512)[
                    :, g0:g0 + kn, tt * 128:(tt + 1) * 128]
                srcv = bk.rearrange("p (v t) -> p v t", t=128)[:, 0:kn, :]
                return nc.vector.tensor_copy(out=dstv, in_=srcv)
            self.transpose_pipe(srcs, wrln)
            self.transpose_wait()

    def ffn(self, src, dst):
        nc = self.nc
        nc.sync.wait_ge(self.sc_w[0], 1)
        nc.sync.wait_ge(self.sc_w[1], 1)
        self.bar()
        xfm = self.bv(A_XFM, 16384)
        gbuf = [self.bv(A_GB + i * 4096, 4096) for i in range(2)]
        ubuf = [self.bv(A_UB + i * 4096, 4096) for i in range(2)]
        hact = self.bv(A_HACT, MC * 512)
        dbuf = [self.bv(0, 4 * MSUB), self.bv(16384, 4 * MSUB)]
        hsil = [self.bv(A_HSIL + i * 512, 512) for i in range(2)]
        wgv = self.wg_gath.ap().rearrange("(s p) f -> p s f", p=128)
        wuv = self.wu_gath.ap().rearrange("(s p) f -> p s f", p=128)
        wdv = self.wd_gath.ap().rearrange("(s p) f -> p s f", p=128)
        xs = self.fv(45056, 2048)
        ot = self.fv(47104, 2048)
        ps_g = [self.ps_a, self.ps_d[0]]
        ps_u = [self.ps_b, self.ps_d[1]]
        for th in range(2):
            self.ln_big_half(src, th, xfm)
            pe0, ae0, de0 = self.pe_n, self.ae_n, self.de_n
            qs_after = {}
            for mg in range(MC):
                b = mg % 2
                if mg >= 2:
                    nc.sync.wait_ge(self.pe_s, pe0 + 2 * (mg - 2) + 2)
                self.ld(gbuf[b], wgv[:, mg, :])
                self.ld(ubuf[b], wuv[:, mg, :])
                qs_after[mg] = self.qs_n
                nc.tensor.wait_ge(self.qs, qs_after[mg])
                if mg >= 2:
                    nc.tensor.wait_ge(self.ae_s, ae0 + (mg - 2) + 1)
                    nc.tensor.wait_ge(self.de_s, de0 + (mg - 2) + 1)
                for k in range(VC):
                    mm = nc.tensor.matmul(ps_g[b].ap(),
                                          gbuf[b][:, k * 128:(k + 1) * 128],
                                          xfm[:, k * 512:(k + 1) * 512],
                                          start=(k == 0), stop=(k == VC - 1))
                    if k == VC - 1:
                        mm.then_inc(self.pe_s, 1)
                self.pe_n += 1
                for k in range(VC):
                    mm = nc.tensor.matmul(ps_u[b].ap(),
                                          ubuf[b][:, k * 128:(k + 1) * 128],
                                          xfm[:, k * 512:(k + 1) * 512],
                                          start=(k == 0), stop=(k == VC - 1))
                    if k == VC - 1:
                        mm.then_inc(self.pe_s, 1)
                self.pe_n += 1
                nc.scalar.wait_ge(self.pe_s, pe0 + 2 * mg + 1)
                if mg >= 2:
                    nc.scalar.wait_ge(self.de_s, de0 + (mg - 2) + 1)
                nc.scalar.activation(out=hsil[b], in_=ps_g[b].ap(),
                                     func=AF.Silu).then_inc(self.ae_s, 1)
                self.ae_n += 1
                nc.vector.wait_ge(self.pe_s, pe0 + 2 * mg + 2)
                nc.vector.wait_ge(self.ae_s, ae0 + mg + 1)
                nc.vector.tensor_tensor(
                    out=hact[:, mg * 512:(mg + 1) * 512], in0=hsil[b],
                    in1=ps_u[b].ap(), op=OP.mult).then_inc(self.de_s, 1)
                self.de_n += 1
            nc.sync.wait_ge(self.de_s, de0 + MC)
            if th == 0:
                nc.sync.wait_ge(self.sc_w[2], 1)
            self.bar()
            pe0, de0 = self.pe_n, self.de_n
            qs_xs = {}
            for hb in range(NCORES):
                for ms in range(NSUB):
                    j = hb * NSUB + ms
                    b = j % 2
                    if j >= 2:
                        nc.sync.wait_ge(self.pe_s, pe0 + j - 1)
                    blk0 = hb * 16 + ms * 4
                    self.ld(dbuf[b].rearrange("p (r f) -> p r f", r=4),
                            wdv[:, blk0:blk0 + 4, :])
                    qs_after[j] = self.qs_n
                    if ms == NSUB - 1:
                        if hb > 0:
                            nc.sync.wait_ge(self.de_s, de0 + hb)
                        for tc in range(4):
                            r0 = th * TH + tc * 128
                            self.ld(xs[:, tc * 512:(tc + 1) * 512],
                                    src.ap()[r0:r0 + 128,
                                             hb * 512:(hb + 1) * 512])
                        qs_xs[hb] = self.qs_n
                    nc.tensor.wait_ge(self.qs, qs_after[j])
                    if ms == 0 and hb > 0:
                        nc.tensor.wait_ge(self.de_s, de0 + hb)
                    drv = dbuf[b].rearrange("p (r f) -> p r f", r=4)
                    for k in range(28):
                        mg = ms * 28 + k
                        for tc in range(4):
                            mm = nc.tensor.matmul(
                                self.ps_d[tc].ap(),
                                hact[:, mg * 512 + tc * 128:
                                     mg * 512 + (tc + 1) * 128],
                                drv[:, :, k * 128:(k + 1) * 128],
                                start=(ms == 0 and k == 0),
                                stop=(ms == NSUB - 1 and k == 27))
                            if k == 27 and tc == 3:
                                mm.then_inc(self.pe_s, 1)
                    self.pe_n += 1
                nc.vector.wait_ge(self.pe_s, pe0 + NSUB * (hb + 1))
                nc.vector.wait_ge(self.qs, qs_xs[hb])
                if hb > 0:
                    nc.vector.wait_ge(self.qa, self.qa_n)
                for tc in range(4):
                    op_ = nc.vector.scalar_tensor_tensor(
                        out=ot[:, tc * 512:(tc + 1) * 512],
                        in0=self.ps_d[tc].ap(), scalar=0.5,
                        in1=xs[:, tc * 512:(tc + 1) * 512],
                        op0=OP.mult, op1=OP.add)
                    if tc == 3:
                        op_.then_inc(self.de_s, 1)
                self.de_n += 1
                nc.scalar.wait_ge(self.de_s, de0 + hb + 1)
                for tc in range(4):
                    r0 = th * TH + tc * 128
                    self.st(dst.ap()[r0:r0 + 128, hb * 512:(hb + 1) * 512],
                            ot[:, tc * 512:(tc + 1) * 512])
            self.st_wait()
            self.bar()

    def ln_inner(self, h_tiles, w_nm, b_nm, l, out_tiles, silu):
        nc = self.nc
        sq = self.fv(F_MTMP, 1024)
        for half in range(2):
            tgt = self.ps_d[half].ap()[0:1, :]
            for ct in range(4):
                nc.tensor.matmul(tgt, self.ones_f.ap(),
                                 h_tiles[ct][:, half * 512:(half + 1) * 512],
                                 start=(ct == 0), stop=(ct == 3))
        self.bar()
        for ct in range(4):
            nc.scalar.activation(out=sq, in_=h_tiles[ct], func=AF.Square)
            self.bar()
            for half in range(2):
                tgt = self.ps_d[2 + half].ap()[0:1, :]
                nc.tensor.matmul(tgt, self.ones_f.ap(),
                                 sq[:, half * 512:(half + 1) * 512],
                                 start=(ct == 0), stop=(ct == 3))
            self.bar()
        st_ = self.fv(F_STAT, 2048, rows=1)
        for half in range(2):
            nc.vector.tensor_copy(out=st_[:, half * 512:(half + 1) * 512],
                                  in_=self.ps_d[half].ap()[0:1, :])
            nc.vector.tensor_copy(out=st_[:, 1024 + half * 512:1024 + (half + 1) * 512],
                                  in_=self.ps_d[2 + half].ap()[0:1, :])
        nc.vector.tensor_scalar(out=st_[:, 0:1024], in0=st_[:, 0:1024],
                                scalar1=-1.0 / DA, scalar2=None, op0=OP.mult)
        nc.vector.tensor_scalar(out=st_[:, 1024:2048], in0=st_[:, 1024:2048],
                                scalar1=1.0 / DA, scalar2=None, op0=OP.mult)
        self.bar()
        musq = self.fv(F_MTMP, 1024, rows=1)
        nc.scalar.activation(out=musq, in_=st_[:, 0:1024], func=AF.Square)
        self.bar()
        nc.vector.tensor_tensor(out=st_[:, 1024:2048], in0=st_[:, 1024:2048],
                                in1=musq, op=OP.subtract)
        self.bar()
        nc.scalar.activation(out=st_[:, 1024:2048], in_=st_[:, 1024:2048],
                             func=AF.Ln, bias=self.eps_c[0:1, :], scale=1.0)
        self.bar()
        nc.scalar.activation(out=st_[:, 1024:2048], in_=st_[:, 1024:2048],
                             func=AF.Exp, scale=self.mhalf_c[0:1, :])
        self.bar()
        for half in range(2):
            nc.tensor.matmul(self.ps_d[half].ap(), self.ones_r.ap(),
                             st_[:, half * 512:(half + 1) * 512],
                             start=True, stop=True)
            nc.tensor.matmul(self.ps_d[2 + half].ap(), self.ones_r.ap(),
                             st_[:, 1024 + half * 512:1024 + (half + 1) * 512],
                             start=True, stop=True)
        self.bar()
        tmp = self.fv(F_MTMP, 1024)
        for ct in range(4):
            for half in range(2):
                s0 = half * 512
                nc.vector.tensor_tensor(out=tmp[:, s0:s0 + 512],
                                        in0=h_tiles[ct][:, s0:s0 + 512],
                                        in1=self.ps_d[half].ap(), op=OP.add)
                nc.vector.tensor_tensor(out=tmp[:, s0:s0 + 512],
                                        in0=tmp[:, s0:s0 + 512],
                                        in1=self.ps_d[2 + half].ap(), op=OP.mult)
            wc = self.pcol(w_nm, l, ct)
            bc = self.pcol(b_nm, l, ct)
            if silu:
                nc.vector.tensor_scalar(out=tmp, in0=tmp, scalar1=wc, scalar2=bc,
                                        op0=OP.mult, op1=OP.add)
                self.bar()
                nc.scalar.activation(out=out_tiles[ct], in_=tmp, func=AF.Silu)
            else:
                nc.vector.tensor_scalar(out=out_tiles[ct], in0=tmp, scalar1=wc,
                                        scalar2=bc, op0=OP.mult, op1=OP.add)
            self.bar()

    def sconv(self, l, h_tiles, y_tiles):
        nc = self.nc
        u_t = [self.fv(F_U + ct * 1024, 1024) for ct in range(4)]
        v_t = [self.fv(F_V + ct * 1024, 1024) for ct in range(4)]
        C_t = [self.fv(F_C + ct * 1024, 1024) for ct in range(4)]
        S_t = [self.fv(F_S + ct * 1024, 1024) for ct in range(4)]
        P_t = [self.fv(F_P + ct * 1056, 1056) for ct in range(4)]
        rho_t = self.fv(F_RHO, 1024)
        tmod = self.fv(F_TMD, 1024)
        CTH = self.scol(0, 4)
        STH = self.scol(4, 4)
        RH = self.scol(8, 4)
        T0 = self.scol(12, 4)
        T1 = self.scol(16, 4)
        T2 = self.scol(20, 4)
        stage = self.scol(24, 8)
        R1C = self.scol(32, 4)
        R1S = self.scol(36, 4)
        BC = self.scol(40, 4)
        BS = self.scol(44, 4)
        UI0 = self.scol(48, 4)
        VI0 = self.scol(52, 4)
        UIN = self.scol(56, 4)
        VIN = self.scol(60, 4)
        pr4 = self.pcol("phz_re", l, 0, 4)
        pi4 = self.pcol("phz_im", l, 0, 4)
        lcr4 = self.pcol("lc_re", l, 0, 4)
        lci4 = self.pcol("lc_im", l, 0, 4)
        cmask = self.pcol("cmask", 0, 0, 8)
        nc.scalar.activation(out=T0, in_=pr4, func=AF.Square)
        nc.scalar.activation(out=T1, in_=pi4, func=AF.Square)
        self.vbar()
        nc.vector.tensor_tensor(out=T0, in0=T0, in1=T1, op=OP.add)
        self.vbar()
        nc.scalar.activation(out=T1, in_=T0, func=AF.Sqrt)
        self.vbar()
        nc.vector.reciprocal(out=T2, in_=T1)
        self.vbar()
        nc.vector.tensor_tensor(out=T2, in0=T0, in1=T2, op=OP.mult)
        self.vbar()
        nc.vector.tensor_tensor(out=T1, in0=T1, in1=T2, op=OP.add)
        self.vbar()
        nc.vector.tensor_scalar(out=T1, in0=T1, scalar1=0.5, scalar2=None,
                                op0=OP.mult)
        self.vbar()
        nc.vector.reciprocal(out=T2, in_=T1)
        nc.scalar.activation(out=RH, in_=T1, func=AF.Tanh)
        self.vbar()
        nc.vector.tensor_tensor(out=CTH, in0=pr4, in1=T2, op=OP.mult)
        nc.vector.tensor_tensor(out=STH, in0=pi4, in1=T2, op=OP.mult)
        self.vbar()
        nc.vector.tensor_tensor(out=T0, in0=STH, in1=lci4, op=OP.mult)
        nc.vector.tensor_tensor(out=T1, in0=CTH, in1=lci4, op=OP.mult)
        self.vbar()
        nc.vector.tensor_tensor(out=UI0, in0=CTH, in1=lcr4, op=OP.mult)
        nc.vector.tensor_tensor(out=VI0, in0=STH, in1=lcr4, op=OP.mult)
        self.vbar()
        nc.vector.tensor_tensor(out=UI0, in0=UI0, in1=T0, op=OP.subtract)
        nc.vector.tensor_tensor(out=VI0, in0=VI0, in1=T1, op=OP.add)
        self.vbar()
        nc.vector.tensor_scalar(out=VI0, in0=VI0, scalar1=-1.0, scalar2=None,
                                op0=OP.mult)
        for ct in range(4):
            Ct, St, Pt = C_t[ct], S_t[ct], P_t[ct]
            nc.vector.memset(Ct[:, 0:1], 1.0)
            nc.vector.memset(St[:, 0:1], 0.0)
            nc.vector.memset(Pt[:, 0:1], 1.0)
        self.vbar()
        for ct in range(4):
            Ct, St, Pt = C_t[ct], S_t[ct], P_t[ct]
            nc.vector.tensor_copy(out=Ct[:, 1:2], in_=CTH[:, ct:ct + 1])
            nc.vector.tensor_copy(out=St[:, 1:2], in_=STH[:, ct:ct + 1])
            nc.vector.tensor_copy(out=Pt[:, 1:2], in_=RH[:, ct:ct + 1])
        self.vbar()
        C4 = self.fv(F_C, 4096).rearrange("p (t j) -> p t j", t=4)
        S4 = self.fv(F_S, 4096).rearrange("p (t j) -> p t j", t=4)
        P4 = self.fv(F_P, 4224).rearrange("p (t j) -> p t j", t=4)
        t4a = self.fv(F_TMD, 4096).rearrange("p (t j) -> p t j", t=4)

        def bcol(V, j, n):
            return V[:, :, j:j + 1].broadcast_to([128, 4, n])
        f = 2
        while f < 1056:
            m = min(f - 1, 1056 - f)
            mcs = min(f - 1, 1024 - f) if f < 1024 else 0
            if mcs > 0:
                nc.vector.tensor_tensor(out=t4a[:, :, 0:mcs],
                                        in0=C4[:, :, 1:1 + mcs],
                                        in1=bcol(S4, f - 1, mcs), op=OP.mult)
                nc.vector.tensor_tensor(out=S4[:, :, f:f + mcs],
                                        in0=S4[:, :, 1:1 + mcs],
                                        in1=bcol(C4, f - 1, mcs), op=OP.mult)
                nc.vector.tensor_tensor(out=t4a[:, :, 512:512 + mcs],
                                        in0=S4[:, :, 1:1 + mcs],
                                        in1=bcol(S4, f - 1, mcs), op=OP.mult)
                nc.vector.tensor_tensor(out=C4[:, :, f:f + mcs],
                                        in0=C4[:, :, 1:1 + mcs],
                                        in1=bcol(C4, f - 1, mcs), op=OP.mult)
                self.vbar()
                nc.vector.tensor_tensor(out=S4[:, :, f:f + mcs],
                                        in0=S4[:, :, f:f + mcs],
                                        in1=t4a[:, :, 0:mcs], op=OP.add)
                nc.vector.tensor_tensor(out=C4[:, :, f:f + mcs],
                                        in0=C4[:, :, f:f + mcs],
                                        in1=t4a[:, :, 512:512 + mcs],
                                        op=OP.subtract)
            nc.vector.tensor_tensor(out=P4[:, :, f:f + m],
                                    in0=P4[:, :, 1:1 + m],
                                    in1=bcol(P4, f - 1, m), op=OP.mult)
            self.vbar()
            f += m
        for ct in range(4):
            Ct, St = C_t[ct], S_t[ct]
            nc.vector.tensor_scalar_mul(out=T0[:, ct:ct + 1],
                                        in0=St[:, 1023:1024],
                                        scalar1=STH[:, ct:ct + 1])
            nc.vector.tensor_scalar_mul(out=T1[:, ct:ct + 1],
                                        in0=Ct[:, 1023:1024],
                                        scalar1=STH[:, ct:ct + 1])
        self.vbar()
        for ct in range(4):
            Ct, St = C_t[ct], S_t[ct]
            nc.vector.scalar_tensor_tensor(out=R1C[:, ct:ct + 1],
                                           in0=Ct[:, 1023:1024],
                                           scalar=CTH[:, ct:ct + 1],
                                           in1=T0[:, ct:ct + 1],
                                           op0=OP.mult, op1=OP.subtract)
            nc.vector.scalar_tensor_tensor(out=R1S[:, ct:ct + 1],
                                           in0=St[:, 1023:1024],
                                           scalar=CTH[:, ct:ct + 1],
                                           in1=T1[:, ct:ct + 1],
                                           op0=OP.mult, op1=OP.add)
        self.vbar()
        pw_c = [self.fv(F_TMD + ct * 16, 8) for ct in range(4)]
        pw_s = [self.fv(F_TMD + ct * 16 + 8, 8) for ct in range(4)]
        for ct in range(4):
            nc.vector.memset(pw_c[ct][:, 0:1], 1.0)
            nc.vector.memset(pw_s[ct][:, 0:1], 0.0)
        self.vbar()
        for k in range(1, 8):
            for ct in range(4):
                nc.vector.tensor_scalar_mul(out=T0[:, ct:ct + 1],
                                            in0=pw_s[ct][:, k - 1:k],
                                            scalar1=R1S[:, ct:ct + 1])
                nc.vector.tensor_scalar_mul(out=T1[:, ct:ct + 1],
                                            in0=pw_c[ct][:, k - 1:k],
                                            scalar1=R1S[:, ct:ct + 1])
            self.vbar()
            for ct in range(4):
                nc.vector.scalar_tensor_tensor(out=pw_c[ct][:, k:k + 1],
                                               in0=pw_c[ct][:, k - 1:k],
                                               scalar=R1C[:, ct:ct + 1],
                                               in1=T0[:, ct:ct + 1],
                                               op0=OP.mult, op1=OP.subtract)
                nc.vector.scalar_tensor_tensor(out=pw_s[ct][:, k:k + 1],
                                               in0=pw_s[ct][:, k - 1:k],
                                               scalar=R1C[:, ct:ct + 1],
                                               in1=T1[:, ct:ct + 1],
                                               op0=OP.mult, op1=OP.add)
            self.vbar()
        cmask8 = self.pcol("cmask8", 0, 0, 64)
        pw_all = self.fv(F_TMD, 64)
        sel64 = self.fv(F_TMD + 64, 64)
        bcs8 = self.fv(F_TMD + 128, 8)
        nc.vector.tensor_tensor(out=sel64, in0=pw_all, in1=cmask8, op=OP.mult)
        self.vbar()
        nc.vector.tensor_reduce(
            out=bcs8.unsqueeze(2),
            in_=sel64.rearrange("p (g k) -> p g k", k=8), axis=AXX, op=OP.add)
        self.vbar()
        for ct in range(4):
            nc.vector.tensor_copy(out=BC[:, ct:ct + 1],
                                  in_=bcs8[:, 2 * ct:2 * ct + 1])
            nc.vector.tensor_copy(out=BS[:, ct:ct + 1],
                                  in_=bcs8[:, 2 * ct + 1:2 * ct + 2])
        self.vbar()
        de0 = self.de_n
        for ct in range(4):
            Ct, St = C_t[ct], S_t[ct]
            bcc = BC[:, ct:ct + 1]
            bsc = BS[:, ct:ct + 1]
            nc.vector.tensor_scalar_mul(out=tmod, in0=St[:, 0:1024],
                                        scalar1=bsc)
            nc.vector.tensor_scalar_mul(out=St[:, 0:1024], in0=St[:, 0:1024],
                                        scalar1=bcc)
            nc.vector.scalar_tensor_tensor(out=St[:, 0:1024],
                                           in0=Ct[:, 0:1024], scalar=bsc,
                                           in1=St[:, 0:1024],
                                           op0=OP.mult, op1=OP.add)
            nc.vector.tensor_scalar_mul(out=Ct[:, 0:1024], in0=Ct[:, 0:1024],
                                        scalar1=bcc)
            nc.vector.tensor_tensor(out=Ct[:, 0:1024], in0=Ct[:, 0:1024],
                                    in1=tmod, op=OP.subtract)
            nc.vector.memset(rho_t, 1.0)
            self.vbar()
            nc.vector.tensor_scalar_mul(out=rho_t, in0=rho_t,
                                        scalar1=RH[:, ct:ct + 1])
            self.vbar()
            nc.vector.tensor_tensor(out=tmod, in0=Ct[:, 0:1024],
                                    in1=y_tiles[ct], op=OP.mult)
            nc.vector.tensor_tensor_scan(out=u_t[ct], data0=rho_t,
                                         data1=tmod, initial=0.0,
                                         op0=OP.mult, op1=OP.add)
            nc.vector.tensor_tensor(out=tmod, in0=St[:, 0:1024],
                                    in1=y_tiles[ct], op=OP.mult)
            nc.vector.tensor_tensor_scan(out=v_t[ct], data0=rho_t,
                                         data1=tmod, initial=0.0,
                                         op0=OP.mult, op1=OP.add)
            nc.vector.tensor_copy(out=stage[:, 2 * ct:2 * ct + 1],
                                  in_=u_t[ct][:, 1023:1024])
            nc.vector.tensor_copy(out=stage[:, 2 * ct + 1:2 * ct + 2],
                                  in_=v_t[ct][:, 1023:1024])
        self.vbar()
        nc.vector.engine_nop().then_inc(self.st_s, 1)
        self.st_n += 1
        nc.gpsimd.wait_ge(self.st_s, self.st_n)
        nc.gpsimd.dma_start(
            out=self.cc_in[l].ap().rearrange("a (p c) -> (a p) c", p=128),
            in_=stage).then_inc(self.qg, 16)
        self.qg_n += 16
        nc.gpsimd.wait_ge(self.qg, self.qg_n)
        nc.gpsimd.collective_compute(
            "AllGather", OP.bypass, replica_groups=[list(range(NCORES))],
            ins=[self.cc_in[l].ap().opt()], outs=[self.cc_out[l].ap().opt()],
        ).then_inc(self.sc_l[l])
        nc.sync.wait_ge(self.sc_l[l], 1)
        eraw = self.fv(F_TMD, 1024, rows=8)
        self.ld(eraw, self.cc_out[l].ap())
        self.ld_wait()
        self.bar()
        ends = self.fv(F_RHO, 64)
        erv = eraw.rearrange("r (p c) -> r p c", c=8)
        for ct in range(4):
            for uv in range(2):
                nc.tensor.transpose(self.ps_m.ap()[:, 8 * (2 * ct + uv):
                                                   8 * (2 * ct + uv) + 8],
                                    erv[:, :, 2 * ct + uv],
                                    self.ident_f.ap()[0:8, 0:8])
        self.bar()
        nc.vector.tensor_copy(out=ends, in_=self.ps_m.ap()[:, 0:64])
        self.bar()
        w8u = [self.fv(F_TMD + ct * 16, 8) for ct in range(4)]
        w8v = [self.fv(F_TMD + ct * 16 + 8, 8) for ct in range(4)]
        for ct in range(4):
            nc.vector.tensor_copy(out=w8u[ct][:, 0:1], in_=UI0[:, ct:ct + 1])
            nc.vector.tensor_copy(out=w8v[ct][:, 0:1], in_=VI0[:, ct:ct + 1])
        self.vbar()
        for cx in range(1, 8):
            for ct in range(4):
                E = P_t[ct][:, 1024:1025]
                Ue = ends[:, (ct * 2) * 8:(ct * 2) * 8 + 8]
                Ve = ends[:, (ct * 2 + 1) * 8:(ct * 2 + 1) * 8 + 8]
                nc.vector.scalar_tensor_tensor(
                    out=w8u[ct][:, cx:cx + 1], in0=w8u[ct][:, cx - 1:cx],
                    scalar=E, in1=Ue[:, cx - 1:cx], op0=OP.mult, op1=OP.add)
                nc.vector.scalar_tensor_tensor(
                    out=w8v[ct][:, cx:cx + 1], in0=w8v[ct][:, cx - 1:cx],
                    scalar=E, in1=Ve[:, cx - 1:cx], op0=OP.mult, op1=OP.add)
            self.vbar()
        w8all = self.fv(F_TMD, 64)
        selb64 = self.fv(F_TMD + 64, 64)
        uv8 = self.fv(F_TMD + 128, 8)
        nc.vector.tensor_tensor(out=selb64, in0=w8all, in1=cmask8, op=OP.mult)
        self.vbar()
        nc.vector.tensor_reduce(
            out=uv8.unsqueeze(2),
            in_=selb64.rearrange("p (g k) -> p g k", k=8), axis=AXX, op=OP.add)
        self.vbar()
        for ct in range(4):
            nc.vector.tensor_copy(out=UIN[:, ct:ct + 1],
                                  in_=uv8[:, 2 * ct:2 * ct + 1])
            nc.vector.tensor_copy(out=VIN[:, ct:ct + 1],
                                  in_=uv8[:, 2 * ct + 1:2 * ct + 2])
        self.vbar()
        for ct in range(4):
            nc.vector.scalar_tensor_tensor(out=u_t[ct], in0=P_t[ct][:, 1:1025],
                                           scalar=UIN[:, ct:ct + 1],
                                           in1=u_t[ct], op0=OP.mult,
                                           op1=OP.add)
            nc.vector.scalar_tensor_tensor(out=v_t[ct], in0=P_t[ct][:, 1:1025],
                                           scalar=VIN[:, ct:ct + 1],
                                           in1=v_t[ct], op0=OP.mult,
                                           op1=OP.add)
            nc.vector.tensor_tensor(out=u_t[ct], in0=u_t[ct],
                                    in1=C_t[ct][:, 0:1024], op=OP.mult)
            nc.vector.tensor_tensor(out=v_t[ct], in0=v_t[ct],
                                    in1=S_t[ct][:, 0:1024], op=OP.mult)
            nc.vector.tensor_tensor(out=u_t[ct], in0=u_t[ct], in1=v_t[ct],
                                    op=OP.add)
            scl = self.pcol("sc_lin", l, ct)
            nc.vector.scalar_tensor_tensor(out=h_tiles[ct], in0=u_t[ct],
                                           scalar=scl, in1=h_tiles[ct],
                                           op0=OP.mult, op1=OP.add)
        self.bar()

    def attn(self, src, dst):
        nc = self.nc
        h_tiles = [self.fv(F_H + ct * 1024, 1024) for ct in range(4)]
        y_tiles = [self.fv(F_Y + ct * 1024, 1024) for ct in range(4)]
        xfm = self.bv(A_XFM2, 16384)
        wsin = self.bv(A_WIN, 4096)
        winTv = self.winT.ap().rearrange("(s p) f -> p s f", p=128)
        for th in range(2):
            self.ln_big_half(src, th, xfm)
            for dc in range(DC):
                self.ld(wsin, winTv[:, dc, :])
                self.ld_wait()
                self.bar()
                for k in range(VC):
                    nc.tensor.matmul(self.ps_a.ap(),
                                     wsin[:, k * 128:(k + 1) * 128],
                                     xfm[:, k * 512:(k + 1) * 512],
                                     start=(k == 0), stop=(k == VC - 1))
                self.bar()
                nc.vector.tensor_scalar(
                    out=h_tiles[dc][:, th * 512:(th + 1) * 512],
                    in0=self.ps_a.ap(), scalar1=self.pcol("bin", 0, dc),
                    scalar2=None, op0=OP.add)
                self.bar()
        self.dbg_h(0, h_tiles)
        u_mlp = self.bv(A_UMLP, 8192)
        y2 = self.bv(A_Y2, 4096)
        wstr = self.bv(A_WSTR, 8192)
        w1Tv = self.w1T.ap().rearrange("(s p) f -> p s f", p=128)
        w2Tv = self.w2T.ap().rearrange("(s p) f -> p s f", p=128)
        tmp = self.fv(F_MTMP, 512)
        for l in range(NL):
            self.ln_inner(h_tiles, "ln1_w", "ln1_b", l, y_tiles, True)
            if l == 0:
                self.dbg_h(1, y_tiles)
            self.sconv(l, h_tiles, y_tiles)
            if l == 0:
                self.dbg_h(2, h_tiles)
            y2t = [y2[:, ct * 1024:(ct + 1) * 1024] for ct in range(4)]
            self.ln_inner(h_tiles, "ln2_w", "ln2_b", l, y2t, False)

            self.ld(wstr, w1Tv[:, l, :])
            self.ld_wait()
            self.bar()
            for th in range(2):
                for fc in range(FC):
                    for dc in range(DC):
                        nc.tensor.matmul(
                            self.ps_a.ap(),
                            wstr[:, (fc * 4 + dc) * 128:(fc * 4 + dc + 1) * 128],
                            y2t[dc][:, th * 512:(th + 1) * 512],
                            start=(dc == 0), stop=(dc == DC - 1))
                    self.bar()
                    nc.scalar.activation(
                        out=u_mlp[:, fc * 512:(fc + 1) * 512],
                        in_=self.ps_a.ap(), func=AF.Silu,
                        bias=self.pcol("b1", l, fc), scale=1.0)
                    self.bar()
                if th == 0:
                    u_sav = self.bv(A_UMLP + 8192, 8192)
                    nc.vector.tensor_copy(out=u_sav, in_=u_mlp)
                    self.bar()
            self.ld(wstr, w2Tv[:, l, :])
            self.ld_wait()
            self.bar()
            for th in range(2):
                usrc = self.bv(A_UMLP + 8192, 8192) if th == 0 else u_mlp
                for dc in range(DC):
                    for fc in range(FC):
                        nc.tensor.matmul(
                            self.ps_b.ap(),
                            wstr[:, (dc * 16 + fc) * 128:(dc * 16 + fc + 1) * 128],
                            usrc[:, fc * 512:(fc + 1) * 512],
                            start=(fc == 0), stop=(fc == FC - 1))
                    self.bar()
                    nc.vector.tensor_scalar(out=tmp, in0=self.ps_b.ap(),
                                            scalar1=self.pcol("b2", l, dc),
                                            scalar2=None, op0=OP.add)
                    self.bar()
                    nc.vector.tensor_tensor(
                        out=h_tiles[dc][:, th * 512:(th + 1) * 512],
                        in0=h_tiles[dc][:, th * 512:(th + 1) * 512],
                        in1=tmp, op=OP.add)
                    self.bar()

            self.dbg_h(3 + l, h_tiles)
        hln = self.bv(A_HLN, 4096)
        hlnt = [hln[:, ct * 1024:(ct + 1) * 1024] for ct in range(4)]
        self.ln_inner(h_tiles, "lnl_w", "lnl_b", 0, hlnt, False)
        self.dbg_h(7, hlnt)
        wo = [self.bv(A_WO + dc * 4096, 4096) for dc in range(DC)]
        woutTv = self.woutT.ap().rearrange("(s p) f -> p s f", p=128)
        for dc in range(DC):
            self.ld(wo[dc], woutTv[:, dc, :])
        self.ld_wait()
        self.bar()
        boutt = self.fv(F_BOUT, 512)
        x1s = self.fv(F_X1S, 512)
        wtmp = self.fv(F_WTMP, 512)
        for vb in range(8):
            self.ld(boutt, self.bout_rep.ap()[:, vb * 512:(vb + 1) * 512])
            self.ld_wait()
            self.bar()
            for tcg in range(8):
                for dc in range(DC):
                    nc.tensor.matmul(
                        self.ps_d[0].ap(),
                        hln[:, dc * 1024 + tcg * 128:dc * 1024 + (tcg + 1) * 128],
                        wo[dc][:, vb * 512:(vb + 1) * 512],
                        start=(dc == 0), stop=(dc == DC - 1))
                self.bar()
                self.ld(x1s, src.ap()[tcg * 128:(tcg + 1) * 128,
                                      vb * 512:(vb + 1) * 512])
                self.ld_wait()
                self.bar()
                nc.vector.tensor_tensor(out=wtmp, in0=self.ps_d[0].ap(),
                                        in1=boutt, op=OP.add)
                self.bar()
                nc.vector.tensor_tensor(out=wtmp, in0=wtmp, in1=x1s, op=OP.add)
                self.bar()
                self.st(dst.ap()[tcg * 128:(tcg + 1) * 128,
                                 vb * 512:(vb + 1) * 512], wtmp)
                self.st_wait()
                self.bar()

    def dbg_h(self, stage, tiles):
        if not DEBUG:
            return
        self.allbar()
        for ct in range(4):
            dst = self.dbgh.ap()[stage, ct * 128:(ct + 1) * 128, :]
            if tiles[ct].dtype == BF16:
                self.nc.gpsimd.dma_start(out=dst, in_=tiles[ct]).then_inc(self.qg, 16)
                self.qg_n += 16
            else:
                self.st(dst, tiles[ct])
        self.st_wait()
        self.nc.gpsimd.wait_ge(self.qg, self.qg_n)
        self.allbar()

    def dbg_copy(self):
        if not DEBUG:
            return
        buf = self.fv(0, 4096)
        for (srct, dstt) in [(self.x1b, self.dbg1), (self.x2b, self.dbg2)]:
            for tt in range(8):
                self.ld(buf, srct.ap()[tt * 128:(tt + 1) * 128, :])
                self.ld_wait()
                self.bar()
                self.st(dstt.ap()[tt * 128:(tt + 1) * 128, :], buf)
                self.st_wait()
                self.bar()

    def emit(self):
        self.prep_consts()
        self.prep_ffn_weights()
        self.prep_attn_weights()
        self.bar()
        self.ffn(self.x_in, self.x1b)
        self.bar()
        self.attn(self.x1b, self.x2b)
        self.bar()
        self.ffn(self.x2b, self.out)
        self.bar()
        self.dbg_copy()
        self.st_wait()
        self.nc.sync.wait_ge(self.qa, self.qa_n)
        self.bar()


_CACHE = {}


def _build():
    if "nc" not in _CACHE:
        kb = KB()
        kb.emit()
        _CACHE["nc"] = kb.nc
    return _CACHE["nc"]


def _pack_params(inputs, core):
    f32 = np.float32
    prm = np.zeros((128, PC_DATA), f32)

    def put4(nm, arr):
        a = np.asarray(arr, f32).reshape(NL, 4, 128)
        for l in range(NL):
            c = PCMAP[(nm, l)]
            prm[:, c:c + 4] = a[l].T

    put4("ln1_w", inputs["L0_attn_ln1_w"])
    put4("ln1_b", inputs["L0_attn_ln1_b"])
    put4("ln2_w", inputs["L0_attn_ln2_w"])
    put4("ln2_b", inputs["L0_attn_ln2_b"])
    put4("sc_lin", inputs["L0_attn_sc_lin"])
    put4("phz_re", inputs["L0_attn_phz_re"])
    put4("phz_im", inputs["L0_attn_phz_im"])
    put4("lc_re", inputs["L0_attn_lc_re"])
    put4("lc_im", inputs["L0_attn_lc_im"])
    put4("b2", inputs["L0_attn_b2"])
    b1 = np.asarray(inputs["L0_attn_b1"], f32).reshape(NL, 16, 128)
    for l in range(NL):
        c = PCMAP[("b1", l)]
        prm[:, c:c + 16] = b1[l].T
    for nm, key in [("bin", "L0_attn_bin"), ("lnl_w", "L0_attn_lnl_w"),
                    ("lnl_b", "L0_attn_lnl_b")]:
        c = PCMAP[(nm, 0)]
        prm[:, c:c + 4] = np.asarray(inputs[key], f32).reshape(4, 128).T
    c = PCMAP[("cmask", 0)]
    prm[:, c + core] = 1.0
    c = PCMAP[("negt", 0)]
    prm[:, c:c + 14] = -np.arange(1, 15, dtype=f32)[None, :]
    c = PCMAP[("cmask8", 0)]
    m8 = np.zeros(8, f32)
    m8[core] = 1.0
    prm[:, c:c + 64] = np.tile(m8, 8)[None, :]
    return prm


def kernel(**inputs):
    nc = _build()
    f32 = np.float32
    x = np.asarray(inputs["L0_attn_x"], f32).reshape(8192, D)
    gq = np.asarray(inputs["L0_ffn_gate_q"], np.int32)
    uq = np.asarray(inputs["L0_ffn_up_q"], np.int32)
    dq = np.asarray(inputs["L0_ffn_down_q"], np.int32)
    ga = np.asarray(inputs["L0_ffn_gate_absmax"], f32).reshape(M, 64)
    ua = np.asarray(inputs["L0_ffn_up_absmax"], f32).reshape(M, 64)
    da = np.asarray(inputs["L0_ffn_down_absmax"], f32).reshape(D, M // 64)
    bout_rep = np.tile(np.asarray(inputs["L0_attn_bout"], f32)[None, :], (128, 1))
    in_maps = []
    for c in range(NCORES):
        mo, ho = c * MO, c * HO
        in_maps.append({
            "x_in": np.ascontiguousarray(x[c * T:(c + 1) * T]),
            "gq": np.ascontiguousarray(gq[mo:mo + MO]),
            "uq": np.ascontiguousarray(uq[mo:mo + MO]),
            "dq": np.ascontiguousarray(dq[ho:ho + HO]),
            "ga": np.ascontiguousarray(ga[mo:mo + MO]),
            "ua": np.ascontiguousarray(ua[mo:mo + MO]),
            "da": np.ascontiguousarray(da[ho:ho + HO]),
            "win": np.asarray(inputs["L0_attn_Win"], f32),
            "w1": np.asarray(inputs["L0_attn_W1"], f32),
            "w2": np.asarray(inputs["L0_attn_W2"], f32),
            "wout": np.asarray(inputs["L0_attn_Wout"], f32),
            "prm_in": _pack_params(inputs, c),
            "bout_rep": bout_rep,
        })
    res = run_bass_kernel_spmd(nc, in_maps, core_ids=list(range(NCORES)))
    global _LAST_RES
    _LAST_RES = res
    out = np.concatenate([res.results[c]["out"] for c in range(NCORES)], axis=0)
    return out.reshape(1, 8192, D).astype(np.float32)
